# version 25
# baseline (speedup 1.0000x reference)
from contextlib import ExitStack

import numpy as np
import ml_dtypes

import concourse.bass as bass
import concourse.mybir as mybir
from concourse.alu_op_type import AluOpType
from concourse.bass_utils import run_bass_kernel_spmd

F32 = mybir.dt.float32
BF16 = mybir.dt.bfloat16
AF = mybir.ActivationFunctionType
AX = mybir.AxisListType

T_FULL = 4096
T_OWN = 2048
C = 1024
NKC = 8
NPAIR = 8
PAIRS_PER_G = 4
NGROUP = 2
NQT = 4
NKT = 32
EPS = 1e-5
MINUS_LN8 = -2.0794415416798357


class Sem:
    def __init__(self, handle):
        self.h = handle
        self.n = 0

    def inc(self, k=1):
        self.n += k
        return self.n


class Script:

    def __init__(self, sv, ss):
        self.items = []
        self._sv = sv
        self._ss = ss

    def t(self, fn):
        self.items.append(("tensor", fn, False))

    def v(self, fn):
        self.items.append(("vector", fn, True))
        return self._sv.inc()

    def s(self, fn):
        self.items.append(("scalar", fn, True))
        return self._ss.inc()

    def y(self, fn):
        self.items.append(("sync", fn, False))

    def g(self, fn):
        self.items.append(("gpsimd", fn, False))

    def wait(self, eng, sem, val):
        if val <= 0:
            return
        self.items.append(
            (eng, lambda e, s=sem.h, v=val: e.wait_ge(s, v), False))

    def run(self, engine_name, eng, chain_sem, chain_base):
        n = chain_base
        for name, fn, chained in self.items:
            if name != engine_name:
                continue
            if chained:
                if n > 0:
                    eng.wait_ge(chain_sem, n)
                ins = fn(eng)
                assert ins is not None, "chained entry must return instruction"
                ins.then_inc(chain_sem, 1)
                n += 1
            else:
                fn(eng)
        return n


def emit_block(nc, script, L):
    sv, ss = L["sv"], L["ss"]
    with nc.Block() as block:
        @block.tensor
        def _(t):
            script.run("tensor", t, None, 0)

        @block.vector
        def _(v):
            base = L["chain_done"]["vector"]
            L["chain_done"]["vector"] = script.run("vector", v, sv.h, base)

        @block.scalar
        def _(s):
            base = L["chain_done"]["scalar"]
            L["chain_done"]["scalar"] = script.run("scalar", s, ss.h, base)

        @block.sync
        def _(y):
            script.run("sync", y, None, 0)

        @block.gpsimd
        def _(g):
            script.run("gpsimd", g, None, 0)


def barrier(sc, L):
    st, sv, ss, sy, sg = L["st"], L["sv"], L["ss"], L["sy"], L["sg"]
    t_t = st.inc()
    sc.t(lambda e, s=st.h: e.drain().then_inc(s, 1))
    t_v = sc.v(lambda e: e.drain())
    t_s = sc.s(lambda e: e.drain())
    t_y = sy.inc()
    sc.y(lambda e, s=sy.h: e.drain().then_inc(s, 1))
    t_g = sg.inc()
    sc.g(lambda e, s=sg.h: e.drain().then_inc(s, 1))
    targets = {"tensor": (st, t_t), "vector": (sv, t_v), "scalar": (ss, t_s),
               "sync": (sy, t_y), "gpsimd": (sg, t_g)}
    for e in targets:
        for e2, (sem, val) in targets.items():
            if e2 != e:
                sc.wait(e, sem, val)


def build_nc():
    nc = bass.Bass("TRN2", target_bir_lowering=False, debug=False)

    L = {}
    L["x_hbm"] = nc.dram_tensor("x", [T_FULL, C], F32, kind="ExternalInput").ap()
    L["wqkv"] = nc.dram_tensor("wqkv", [C, 3 * C], F32, kind="ExternalInput").ap()
    L["bqkv"] = nc.dram_tensor("bqkv", [3 * C], F32, kind="ExternalInput").ap()
    L["gam"] = nc.dram_tensor("gam", [C], F32, kind="ExternalInput").ap()
    L["bet"] = nc.dram_tensor("bet", [C], F32, kind="ExternalInput").ap()
    L["wout"] = nc.dram_tensor("wout", [C, C], F32, kind="ExternalInput").ap()
    L["bout"] = nc.dram_tensor("bout", [C], F32, kind="ExternalInput").ap()
    L["win"] = nc.dram_tensor("win", [C, 64], F32, kind="ExternalInput").ap()
    L["binf"] = nc.dram_tensor("binf", [64], F32, kind="ExternalInput").ap()
    L["rw1"] = nc.dram_tensor("rw1", [64, C], F32, kind="ExternalInput").ap()
    L["b1f"] = nc.dram_tensor("b1f", [C], F32, kind="ExternalInput").ap()
    L["w2b"] = nc.dram_tensor("w2b", [C, 64], F32, kind="ExternalInput").ap()
    L["b2"] = nc.dram_tensor("b2", [64], F32, kind="ExternalInput").ap()
    L["wfo"] = nc.dram_tensor("wfo", [64, C], F32, kind="ExternalInput").ap()
    L["bffo"] = nc.dram_tensor("bffo", [C], F32, kind="ExternalInput").ap()
    L["ident_h"] = nc.dram_tensor("ident", [128, 128], BF16, kind="ExternalInput").ap()
    L["identf_h"] = nc.dram_tensor("identf", [128, 128], F32, kind="ExternalInput").ap()
    L["onesb_h"] = nc.dram_tensor("onesb", [128, 128], BF16, kind="ExternalInput").ap()
    L["onesf_h"] = nc.dram_tensor("onesf", [1, 128], F32, kind="ExternalInput").ap()
    L["consts_h"] = nc.dram_tensor("consts", [128, 2], F32, kind="ExternalInput").ap()
    L["out_hbm"] = nc.dram_tensor("out", [T_OWN, C], F32, kind="ExternalOutput").ap()

    es = ExitStack()
    with es:
        def SB(name, shape, dt):
            return es.enter_context(nc.sbuf_tensor(name, shape, dt))

        def SEM(name):
            return es.enter_context(nc.semaphore(name))

        L.update(
            ident_sb=SB("ident_sb", [128, 128], BF16),
            identf_sb=SB("identf_sb", [128, 128], F32),
            onesb_sb=SB("onesb_sb", [128, 128], BF16),
            onesf_sb=SB("onesf_sb", [1, 128], F32),
            consts_sb=SB("consts_sb", [128, 2], F32),
            bqkvT_sb=SB("bqkvT_sb", [128, 24], F32),
            gamT_sb=SB("gamT_sb", [128, 8], F32),
            betT_sb=SB("betT_sb", [128, 8], F32),
            bin_sb=SB("bin_sb", [64, 1], F32),
            b1T_sb=SB("b1T_sb", [128, 8], F32),
            b2_sb=SB("b2_sb", [64, 1], F32),
            negB_sb=SB("negB_sb", [128, NPAIR], F32),
            attnT_sb=SB("attnT_sb", [128, NPAIR * T_OWN], BF16),
        )
        dc, dg = Sem(SEM("s_dc")), Sem(SEM("s_dg2"))
        dxp = [Sem(SEM("s_dx0")), Sem(SEM("s_dx1")),
               Sem(SEM("s_dx2")), Sem(SEM("s_dx3"))]
        dwp = [Sem(SEM("s_dw0")), Sem(SEM("s_dw1"))]
        dwo = Sem(SEM("s_dwo"))
        dop = [Sem(SEM("s_do0")), Sem(SEM("s_do1"))]
        st, sv, ss = Sem(SEM("s_t")), Sem(SEM("s_v")), Sem(SEM("s_s"))
        sy, sg = Sem(SEM("s_y")), Sem(SEM("s_g"))
        L.update(dc=dc, dg=dg, dxp=dxp, dwp=dwp, dwo=dwo, dop=dop,
                 st=st, sv=sv, ss=ss, sy=sy, sg=sg,
                 chain_done={"vector": 0, "scalar": 0})

        with ExitStack() as es12:
            def SB12(name, shape, dt):
                return es12.enter_context(nc.sbuf_tensor(name, shape, dt))

            L.update(
                kT_sb=SB12("kT_sb", [128, 2 * T_FULL], BF16),
                v_sb=SB12("v_sb", [128, 2 * T_FULL], BF16),
                qT_sb=SB12("qT_sb", [128, 2 * T_OWN], BF16),
                xhq_sb=SB12("xhq_sb", [128, NKC * T_FULL], BF16),
                xtile_sb=SB12("xtile_sb", [128, 3 * C], F32),
                xh_tile=SB12("xh_tile", [128, 2 * C], BF16),
                wq_sb=SB12("wq_sb", [128, 2 * 1024], BF16),
                wq3_sb=SB12("wq3_sb", [128, 3 * 1024], BF16),
                vstage_sb=SB12("vstage_sb", [128, 8 * 512], BF16),
                vtmp_sb=SB12("vtmp_sb", [128, 2 * 512], BF16),
                stats_sb=SB12("stats_sb", [128, 12], F32),
                mv_sb=SB12("mv_sb", [128, 8], F32),
                rstd_sb=SB12("rstd_sb", [128, 4], F32),
                nmax_sb=SB12("nmax_sb", [128, 8], F32),
                PT_sb=SB12("PT_sb", [128, 4 * 1024], BF16),
                zb_sb=SB12("zb_sb", [128, 512], F32),
            )
            L["sq_sb"] = L["xh_tile"]
            with ExitStack() as esA:
                L["score_ps"] = esA.enter_context(
                    nc.psum_tensor("score_ps", [128, 2048], F32))
                L["pv_ps"] = esA.enter_context(
                    nc.psum_tensor("pv_ps", [128, 512], F32))
                L["z_ps"] = esA.enter_context(
                    nc.psum_tensor("z_ps", [128, 512], F32))
                L["qkv_ps"] = esA.enter_context(
                    nc.psum_tensor("qkv_ps", [128, 512], F32))
                L["tp_ps"] = esA.enter_context(
                    nc.psum_tensor("tp_ps", [128, 1024], BF16))
                sc = Script(sv, ss)
                p0_loads(sc, L)
                L["plstate"] = {
                    "lqe": 0,
                    "last_wq_consumed": [0, 0],
                    "last_tp_evac": 0,
                    "last_tp_evac_v": 0,
                    "last_qkv_evac": 0,
                    "last_vtmp_read": [0, 0],
                    "v_negB": [0] * 8,
                    "kv_parity_free": [0, 0],
                    "pvz_free": 0,
                    "vstage_ev": [0] * 8,
                    "it_hist": [],
                }
                phase_ln(sc, L)
                pipeline(sc, L)
                barrier(sc, L)
                emit_block(nc, sc, L)

        with ExitStack() as es34:
            def SB34(name, shape, dt):
                return es34.enter_context(nc.sbuf_tensor(name, shape, dt))

            L.update(
                wout_sb=SB34("wout_sb", [128, 8 * C], BF16),
                boutrow_sb=SB34("boutrow_sb", [1, C], F32),
                bfforow_sb=SB34("bfforow_sb", [1, C], F32),
                win_sb=SB34("win_sb", [128, 8 * 64], BF16),
                rw1_sb=SB34("rw1_sb", [64, C], BF16),
                w2b_sb=SB34("w2b_sb", [128, 8 * 64], BF16),
                wfo_sb=SB34("wfo_sb", [64, C], BF16),
                boutbc_sb=SB34("boutbc_sb", [128, C], F32),
                bffobc_sb=SB34("bffobc_sb", [128, C], F32),
                x2_sb=SB34("x2_sb", [128, 8 * C], F32),
                x2bf_sb=SB34("x2bf_sb", [128, 4 * C], BF16),
                x2T_sb=SB34("x2T_sb", [128, 2 * 4096], BF16),
                xres_sb=SB34("xres_sb", [128, 4 * C], F32),
                sT_sb=SB34("sT_sb", [64, 2 * 512], BF16),
                hT_sb=SB34("hT_sb", [128, 2 * 4096], BF16),
                gT_sb=SB34("gT_sb", [64, 2 * 512], BF16),
                u_sb=SB34("u_sb", [128, 8 * C], F32),
                stats2_sb=SB34("stats2_sb", [128, 96], F32),
                mv2_sb=SB34("mv2_sb", [128, 16], F32),
                rstd2_sb=SB34("rstd2_sb", [128, 8], F32),
            )
            L["A_ps"] = es34.enter_context(
                nc.psum_tensor("A_ps", [128, 2048], F32))
            L["F_ps"] = es34.enter_context(
                nc.psum_tensor("F_ps", [128, 1024], F32))
            L["B_ps"] = es34.enter_context(
                nc.psum_tensor("B_ps", [128, 1024], BF16))
            L["C_ps"] = es34.enter_context(
                nc.psum_tensor("C_ps", [64, 512], F32))
            sc = Script(sv, ss)
            p34(sc, L)
            sc.wait("sync", dop[0], dop[0].n)
            sc.wait("sync", dop[1], dop[1].n)
            emit_block(nc, sc, L)

    return nc



def p0_loads(sc, L):
    dc, dg = L["dc"], L["dg"]

    def ld(dst, src, slow=False):
        sc.y(lambda e, d=dst, s=src, sl=slow: e.dma_start(
            out=d, in_=s, allow_slow_non_contiguous=sl).then_inc(dc.h, 16))
        dc.inc(16)

    def ldg(dst, src):
        sc.g(lambda e, d=dst, s=src: e.dma_start(
            out=d, in_=s).then_inc(dg.h, 16))
        dg.inc(16)

    ld(L["ident_sb"][:], L["ident_h"])
    ld(L["identf_sb"][:], L["identf_h"])
    ld(L["onesb_sb"][:], L["onesb_h"])
    ld(L["onesf_sb"][:], L["onesf_h"])
    ld(L["consts_sb"][:], L["consts_h"])
    ld(L["bqkvT_sb"][:], L["bqkv"].rearrange("(m p) -> p m", p=128), slow=True)
    ld(L["gamT_sb"][:], L["gam"].rearrange("(m p) -> p m", p=128), slow=True)
    ld(L["betT_sb"][:], L["bet"].rearrange("(m p) -> p m", p=128), slow=True)
    ld(L["bin_sb"][:], L["binf"][:, None], slow=True)
    ld(L["b1T_sb"][:], L["b1f"].rearrange("(m p) -> p m", p=128), slow=True)
    ld(L["b2_sb"][:], L["b2"][:, None], slow=True)
    for e in ("tensor", "vector", "scalar"):
        sc.wait(e, dc, dc.n)
        sc.wait(e, dg, dg.n)


def phase_ln(sc, L):
    dxp, dwo = L["dxp"], L["dwo"]
    st, sv, ss = L["st"], L["sv"], L["ss"]
    xtile_sb, xh_tile, stats_sb, mv_sb, rstd_sb = (
        L["xtile_sb"], L["xh_tile"], L["stats_sb"], L["mv_sb"], L["rstd_sb"])
    xhq_sb, wq3_sb = L["xhq_sb"], L["wq3_sb"]
    kT_sb, qT_sb = L["kT_sb"], L["qT_sb"]
    qkv_ps, score_ps = L["qkv_ps"], L["score_ps"]
    x_hbm, wqkv = L["x_hbm"], L["wqkv"]
    ident_sb, gamT_sb, betT_sb, bqkvT_sb = (
        L["ident_sb"], L["gamT_sb"], L["betT_sb"], L["bqkvT_sb"])
    eps_ap = L["consts_sb"][:, 0:1]
    state = L["plstate"]

    dws = dwo
    for j, m in enumerate((8, 16, 0)):
        srcw = wqkv[:, 128 * m:128 * m + 128].rearrange(
            "(kc p) c -> p kc c", p=128)
        sc.g(lambda e, d=wq3_sb[:, j * 1024:(j + 1) * 1024].rearrange(
             "p (kc c) -> p kc c", kc=8), s=srcw, hh=dws.h:
             e.dma_start(out=d, in_=s).then_inc(hh, 16))
        dws.inc(16)
    w3_ready = dws.n

    NB = 3
    dma_cnt = [0] * NB
    v_xh_t = [0] * 32
    s_rstd_t = [0] * 32
    s_evac_t = [None] * 32
    last_xh_read = [0, 0]
    last_score_evac = [0, 0]
    last_qkv_evac = [0]
    w3_waited = [False]

    def issue_dma(j, guard):
        if j > 31:
            return
        b = j % NB
        xt = xtile_sb[:, b * C:(b + 1) * C]
        if guard:
            sc.wait("sync", sv, guard)
        dxs = dxp[b]
        sc.y(lambda e, d=xt, s=x_hbm[j * 128:j * 128 + 128, :], hh=dxs.h:
             e.dma_start(out=d, in_=s).then_inc(hh, 16))
        dma_cnt[b] = dxs.inc(16)

    pend = []
    for n in range(8):
        pend.append((8, n))
        pend.append((16, n))
        if n < 4:
            pend.append((0, n))
    pi = [0]
    pending_evac = []

    def qkv_burst():
        if pi[0] >= len(pend):
            return False
        m, n = pend[pi[0]]
        if s_evac_t[4 * n + 3] is None:
            return False
        pi[0] += 1
        if not w3_waited[0]:
            sc.wait("tensor", dws, w3_ready)
            w3_waited[0] = True
        j = {8: 0, 16: 1, 0: 2}[m]
        wbuf = wq3_sb[:, j * 1024:(j + 1) * 1024]
        sc.wait("tensor", ss, max(s_evac_t[4 * n + 3], last_qkv_evac[0]))
        for kc in range(8):
            last = kc == 7
            def mk(o, w, r, s0, s1, last=last):
                def f(e):
                    ins = e.matmul(o, w, r, start=s0, stop=s1)
                    if last:
                        ins.then_inc(st.h, 1)
                    return ins
                return f
            sc.t(mk(qkv_ps[:], wbuf[:, kc * 128:kc * 128 + 128],
                    xhq_sb[:, kc * T_FULL + n * 512:
                           kc * T_FULL + n * 512 + 512],
                    kc == 0, kc == 7))
        pending_evac.append((st.inc(), m, n))
        return True

    def qkv_evac():
        if not pending_evac:
            return False
        t_mm, m, n = pending_evac.pop(0)
        if m == 8:
            dst = kT_sb[:, n * 512:n * 512 + 512]
        elif m == 16:
            dst = L["vstage_sb"][:, n * 512:n * 512 + 512]
        else:
            dst = qT_sb[:, n * 512:n * 512 + 512]
        sc.wait("scalar", st, t_mm)
        s_ev = sc.s(lambda e, o=dst, i_=qkv_ps[:], b=bqkvT_sb[:, m:m + 1]:
                    e.activation(out=o, in_=i_, func=AF.Identity, bias=b))
        last_qkv_evac[0] = s_ev
        if m == 16:
            state["vstage_ev"][n] = s_ev
        return True

    issue_dma(0, 0)
    issue_dma(1, 0)
    for j in range(33):
        if j < 32:
            b = j % NB
            xt = xtile_sb[:, b * C:(b + 1) * C]
            sc.wait("vector", dxp[b], dma_cnt[b])
            sc.v(lambda e, o=stats_sb[:, 0:6], i_=xt[:, 0:512]:
                 e.bn_stats(o, i_))
            sc.v(lambda e, o=stats_sb[:, 6:12], i_=xt[:, 512:1024]:
                 e.bn_stats(o, i_))
            mvb = mv_sb[:, 2 * (j % 2):2 * (j % 2) + 2]
            v_stats = sc.v(lambda e, o=mvb, i_=stats_sb[:]: e.bn_aggr(o, i_))
            rst = rstd_sb[:, j % 2:j % 2 + 1]
            sc.wait("scalar", sv, v_stats)
            sc.s(lambda e, o=rst, i_=mvb[:, 1:2], b=eps_ap:
                 e.activation(out=o, in_=i_, func=AF.Ln, bias=b))
            s_rstd_t[j] = sc.s(lambda e, o=rst, i_=rst:
                               e.activation(out=o, in_=i_, func=AF.Exp,
                                            scale=-0.5))
        if j >= 1:
            i = j - 1
            qkv_evac()
            par = i % 2
            bi = i % NB
            xt_i = xtile_sb[:, bi * C:(bi + 1) * C]
            xht = xh_tile[:, par * C:(par + 1) * C]
            mvb = mv_sb[:, 2 * par:2 * par + 2]
            rst = rstd_sb[:, par:par + 1]
            sc.wait("vector", ss, s_rstd_t[i])
            sc.wait("vector", st, last_xh_read[par])
            v_xh = sc.v(lambda e, o=xht, i_=xt_i, m=mvb[:, 0:1], r=rst:
                        e.tensor_scalar(o, i_, m, r, op0=AluOpType.subtract,
                                        op1=AluOpType.mult))
            v_xh_t[i] = v_xh
            issue_dma(i + 2, v_xh_t[i - 1] if i >= 1 else 0)
            sps = score_ps[:, par * 1024:par * 1024 + 1024]
            sc.wait("tensor", sv, v_xh)
            sc.wait("tensor", ss, last_score_evac[par])
            for kc in range(8):
                last = kc == 7
                def mk(o, w, r, last=last):
                    def f(e):
                        ins = e.matmul(o, w, r, start=True, stop=True,
                                       skip_group_check=True)
                        if last:
                            ins.then_inc(st.h, 1)
                        return ins
                    return f
                sc.t(mk(sps[:, kc * 128:kc * 128 + 128],
                        xht[:, kc * 128:kc * 128 + 128], ident_sb[:]))
            t_tp = st.inc()
            last_xh_read[par] = t_tp
            sc.wait("scalar", st, t_tp)
            s_e = 0
            for kc in range(8):
                dst = xhq_sb[:, kc * T_FULL + i * 128:
                             kc * T_FULL + i * 128 + 128]
                s_e = sc.s(lambda e, o=dst, i_=sps[:, kc * 128:kc * 128 + 128],
                           ga=gamT_sb[:, kc:kc + 1], be=betT_sb[:, kc:kc + 1]:
                           e.activation(out=o, in_=i_, func=AF.Identity,
                                        bias=be, scale=ga))
            s_evac_t[i] = s_e
            last_score_evac[par] = s_e
            qkv_burst()
    while pi[0] < len(pend) or pending_evac:
        made = qkv_evac()
        made = qkv_burst() or made
        if not made:
            break
    L["v_gb_s"] = s_evac_t[31]
    state["lqe_s"] = last_qkv_evac[0]
    state["last_qkv_evac"] = 0
    state["last_tp_evac"] = 0
    L["ln_tp_evac"] = 0


def pipeline(sc, L):
    st, sv, ss = L["st"], L["sv"], L["ss"]
    state = L["plstate"]
    sc.wait("vector", ss, state["lqe_s"])
    sc.wait("tensor", ss, state["lqe_s"])

    def full_gen(sg):
        yield from qkv_steps_gen(sc, L, sg, state)
        yield from bound_gen(sc, L, sg, state)

    ident_sb, tp_ps, v_sb = L["ident_sb"], L["tp_ps"], L["v_sb"]
    vstage = L["vstage_sb"]

    def vstep_gen():
        for n in range(8):
            sc.wait("tensor", ss, state["vstage_ev"][n])
            sc.wait("tensor", ss, state["last_tp_evac"])
            sc.wait("tensor", sv, state["last_tp_evac_v"])
            for blk in range(4):
                last = blk == 3
                def mk(o, i_, idn, last=last):
                    def f(e):
                        ins = e.transpose(o, i_, idn)
                        if last:
                            ins.then_inc(st.h, 1)
                        return ins
                    return f
                sc.t(mk(tp_ps[:, blk * 128:blk * 128 + 128],
                        vstage[:, n * 512 + blk * 128:
                               n * 512 + blk * 128 + 128], ident_sb[:]))
            t_vt = st.inc()
            sc.wait("vector", st, t_vt)
            state["last_tp_evac_v"] = sc.v(
                lambda e, o=v_sb[:, (n * 4) * 128:(n * 4 + 4) * 128],
                i_=tp_ps[:, 0:512]: e.tensor_copy(o, i_))
            yield n

    vsteps = vstep_gen()
    for _ in bound_gen(sc, L, 0, state):
        next(vsteps, None)
    for _ in vsteps:
        pass
    state["pvz_free"] = state["last_tp_evac_v"]
    for sg in range(8):
        gen = full_gen(sg + 1) if sg + 1 < 8 else None
        p2_subgroup(sc, L, sg, state, gen)
        if gen is not None:
            for _ in gen:
                pass
        state["kv_parity_free"][sg % 2] = st.n


def qkv_steps_gen(sc, L, sg, state):
    dwp = L["dwp"]
    st, sv, ss = L["st"], L["sv"], L["ss"]
    xhq_sb, wq_sb, vtmp_sb = L["xhq_sb"], L["wq_sb"], L["vtmp_sb"]
    kT_sb, v_sb, qT_sb = L["kT_sb"], L["v_sb"], L["qT_sb"]
    qkv_ps, tp_ps = L["qkv_ps"], L["tp_ps"]
    wqkv = L["wqkv"]
    ident_sb, bqkvT_sb = L["ident_sb"], L["bqkvT_sb"]
    par = sg % 2
    kbase = par * T_FULL
    qbase = par * T_OWN

    k_tiles = [8 + sg]
    v_tiles = [16 + sg]
    q_tiles = [sg]
    mlist = k_tiles + v_tiles + q_tiles
    wq_ready = {}

    def issue_wload(j):
        m = mlist[j]
        wbuf = wq_sb[:, (j % 2) * 1024:(j % 2) * 1024 + 1024]
        sc.wait("gpsimd", st, state["last_wq_consumed"][j % 2])
        srcw = wqkv[:, 128 * m:128 * m + 128].rearrange(
            "(kc p) c -> p kc c", p=128)
        dws = dwp[j % 2]
        sc.g(lambda e, d=wbuf.rearrange("p (kc c) -> p kc c", kc=8), s=srcw,
             hh=dws.h: e.dma_start(out=d, in_=s).then_inc(hh, 16))
        wq_ready[j] = (dws, dws.inc(16))

    issue_wload(0)
    issue_wload(1)
    first = True
    for j, m in enumerate(mlist):
        wbuf = wq_sb[:, (j % 2) * 1024:(j % 2) * 1024 + 1024]
        nrange = 4 if m in q_tiles else 8
        for n in range(nrange):
            if m in k_tiles:
                dst = kT_sb[:, kbase + n * 512:kbase + n * 512 + 512]
            elif m in v_tiles:
                dst = vtmp_sb[:, (n % 2) * 512:(n % 2) * 512 + 512]
            else:
                dst = qT_sb[:, qbase + n * 512:qbase + n * 512 + 512]
            sc.wait("tensor", wq_ready[j][0], wq_ready[j][1])
            if first:
                sc.wait("tensor", ss, L["v_gb_s"])
                if sg >= 2:
                    sc.wait("vector", st, state["kv_parity_free"][par])
                first = False
            sc.wait("tensor", sv, state["last_qkv_evac"])
            for kc in range(8):
                if kc == 4:
                    yield (m, n, "mid")
                last = kc == 7
                def mk(o, w, r, s0, s1, last=last):
                    def f(e):
                        ins = e.matmul(o, w, r, start=s0, stop=s1)
                        if last:
                            ins.then_inc(st.h, 1)
                        return ins
                    return f
                sc.t(mk(qkv_ps[:], wbuf[:, kc * 128:kc * 128 + 128],
                        xhq_sb[:, kc * T_FULL + n * 512:
                               kc * T_FULL + n * 512 + 512],
                        kc == 0, kc == 7))
            t_mm = st.inc()
            yield (m, n, 0)
            sc.wait("vector", st, t_mm)
            if m in v_tiles:
                sc.wait("vector", st, state["last_vtmp_read"][n % 2])
            v_ev = sc.v(lambda e, o=dst, i_=qkv_ps[:], b=bqkvT_sb[:, m:m + 1]:
                        e.tensor_scalar(o, i_, b, None, op0=AluOpType.add,
                                        op1=AluOpType.bypass))
            state["last_qkv_evac"] = v_ev
            if n == nrange - 1:
                state["last_wq_consumed"][j % 2] = st.n
                if j + 2 < len(mlist):
                    issue_wload(j + 2)
            if m in v_tiles:
                sc.wait("tensor", sv, v_ev)
                sc.wait("tensor", ss, state["last_tp_evac"])
                sc.wait("tensor", sv, state["last_tp_evac_v"])
                for blk in range(4):
                    last = blk == 3
                    def mk(o, i_, idn, last=last):
                        def f(e):
                            ins = e.transpose(o, i_, idn)
                            if last:
                                ins.then_inc(st.h, 1)
                            return ins
                        return f
                    sc.t(mk(tp_ps[:, blk * 128:blk * 128 + 128],
                            dst[:, blk * 128:blk * 128 + 128],
                            ident_sb[:]))
                t_vt = st.inc()
                state["last_vtmp_read"][n % 2] = t_vt
                vdst = v_sb[:, kbase + (n * 4) * 128:
                            kbase + (n * 4 + 4) * 128]
                sc.wait("vector", st, t_vt)
                state["last_tp_evac_v"] = sc.v(
                    lambda e, o=vdst, i_=tp_ps[:, 0:512]:
                    e.tensor_copy(o, i_))
            yield (m, n, 4)


def bound_gen(sc, L, sg, state):
    st, sv, ss = L["st"], L["sv"], L["ss"]
    kT_sb, qT_sb, sq_sb, nmax_sb, negB_sb = (
        L["kT_sb"], L["qT_sb"], L["sq_sb"], L["nmax_sb"], L["negB_sb"])
    onesb_sb = L["onesb_sb"]
    qkv_ps = L["qkv_ps"]
    mln8_ap = L["consts_sb"][:, 1:2]
    par = sg % 2
    kbase = par * T_FULL
    qbase = par * T_OWN
    last_sq_read = state.setdefault("bound_sq_read", [0])

    v_nb = 0
    for pl in range(1):
        p = sg
        srcs = [(qT_sb[:, qbase:qbase + T_OWN], 0),
                (kT_sb[:, kbase:kbase + 2048], 1),
                (kT_sb[:, kbase + 2048:kbase + 4096], 2)]
        for (srcq, ci) in srcs:
            sc.wait("vector", st, last_sq_read[0])
            v_sq = sc.v(lambda e, o=sq_sb[:], a=srcq:
                        e.tensor_tensor(o, a, a, op=AluOpType.mult))
            sc.wait("tensor", sv, v_sq)
            sc.wait("tensor", sv, state["last_qkv_evac"])
            acc = nmax_sb[:, ci + 4:ci + 5]
            tgt = nmax_sb[:, ci:ci + 1]
            for n4 in range(4):
                def mk(o, w, r):
                    def f(e):
                        ins = e.matmul(o, w, r, start=True, stop=True)
                        ins.then_inc(st.h, 1)
                        return ins
                    return f
                sc.t(mk(qkv_ps[:], onesb_sb[:],
                        sq_sb[:, n4 * 512:n4 * 512 + 512]))
                t_nm = st.inc()
                last_sq_read[0] = t_nm
                sc.wait("vector", st, t_nm)
                if n4 == 0:
                    v_r = sc.v(lambda e, o=tgt, i_=qkv_ps[:]:
                               e.tensor_reduce(o, i_, axis=AX.X,
                                               op=AluOpType.max))
                else:
                    sc.v(lambda e, o=acc, i_=qkv_ps[:]:
                         e.tensor_reduce(o, i_, axis=AX.X, op=AluOpType.max))
                    v_r = sc.v(lambda e, o=tgt, a=tgt, b=acc:
                               e.tensor_tensor(o, a, b, op=AluOpType.max))
                sc.wait("tensor", sv, v_r)
                yield ("bound", p, ci, n4)
        v_k = sc.v(lambda e, o=nmax_sb[:, 3:4], a=nmax_sb[:, 1:2],
                   b=nmax_sb[:, 2:3]:
                   e.tensor_tensor(o, a, b, op=AluOpType.max))
        sc.wait("scalar", sv, v_k)
        sc.s(lambda e, o=nmax_sb[:, 4:5], i_=nmax_sb[:, 0:1]:
             e.activation(out=o, in_=i_, func=AF.Ln, bias=0.0))
        s_ln = sc.s(lambda e, o=nmax_sb[:, 5:6], i_=nmax_sb[:, 3:4]:
                    e.activation(out=o, in_=i_, func=AF.Ln, bias=0.0))
        sc.wait("vector", ss, s_ln)
        v_add = sc.v(lambda e, o=nmax_sb[:, 6:7], a=nmax_sb[:, 4:5],
                     b=nmax_sb[:, 5:6]:
                     e.tensor_tensor(o, a, b, op=AluOpType.add))
        sc.wait("scalar", sv, v_add)
        s_B = sc.s(lambda e, o=nmax_sb[:, 7:8], i_=nmax_sb[:, 6:7], b=mln8_ap:
                   e.activation(out=o, in_=i_, func=AF.Exp, scale=0.5,
                                bias=b))
        sc.wait("vector", ss, s_B)
        v_nb = sc.v(lambda e, o=negB_sb[:, p:p + 1], i_=nmax_sb[:, 7:8]:
                    e.tensor_scalar(o, i_, -1.0, -1.0, op0=AluOpType.mult,
                                    op1=AluOpType.add))
        yield ("negB", p, 0)
    state["v_negB"][sg] = v_nb


def p2_subgroup(sc, L, sg, state, gen):
    st, sv, ss = L["st"], L["sv"], L["ss"]
    kT_sb, v_sb, qT_sb, PT_sb, zb_sb, attnT_sb = (
        L["kT_sb"], L["v_sb"], L["qT_sb"], L["PT_sb"], L["zb_sb"],
        L["attnT_sb"])
    onesb_sb, negB_sb = L["onesb_sb"], L["negB_sb"]
    score_ps, pv_ps, z_ps = L["score_ps"], L["pv_ps"], L["z_ps"]
    par = sg % 2
    kbase = par * T_FULL
    qbase = par * T_OWN
    it_hist = state["it_hist"]
    p = sg

    def emit_scores(qt, kt, guard):
        spar = kt % 2
        sps = score_ps[:, spar * 1024:spar * 1024 + 1024]
        kA = kT_sb[0:64, kbase + kt * 128:kbase + kt * 128 + 128]
        kB = kT_sb[64:128, kbase + kt * 128:kbase + kt * 128 + 128]
        qA = qT_sb[0:64, qbase + qt * 512:qbase + qt * 512 + 512]
        qB = qT_sb[64:128, qbase + qt * 512:qbase + qt * 512 + 512]
        if guard is not None:
            sc.wait("tensor", ss, guard)
        sc.t(lambda e, o=sps[:, 0:512], w=kA, r=qA:
             e.matmul(o, w, r, start=True, stop=True))
        sc.t(lambda e, o=sps[:, 512:1024], w=kB, r=qB:
             e.matmul(o, w, r, start=True, stop=True).then_inc(st.h, 1))
        return st.inc()

    def emit_pvz(qt, kt, it, exp_done_ref=None):
        spar = kt % 4
        pt = PT_sb[:, spar * 1024:spar * 1024 + 1024]
        vch = v_sb[:, kbase + kt * 128:kbase + kt * 128 + 128]
        sc.wait("tensor", ss, exp_done_ref[kt])
        if kt == 0 and it >= 1:
            sc.wait("tensor", sv, it_hist[it - 1]["norm_v"])
        elif kt == 0:
            sc.wait("tensor", sv, state["pvz_free"])
        s0, s1 = (kt == 0), (kt == NKT - 1)
        sc.t(lambda e, o=pv_ps[0:64, :], w=vch[:, 0:64], r=pt[:, 0:512],
             a=s0, b=s1: e.matmul(o, w, r, start=a, stop=b,
                                  skip_group_check=True))
        sc.t(lambda e, o=pv_ps[64:128, :], w=vch[:, 64:128],
             r=pt[:, 512:1024], a=s0, b=s1:
             e.matmul(o, w, r, start=a, stop=b, skip_group_check=True))
        sc.t(lambda e, o=z_ps[0:64, :], w=onesb_sb[:, 0:64],
             r=pt[:, 0:512], a=s0, b=s1:
             e.matmul(o, w, r, start=a, stop=b, skip_group_check=True))
        sc.t(lambda e, o=z_ps[64:128, :], w=onesb_sb[:, 0:64],
             r=pt[:, 512:1024], a=s0, b=s1:
             e.matmul(o, w, r, start=a, stop=b,
                      skip_group_check=True).then_inc(st.h, 1))
        return st.inc()

    def finish_qt(h, pqt, hit):
        sc.wait("tensor", ss, h["exp_done"][NKT - 1])
        h["pvz_done"][NKT - 1] = emit_pvz(pqt, NKT - 1, hit, h["exp_done"])
        if gen is not None:
            next(gen, None)
            next(gen, None)
        zbh = zb_sb[:]
        sc.wait("vector", st, h["pvz_done"][NKT - 1])
        sc.v(lambda e, o=zbh, i_=z_ps[:]: e.reciprocal(o, i_))
        dst = attnT_sb[:, p * T_OWN + pqt * 512:p * T_OWN + pqt * 512 + 512]
        h["norm_v"] = sc.v(lambda e, o=dst, a=pv_ps[:], b=zbh:
                           e.tensor_tensor(o, a, b, op=AluOpType.mult))

    first_of_sg = True
    for qt in range(NQT):
        it = len(it_hist)
        exp_done = []
        sc_done = []
        pvz_done = {}
        gprev = it_hist[it - 1]["exp_done"][NKT - 2] if it >= 1 else None
        sc_done.append(emit_scores(qt, 0, gprev))
        if first_of_sg:
            sc.wait("scalar", sv, state["v_negB"][sg])
            first_of_sg = False
        if qt >= 1:
            finish_qt(it_hist[it - 1], qt - 1, it - 1)
        it_hist.append(dict(exp_done=exp_done, pvz_done=pvz_done, norm_v=0))
        for kt in range(NKT):
            if kt + 1 < NKT:
                if kt + 1 >= 2:
                    gg = exp_done[kt - 1]
                elif it >= 1:
                    gg = it_hist[it - 1]["exp_done"][NKT - 1]
                else:
                    gg = None
                sc_done.append(emit_scores(qt, kt + 1, gg))
            else:
                sc.wait("tensor", ss, exp_done[kt - 1])
            if kt % 2 == 1:
                if kt >= 2:
                    pvz_done[kt - 2] = emit_pvz(qt, kt - 2, it, exp_done)
                pvz_done[kt - 1] = emit_pvz(qt, kt - 1, it, exp_done)
            spar = kt % 4
            pt = PT_sb[:, spar * 1024:spar * 1024 + 1024]
            g1 = sc_done[kt]
            if kt >= 4:
                g1 = max(g1, pvz_done[kt - 4])
            sc.wait("scalar", st, g1)
            if kt < 4 and it >= 1:
                sc.wait("scalar", st,
                        it_hist[it - 1]["pvz_done"][NKT - 4 + kt])
            s_exp = sc.s(lambda e, o=pt,
                         i_=score_ps[:, (kt % 2) * 1024:(kt % 2) * 1024 + 1024],
                         b=negB_sb[:, p:p + 1]:
                         e.activation(out=o, in_=i_, func=AF.Exp, bias=b,
                                      scale=0.125))
            exp_done.append(s_exp)
            if gen is not None and kt % 2 == 0:
                next(gen, None)
    finish_qt(it_hist[-1], NQT - 1, len(it_hist) - 1)


def p34(sc, L):
    dxp, dwo, dop = L["dxp"], L["dwo"], L["dop"]
    st, sv, ss = L["st"], L["sv"], L["ss"]
    attnT_sb, wout_sb = L["attnT_sb"], L["wout_sb"]
    x2_sb, x2bf_sb, x2T_sb, xres_sb = (L["x2_sb"], L["x2bf_sb"], L["x2T_sb"],
                                       L["xres_sb"])
    boutbc_sb, bffobc_sb = L["boutbc_sb"], L["bffobc_sb"]
    sT_sb, hT_sb, gT_sb, u_sb = L["sT_sb"], L["hT_sb"], L["gT_sb"], L["u_sb"]
    stats2_sb, mv2_sb, rstd2_sb = L["stats2_sb"], L["mv2_sb"], L["rstd2_sb"]
    A_ps, B_ps, C_ps, F_ps = L["A_ps"], L["B_ps"], L["C_ps"], L["F_ps"]
    onesf_sb, boutrow_sb, bfforow_sb = (L["onesf_sb"], L["boutrow_sb"],
                                        L["bfforow_sb"])
    win_sb, rw1_sb, w2b_sb, wfo_sb = (L["win_sb"], L["rw1_sb"], L["w2b_sb"],
                                      L["wfo_sb"])
    bin_sb, b1T_sb, b2_sb = L["bin_sb"], L["b1T_sb"], L["b2_sb"]
    ident_sb = L["ident_sb"]
    x_hbm, out_hbm = L["x_hbm"], L["out_hbm"]
    eps_ap = L["consts_sb"][:, 0:1]

    for dstw, srcw in (
        (wout_sb[:].rearrange("p (kc m) -> p kc m", kc=8),
         L["wout"].rearrange("(kc p) m -> p kc m", p=128)),
        (win_sb[:].rearrange("p (kc m) -> p kc m", kc=8),
         L["win"].rearrange("(kc p) m -> p kc m", p=128)),
        (rw1_sb[:], L["rw1"]),
        (w2b_sb[:].rearrange("p (kc m) -> p kc m", kc=8),
         L["w2b"].rearrange("(kc p) m -> p kc m", p=128)),
        (wfo_sb[:], L["wfo"]),
        (boutrow_sb[:], L["bout"][None, :]),
        (bfforow_sb[:], L["bffo"][None, :]),
    ):
        sc.g(lambda e, d=dstw, s=srcw, hh=dwo.h:
             e.dma_start(out=d, in_=s).then_inc(hh, 16))
        dwo.inc(16)
    w_all = dwo.n
    sc.wait("tensor", dwo, w_all)
    sc.wait("scalar", dwo, w_all)
    sc.wait("vector", dwo, w_all)

    v_bcp = 0
    for (bcdst, brow) in ((boutbc_sb, boutrow_sb), (bffobc_sb, bfforow_sb)):
        sc.wait("tensor", sv, v_bcp)
        for nh in range(2):
            last = nh == 1
            def mk(o, w, r, last=last):
                def f(e):
                    ins = e.matmul(o, w, r, start=True, stop=True)
                    if last:
                        ins.then_inc(st.h, 1)
                    return ins
                return f
            sc.t(mk(A_ps[:, nh * 512:nh * 512 + 512], onesf_sb[:],
                    brow[:, nh * 512:nh * 512 + 512]))
        t_bc = st.inc()
        sc.wait("vector", st, t_bc)
        v_bcp = sc.v(lambda e, o=bcdst[:], i_=A_ps[:, 0:1024]:
                     e.tensor_copy(o, i_))
    sc.wait("tensor", sv, v_bcp)

    def batched_ln(uslot0, outs, outs2, slot):
        stt = stats2_sb[:, slot * 48:slot * 48 + 48]
        mv = mv2_sb[:, slot * 8:slot * 8 + 8]
        rst = rstd2_sb[:, slot * 4:slot * 4 + 4]
        for t in range(4):
            u_ap = u_sb[:, (uslot0 + t) * C:(uslot0 + t + 1) * C]
            sc.v(lambda e, o=stt[:, t * 12:t * 12 + 6], i_=u_ap[:, 0:512]:
                 e.bn_stats(o, i_))
            sc.v(lambda e, o=stt[:, t * 12 + 6:t * 12 + 12],
                 i_=u_ap[:, 512:1024]: e.bn_stats(o, i_))
        v_st = 0
        for t in range(4):
            v_st = sc.v(lambda e, o=mv[:, t * 2:t * 2 + 2],
                        i_=stt[:, t * 12:t * 12 + 12]: e.bn_aggr(o, i_))
        sc.wait("scalar", sv, v_st)
        var_ap = mv[:].rearrange("p (t two) -> p two t", two=2)[:, 1, :]
        sc.s(lambda e, o=rst, i_=var_ap, b=eps_ap:
             e.activation(out=o, in_=i_, func=AF.Ln, bias=b))
        s_r = sc.s(lambda e, o=rst, i_=rst:
                   e.activation(out=o, in_=i_, func=AF.Exp, scale=-0.5))
        sc.wait("vector", ss, s_r)
        rets = []
        for t in range(4):
            u_ap = u_sb[:, (uslot0 + t) * C:(uslot0 + t + 1) * C]
            v_ap = sc.v(lambda e, o=outs[t], i_=u_ap, m=mv[:, t * 2:t * 2 + 1],
                        r=rst[:, t:t + 1]:
                        e.tensor_scalar(o, i_, m, r, op0=AluOpType.subtract,
                                        op1=AluOpType.mult))
            if outs2[t] is not None:
                sc.wait("scalar", sv, v_ap)
                s_c = sc.s(lambda e, o=outs2[t], i_=outs[t]:
                           e.copy(out=o, in_=i_))
                rets.append(("s", s_c))
            else:
                rets.append(("v", v_ap))
        return rets

    a_guard_v = [0, 0]
    b_guard = [0, 0]
    o_stored = [0, 0, 0, 0]
    last_xrb = [0, 0, 0, 0]
    t_sT_read = [0, 0]
    t_h_read = [0, 0]
    t_g_read = [0, 0]
    s_binev = [0, 0]
    s_gev = [0, 0]
    f_guard_v = [0]
    f_guard_s = [0]

    x2T_done = [0, 0]
    x2cast_st = [[0] * 4, [0] * 4]

    def p3a_stage(sl):
            spar = sl % 2
            t_ops = []
            for tt in range(4):
                half = tt % 2
                row0 = sl * 512 + tt * 128
                ap = A_ps[:, half * 1024:half * 1024 + 1024]
                xr = xres_sb[:, tt * C:(tt + 1) * C]
                sc.wait("sync", sv, last_xrb[tt])
                dxs = dxp[tt]
                sc.y(lambda e, d=xr, s=x_hbm[row0:row0 + 128, :], hh=dxs.h:
                     e.dma_start(out=d, in_=s).then_inc(hh, 16))
                v_dx = (dxs, dxs.inc(16))
                xrb = xr
                sc.wait("vector", v_dx[0], v_dx[1])
                v_xrb = sc.v(lambda e, o=xrb, i0=xr, b=boutbc_sb[:]:
                             e.scalar_tensor_tensor(o, i0, 2.0, b,
                                                    op0=AluOpType.mult,
                                                    op1=AluOpType.add))
                last_xrb[tt] = v_xrb
                sc.wait("tensor", sv, a_guard_v[half])
                for nh in range(2):
                    for kc in range(8):
                        last = (nh == 1 and kc == 7)
                        def mk(o, w, r, s0, s1, last=last):
                            def f(e):
                                ins = e.matmul(o, w, r, start=s0, stop=s1)
                                if last:
                                    ins.then_inc(st.h, 1)
                                return ins
                            return f
                        sc.t(mk(ap[:, nh * 512:nh * 512 + 512],
                                attnT_sb[:, kc * T_OWN + row0:
                                         kc * T_OWN + row0 + 128],
                                wout_sb[:, kc * 1024 + nh * 512:
                                        kc * 1024 + nh * 512 + 512],
                                kc == 0, kc == 7))
                t_op = st.inc()
                u = u_sb[:, tt * C:(tt + 1) * C]
                sc.wait("vector", st, t_op)
                a_guard_v[half] = sc.v(lambda e, o=u, a=ap, b=xrb:
                                       e.tensor_tensor(o, a, b,
                                                       op=AluOpType.add))
                last_xrb[tt] = a_guard_v[half]

    def p3bc_stage(sl):
            spar = sl % 2
            outs = [x2_sb[:, spar * 4 * C + t * C:spar * 4 * C + (t + 1) * C]
                    for t in range(4)]
            outs2 = [x2bf_sb[:, t * C:(t + 1) * C] for t in range(4)]
            ln_rets = batched_ln(0, outs, outs2, 0)
            x2cast_st[spar] = [c for (k, c) in ln_rets]
            for tt in range(4):
                tph = B_ps[:]
                kind, cnt = ln_rets[tt]
                sc.wait("tensor", ss if kind == "s" else sv, cnt)
                sc.wait("tensor", ss, b_guard[0])
                for kc in range(8):
                    last = kc == 7
                    def mk(o, i_, idn, last=last):
                        def f(e):
                            ins = e.transpose(o, i_, idn)
                            if last:
                                ins.then_inc(st.h, 1)
                            return ins
                        return f
                    sc.t(mk(tph[:, kc * 128:kc * 128 + 128],
                            x2bf_sb[:, tt * C + kc * 128:tt * C + kc * 128 + 128],
                            ident_sb[:]))
                t_tp = st.inc()
                sc.wait("scalar", st, t_tp)
                sc.wait("scalar", st, t_sT_read[spar])
                dst = x2T_sb[:].rearrange("p (sp kc t) -> p sp kc t",
                                          sp=2, kc=8)[
                    :, spar, :, tt * 128:tt * 128 + 128]
                srcv = tph.rearrange("p (kc t) -> p kc t", kc=8)
                b_guard[0] = sc.s(lambda e, o=dst, i_=srcv:
                                  e.copy(out=o, in_=i_))
            x2T_done[sl % 2] = max(b_guard)


    def p4ab_stage(sl):
            spar = sl % 2
            x2Tsl = x2T_sb[:, spar * 4096:spar * 4096 + 4096]
            sTsl = sT_sb[:, spar * 512:spar * 512 + 512]
            hTsl = hT_sb[:, spar * 4096:spar * 4096 + 4096]
            gTsl = gT_sb[:, spar * 512:spar * 512 + 512]
            sc.wait("tensor", ss, x2T_done[spar])
            sc.wait("tensor", ss, max(s_gev))
            sc.wait("tensor", ss, s_binev[spar])
            for kc in range(8):
                last = kc == 7
                def mk(o, w, r, s0, s1, last=last):
                    def f(e):
                        ins = e.matmul(o, w, r, start=s0, stop=s1)
                        if last:
                            ins.then_inc(st.h, 1)
                        return ins
                    return f
                sc.t(mk(C_ps[0:64, :], win_sb[:, kc * 64:kc * 64 + 64],
                        x2Tsl[:, kc * 512:kc * 512 + 512], kc == 0, kc == 7))
            t_sT = st.inc()
            t_sT_read[spar] = t_sT
            sc.wait("scalar", st, t_sT)
            sc.wait("scalar", st, t_h_read[spar])
            s_sT = sc.s(lambda e, o=sTsl, i_=C_ps[0:64, :], b=bin_sb[:]:
                        e.activation(out=o, in_=i_, func=AF.Identity, bias=b))
            s_binev[spar] = s_sT
            sc.wait("tensor", ss, s_sT)
            h_evac = [0, 0]
            for j in range(8):
                half = j % 2
                hp = F_ps[:, half * 512:half * 512 + 512]
                sc.wait("tensor", ss, h_evac[half])
                if j < 2:
                    sc.wait("tensor", sv, f_guard_v[0])
                    sc.wait("tensor", ss, f_guard_s[0])
                sc.t(lambda e, o=hp, w=rw1_sb[:, j * 128:j * 128 + 128],
                     r=sTsl:
                     e.matmul(o, w, r, start=True, stop=True,
                              skip_group_check=True).then_inc(st.h, 1))
                t_h = st.inc()
                t_h_read[spar] = t_h
                sc.wait("scalar", st, t_h)
                if j < 2:
                    sc.wait("scalar", st, t_g_read[spar])
                hev = sc.s(lambda e, o=hTsl[:, j * 512:j * 512 + 512], i_=hp,
                           b=b1T_sb[:, j:j + 1]:
                           e.activation(out=o, in_=i_, func=AF.Relu, bias=b))
                h_evac[half] = hev
                f_guard_s[0] = hev
            sc.wait("tensor", ss, h_evac[0])
            sc.wait("tensor", ss, h_evac[1])
            for j in range(8):
                last = j == 7
                def mk(o, w, r, s0, s1, last=last):
                    def f(e):
                        ins = e.matmul(o, w, r, start=s0, stop=s1)
                        if last:
                            ins.then_inc(st.h, 1)
                        return ins
                    return f
                sc.t(mk(C_ps[0:64, :], w2b_sb[:, j * 64:j * 64 + 64],
                        hTsl[:, j * 512:j * 512 + 512], j == 0, j == 7))
            t_g = st.inc()
            sc.wait("scalar", st, t_g)
            s_g = sc.s(lambda e, o=gTsl, i_=C_ps[0:64, :], b=b2_sb[:]:
                       e.activation(out=o, in_=i_, func=AF.Identity, bias=b))
            s_gev[spar] = s_g
            sc.wait("tensor", ss, s_g)
            for tt in range(4):
                row0 = sl * 512 + tt * 128
                fpA = F_ps[:]
                x2t = x2_sb[:, spar * 4 * C + tt * C:spar * 4 * C + (tt + 1) * C]
                x2pb = x2t
                sc.wait("vector", ss, x2cast_st[spar][tt])
                sc.v(lambda e, o=x2pb, a=x2t, b=bffobc_sb[:]:
                     e.tensor_tensor(o, a, b, op=AluOpType.add))
                sc.wait("tensor", sv, f_guard_v[0])
                sc.wait("tensor", ss, f_guard_s[0])
                for nh in range(2):
                    last = nh == 1
                    def mk(o, w, r, last=last):
                        def f(e):
                            ins = e.matmul(o, w, r, start=True, stop=True)
                            if last:
                                ins.then_inc(st.h, 1)
                            return ins
                        return f
                    sc.t(mk(fpA[:, nh * 512:nh * 512 + 512],
                            gTsl[:, tt * 128:tt * 128 + 128],
                            wfo_sb[:, nh * 512:nh * 512 + 512]))
                t_f = st.inc()
                t_g_read[spar] = t_f
                u = u_sb[:, (4 + tt) * C:(5 + tt) * C]
                sc.wait("vector", st, t_f)
                if o_stored[tt]:
                    sc.wait("vector", dop[tt % 2], dop[tt % 2].n)
                f_guard_v[0] = sc.v(lambda e, o=u, a=fpA, b=x2pb:
                                    e.tensor_tensor(o, a, b, op=AluOpType.add))

    def p4c_stage(sl):
            spar = sl % 2
            outs = [u_sb[:, (4 + t) * C:(5 + t) * C] for t in range(4)]
            v_out = batched_ln(4, outs, [None] * 4, 1)
            for tt in range(4):
                row0 = sl * 512 + tt * 128
                u = u_sb[:, (4 + tt) * C:(5 + tt) * C]
                sc.wait("sync", sv, v_out[tt][1])
                dos = dop[tt % 2]
                sc.y(lambda e, d=out_hbm[row0:row0 + 128, :], s=u, hh=dos.h:
                     e.dma_start(out=d, in_=s).then_inc(hh, 16))
                o_stored[tt] = dos.inc(16)

    for sl in range(5):
        if sl < 4:
            p3a_stage(sl)
        if sl >= 1:
            p4ab_stage(sl - 1)
        if sl < 4:
            p3bc_stage(sl)
        if sl >= 1:
            p4c_stage(sl - 1)

_NC_CACHE = {}


def kernel(**inputs):
    x = np.ascontiguousarray(np.asarray(inputs["L0_attn_x"], dtype=np.float32))
    gam = np.asarray(inputs["L0_attn_ln_gamma"], dtype=np.float32)
    bet = np.asarray(inputs["L0_attn_ln_beta"], dtype=np.float32)
    wqkv = np.ascontiguousarray(np.asarray(inputs["L0_attn_w_qkv"], dtype=np.float32))
    bqkv = np.asarray(inputs["L0_attn_b_qkv"], dtype=np.float32)
    wout = np.ascontiguousarray(np.asarray(inputs["L0_attn_w_out"], dtype=np.float32))
    bout = np.asarray(inputs["L0_attn_b_out"], dtype=np.float32)
    win = np.ascontiguousarray(np.asarray(inputs["L0_ffn_Win"], dtype=np.float32))
    binf = np.asarray(inputs["L0_ffn_bin_"], dtype=np.float32)
    w1 = np.asarray(inputs["L0_ffn_W1"], dtype=np.float32)
    b1 = np.asarray(inputs["L0_ffn_b1"], dtype=np.float32)
    w2 = np.asarray(inputs["L0_ffn_W2"], dtype=np.float32)
    b2 = np.asarray(inputs["L0_ffn_b2"], dtype=np.float32)
    wfo = np.ascontiguousarray(np.asarray(inputs["L0_ffn_Wout"], dtype=np.float32))
    bffo = np.asarray(inputs["L0_ffn_bout"], dtype=np.float32)

    M, INNER = 64, 16
    rw1 = np.zeros((M, M * INNER), np.float32)
    for m in range(M):
        rw1[m, m * INNER:(m + 1) * INNER] = w1[m]
    w2b = np.zeros((M * INNER, M), np.float32)
    for m in range(M):
        w2b[m * INNER:(m + 1) * INNER, m] = w2[m]
    b1flat = np.ascontiguousarray(b1.reshape(M * INNER))

    ident = np.eye(128, dtype=np.float32).astype(ml_dtypes.bfloat16)
    identf = np.eye(128, dtype=np.float32)
    onesb = np.ones((128, 128), np.float32).astype(ml_dtypes.bfloat16)
    onesf = np.ones((1, 128), np.float32)
    consts = np.zeros((128, 2), np.float32)
    consts[:, 0] = EPS
    consts[:, 1] = MINUS_LN8

    shared = dict(wqkv=wqkv, bqkv=bqkv, gam=gam, bet=bet, wout=wout, bout=bout,
                  win=win, binf=binf, rw1=rw1, b1f=b1flat, w2b=w2b, b2=b2,
                  wfo=wfo, bffo=bffo, ident=ident, identf=identf, onesb=onesb, onesf=onesf,
                  consts=consts)

    in_maps = []
    for c in range(8):
        b, h = c // 2, c % 2
        own = x[b, h * T_OWN:(h + 1) * T_OWN]
        other = x[b, (1 - h) * T_OWN:(2 - h) * T_OWN]
        xp = np.ascontiguousarray(np.concatenate([own, other], axis=0))
        m = dict(shared)
        m["x"] = xp
        in_maps.append(m)

    if "nc" not in _NC_CACHE:
        _NC_CACHE["nc"] = build_nc()
    nc = _NC_CACHE["nc"]
    res = run_bass_kernel_spmd(nc, in_maps, core_ids=list(range(8)))

    out = np.empty((4, T_FULL, C), np.float32)
    for c in range(8):
        b, h = c // 2, c % 2
        out[b, h * T_OWN:(h + 1) * T_OWN] = res.results[c]["out"]
    return out


# version 26
# speedup vs baseline: 1.0360x; 1.0360x over previous
from contextlib import ExitStack

import numpy as np
import ml_dtypes

import concourse.bass as bass
import concourse.mybir as mybir
from concourse.alu_op_type import AluOpType
from concourse.bass_utils import run_bass_kernel_spmd

F32 = mybir.dt.float32
BF16 = mybir.dt.bfloat16
AF = mybir.ActivationFunctionType
AX = mybir.AxisListType

T_FULL = 4096
T_OWN = 2048
C = 1024
NKC = 8
NPAIR = 8
PAIRS_PER_G = 4
NGROUP = 2
NQT = 4
NKT = 32
EPS = 1e-5
MINUS_LN8 = -2.0794415416798357


class Sem:
    def __init__(self, handle):
        self.h = handle
        self.n = 0

    def inc(self, k=1):
        self.n += k
        return self.n


class Script:

    def __init__(self, sv, ss):
        self.items = []
        self._sv = sv
        self._ss = ss

    def t(self, fn):
        self.items.append(("tensor", fn, False))

    def v(self, fn):
        self.items.append(("vector", fn, True))
        return self._sv.inc()

    def s(self, fn):
        self.items.append(("scalar", fn, True))
        return self._ss.inc()

    def y(self, fn):
        self.items.append(("sync", fn, False))

    def g(self, fn):
        self.items.append(("gpsimd", fn, False))

    def wait(self, eng, sem, val):
        if val <= 0:
            return
        self.items.append(
            (eng, lambda e, s=sem.h, v=val: e.wait_ge(s, v), False))

    def run(self, engine_name, eng, chain_sem, chain_base):
        n = chain_base
        for name, fn, chained in self.items:
            if name != engine_name:
                continue
            if chained:
                if n > 0:
                    eng.wait_ge(chain_sem, n)
                ins = fn(eng)
                assert ins is not None, "chained entry must return instruction"
                ins.then_inc(chain_sem, 1)
                n += 1
            else:
                fn(eng)
        return n


def emit_block(nc, script, L):
    sv, ss = L["sv"], L["ss"]
    with nc.Block() as block:
        @block.tensor
        def _(t):
            script.run("tensor", t, None, 0)

        @block.vector
        def _(v):
            base = L["chain_done"]["vector"]
            L["chain_done"]["vector"] = script.run("vector", v, sv.h, base)

        @block.scalar
        def _(s):
            base = L["chain_done"]["scalar"]
            L["chain_done"]["scalar"] = script.run("scalar", s, ss.h, base)

        @block.sync
        def _(y):
            script.run("sync", y, None, 0)

        @block.gpsimd
        def _(g):
            script.run("gpsimd", g, None, 0)


def barrier(sc, L):
    st, sv, ss, sy, sg = L["st"], L["sv"], L["ss"], L["sy"], L["sg"]
    t_t = st.inc()
    sc.t(lambda e, s=st.h: e.drain().then_inc(s, 1))
    t_v = sc.v(lambda e: e.drain())
    t_s = sc.s(lambda e: e.drain())
    t_y = sy.inc()
    sc.y(lambda e, s=sy.h: e.drain().then_inc(s, 1))
    t_g = sg.inc()
    sc.g(lambda e, s=sg.h: e.drain().then_inc(s, 1))
    targets = {"tensor": (st, t_t), "vector": (sv, t_v), "scalar": (ss, t_s),
               "sync": (sy, t_y), "gpsimd": (sg, t_g)}
    for e in targets:
        for e2, (sem, val) in targets.items():
            if e2 != e:
                sc.wait(e, sem, val)


def build_nc():
    nc = bass.Bass("TRN2", target_bir_lowering=False, debug=False)

    L = {}
    L["x_hbm"] = nc.dram_tensor("x", [T_FULL, C], F32, kind="ExternalInput").ap()
    L["wqkv"] = nc.dram_tensor("wqkv", [C, 3 * C], F32, kind="ExternalInput").ap()
    L["bqkv"] = nc.dram_tensor("bqkv", [3 * C], F32, kind="ExternalInput").ap()
    L["gam"] = nc.dram_tensor("gam", [C], F32, kind="ExternalInput").ap()
    L["bet"] = nc.dram_tensor("bet", [C], F32, kind="ExternalInput").ap()
    L["wout"] = nc.dram_tensor("wout", [C, C], F32, kind="ExternalInput").ap()
    L["bout"] = nc.dram_tensor("bout", [C], F32, kind="ExternalInput").ap()
    L["win"] = nc.dram_tensor("win", [C, 64], F32, kind="ExternalInput").ap()
    L["binf"] = nc.dram_tensor("binf", [64], F32, kind="ExternalInput").ap()
    L["rw1"] = nc.dram_tensor("rw1", [64, C], F32, kind="ExternalInput").ap()
    L["b1f"] = nc.dram_tensor("b1f", [C], F32, kind="ExternalInput").ap()
    L["w2b"] = nc.dram_tensor("w2b", [C, 64], F32, kind="ExternalInput").ap()
    L["b2"] = nc.dram_tensor("b2", [64], F32, kind="ExternalInput").ap()
    L["wfo"] = nc.dram_tensor("wfo", [64, C], F32, kind="ExternalInput").ap()
    L["bffo"] = nc.dram_tensor("bffo", [C], F32, kind="ExternalInput").ap()
    L["ident_h"] = nc.dram_tensor("ident", [128, 128], BF16, kind="ExternalInput").ap()
    L["identf_h"] = nc.dram_tensor("identf", [128, 128], F32, kind="ExternalInput").ap()
    L["onesb_h"] = nc.dram_tensor("onesb", [128, 128], BF16, kind="ExternalInput").ap()
    L["onesf_h"] = nc.dram_tensor("onesf", [1, 128], F32, kind="ExternalInput").ap()
    L["consts_h"] = nc.dram_tensor("consts", [128, 2], F32, kind="ExternalInput").ap()
    L["out_hbm"] = nc.dram_tensor("out", [T_OWN, C], F32, kind="ExternalOutput").ap()

    es = ExitStack()
    with es:
        def SB(name, shape, dt):
            return es.enter_context(nc.sbuf_tensor(name, shape, dt))

        def SEM(name):
            return es.enter_context(nc.semaphore(name))

        L.update(
            ident_sb=SB("ident_sb", [128, 128], BF16),
            identf_sb=SB("identf_sb", [128, 128], F32),
            onesb_sb=SB("onesb_sb", [128, 128], BF16),
            onesf_sb=SB("onesf_sb", [1, 128], F32),
            consts_sb=SB("consts_sb", [128, 2], F32),
            bqkvT_sb=SB("bqkvT_sb", [128, 24], F32),
            gamT_sb=SB("gamT_sb", [128, 8], F32),
            betT_sb=SB("betT_sb", [128, 8], F32),
            bin_sb=SB("bin_sb", [64, 1], F32),
            b1T_sb=SB("b1T_sb", [128, 8], F32),
            b2_sb=SB("b2_sb", [64, 1], F32),
            negB_sb=SB("negB_sb", [128, NPAIR], F32),
            attnT_sb=SB("attnT_sb", [128, NPAIR * T_OWN], BF16),
        )
        dc, dg = Sem(SEM("s_dc")), Sem(SEM("s_dg2"))
        dxp = [Sem(SEM("s_dx0")), Sem(SEM("s_dx1")),
               Sem(SEM("s_dx2")), Sem(SEM("s_dx3"))]
        dwp = [Sem(SEM("s_dw0")), Sem(SEM("s_dw1"))]
        dwo = Sem(SEM("s_dwo"))
        dop = [Sem(SEM("s_do0")), Sem(SEM("s_do1"))]
        st, sv, ss = Sem(SEM("s_t")), Sem(SEM("s_v")), Sem(SEM("s_s"))
        sy, sg = Sem(SEM("s_y")), Sem(SEM("s_g"))
        L.update(dc=dc, dg=dg, dxp=dxp, dwp=dwp, dwo=dwo, dop=dop,
                 st=st, sv=sv, ss=ss, sy=sy, sg=sg,
                 chain_done={"vector": 0, "scalar": 0})

        with ExitStack() as es12:
            def SB12(name, shape, dt):
                return es12.enter_context(nc.sbuf_tensor(name, shape, dt))

            L.update(
                kT_sb=SB12("kT_sb", [128, 2 * T_FULL], BF16),
                v_sb=SB12("v_sb", [128, 2 * T_FULL], BF16),
                qT_sb=SB12("qT_sb", [128, 2 * T_OWN], BF16),
                xhq_sb=SB12("xhq_sb", [128, NKC * T_FULL], BF16),
                xtile_sb=SB12("xtile_sb", [128, 3 * C], F32),
                xh_tile=SB12("xh_tile", [128, 2 * C], BF16),
                wq_sb=SB12("wq_sb", [128, 2 * 1024], BF16),
                wq3_sb=SB12("wq3_sb", [128, 3 * 1024], BF16),
                vstage_sb=SB12("vstage_sb", [128, 8 * 512], BF16),
                vtmp_sb=SB12("vtmp_sb", [128, 2 * 512], BF16),
                stats_sb=SB12("stats_sb", [128, 12], F32),
                mv_sb=SB12("mv_sb", [128, 8], F32),
                rstd_sb=SB12("rstd_sb", [128, 4], F32),
                nmax_sb=SB12("nmax_sb", [128, 8], F32),
                PT_sb=SB12("PT_sb", [128, 4 * 1024], BF16),
                zb_sb=SB12("zb_sb", [128, 512], F32),
            )
            L["sq_sb"] = L["xh_tile"]
            with ExitStack() as esA:
                L["score_ps"] = esA.enter_context(
                    nc.psum_tensor("score_ps", [128, 2048], F32))
                L["pv_ps"] = esA.enter_context(
                    nc.psum_tensor("pv_ps", [128, 512], F32))
                L["z_ps"] = esA.enter_context(
                    nc.psum_tensor("z_ps", [128, 512], F32))
                L["qkv_ps"] = esA.enter_context(
                    nc.psum_tensor("qkv_ps", [128, 512], F32))
                L["tp_ps"] = esA.enter_context(
                    nc.psum_tensor("tp_ps", [128, 1024], BF16))
                sc = Script(sv, ss)
                p0_loads(sc, L)
                L["plstate"] = {
                    "lqe": 0,
                    "last_wq_consumed": [0, 0],
                    "last_tp_evac": 0,
                    "last_tp_evac_v": 0,
                    "last_qkv_evac": 0,
                    "last_vtmp_read": [0, 0],
                    "v_negB": [0] * 8,
                    "kv_parity_free": [0, 0],
                    "pvz_free": 0,
                    "vstage_ev": [0] * 8,
                    "it_hist": [],
                }
                phase_ln(sc, L)
                pipeline(sc, L)
                barrier(sc, L)
                emit_block(nc, sc, L)

        with ExitStack() as es34:
            def SB34(name, shape, dt):
                return es34.enter_context(nc.sbuf_tensor(name, shape, dt))

            L.update(
                wout_sb=SB34("wout_sb", [128, 8 * C], BF16),
                boutrow_sb=SB34("boutrow_sb", [1, C], F32),
                bfforow_sb=SB34("bfforow_sb", [1, C], F32),
                win_sb=SB34("win_sb", [128, 8 * 64], BF16),
                rw1_sb=SB34("rw1_sb", [64, C], BF16),
                w2b_sb=SB34("w2b_sb", [128, 8 * 64], BF16),
                wfo_sb=SB34("wfo_sb", [64, C], BF16),
                boutbc_sb=SB34("boutbc_sb", [128, C], F32),
                bffobc_sb=SB34("bffobc_sb", [128, C], F32),
                x2_sb=SB34("x2_sb", [128, 8 * C], F32),
                x2bf_sb=SB34("x2bf_sb", [128, 4 * C], BF16),
                x2T_sb=SB34("x2T_sb", [128, 2 * 4096], BF16),
                xres_sb=SB34("xres_sb", [128, 4 * C], F32),
                sT_sb=SB34("sT_sb", [64, 2 * 512], BF16),
                hT_sb=SB34("hT_sb", [128, 2 * 4096], BF16),
                gT_sb=SB34("gT_sb", [64, 2 * 512], BF16),
                u_sb=SB34("u_sb", [128, 8 * C], F32),
                stats2_sb=SB34("stats2_sb", [128, 96], F32),
                mv2_sb=SB34("mv2_sb", [128, 16], F32),
                rstd2_sb=SB34("rstd2_sb", [128, 8], F32),
            )
            L["A_ps"] = es34.enter_context(
                nc.psum_tensor("A_ps", [128, 2048], F32))
            L["F_ps"] = es34.enter_context(
                nc.psum_tensor("F_ps", [128, 1024], F32))
            L["B_ps"] = es34.enter_context(
                nc.psum_tensor("B_ps", [128, 1024], BF16))
            L["C_ps"] = es34.enter_context(
                nc.psum_tensor("C_ps", [64, 512], F32))
            sc = Script(sv, ss)
            p34(sc, L)
            sc.wait("sync", dop[0], dop[0].n)
            sc.wait("sync", dop[1], dop[1].n)
            emit_block(nc, sc, L)

    return nc



def p0_loads(sc, L):
    dc, dg = L["dc"], L["dg"]

    def ld(dst, src, slow=False):
        sc.y(lambda e, d=dst, s=src, sl=slow: e.dma_start(
            out=d, in_=s, allow_slow_non_contiguous=sl).then_inc(dc.h, 16))
        dc.inc(16)

    def ldg(dst, src):
        sc.g(lambda e, d=dst, s=src: e.dma_start(
            out=d, in_=s).then_inc(dg.h, 16))
        dg.inc(16)

    ld(L["ident_sb"][:], L["ident_h"])
    ld(L["identf_sb"][:], L["identf_h"])
    ld(L["onesb_sb"][:], L["onesb_h"])
    ld(L["onesf_sb"][:], L["onesf_h"])
    ld(L["consts_sb"][:], L["consts_h"])
    ld(L["bqkvT_sb"][:], L["bqkv"].rearrange("(m p) -> p m", p=128), slow=True)
    ld(L["gamT_sb"][:], L["gam"].rearrange("(m p) -> p m", p=128), slow=True)
    ld(L["betT_sb"][:], L["bet"].rearrange("(m p) -> p m", p=128), slow=True)
    ld(L["bin_sb"][:], L["binf"][:, None], slow=True)
    ld(L["b1T_sb"][:], L["b1f"].rearrange("(m p) -> p m", p=128), slow=True)
    ld(L["b2_sb"][:], L["b2"][:, None], slow=True)
    for e in ("tensor", "vector", "scalar"):
        sc.wait(e, dc, dc.n)
        sc.wait(e, dg, dg.n)


def phase_ln(sc, L):
    dxp, dwo = L["dxp"], L["dwo"]
    st, sv, ss = L["st"], L["sv"], L["ss"]
    xtile_sb, xh_tile, stats_sb, mv_sb, rstd_sb = (
        L["xtile_sb"], L["xh_tile"], L["stats_sb"], L["mv_sb"], L["rstd_sb"])
    xhq_sb, wq3_sb = L["xhq_sb"], L["wq3_sb"]
    kT_sb, qT_sb = L["kT_sb"], L["qT_sb"]
    qkv_ps, score_ps = L["qkv_ps"], L["score_ps"]
    x_hbm, wqkv = L["x_hbm"], L["wqkv"]
    ident_sb, gamT_sb, betT_sb, bqkvT_sb = (
        L["ident_sb"], L["gamT_sb"], L["betT_sb"], L["bqkvT_sb"])
    eps_ap = L["consts_sb"][:, 0:1]
    state = L["plstate"]

    dws = dwo
    for j, m in enumerate((8, 16, 0)):
        srcw = wqkv[:, 128 * m:128 * m + 128].rearrange(
            "(kc p) c -> p kc c", p=128)
        sc.g(lambda e, d=wq3_sb[:, j * 1024:(j + 1) * 1024].rearrange(
             "p (kc c) -> p kc c", kc=8), s=srcw, hh=dws.h:
             e.dma_start(out=d, in_=s).then_inc(hh, 16))
        dws.inc(16)
    w3_ready = dws.n

    NB = 3
    dma_cnt = [0] * NB
    v_xh_t = [0] * 32
    s_rstd_t = [0] * 32
    s_evac_t = [None] * 32
    last_xh_read = [0, 0]
    last_score_evac = [0, 0]
    v_gb_slice = [0] * 8
    last_qkv_evac = [0]
    w3_waited = [False]

    def issue_dma(j, guard):
        if j > 31:
            return
        b = j % NB
        xt = xtile_sb[:, b * C:(b + 1) * C]
        if guard:
            sc.wait("sync", sv, guard)
        dxs = dxp[b]
        sc.y(lambda e, d=xt, s=x_hbm[j * 128:j * 128 + 128, :], hh=dxs.h:
             e.dma_start(out=d, in_=s).then_inc(hh, 16))
        dma_cnt[b] = dxs.inc(16)

    pend = []
    for n in range(8):
        pend.append((8, n))
        pend.append((16, n))
        if n < 4:
            pend.append((0, n))
    pi = [0]
    pending_evac = []

    def qkv_burst():
        if pi[0] >= len(pend):
            return False
        m, n = pend[pi[0]]
        if not v_gb_slice[n]:
            return False
        pi[0] += 1
        if not w3_waited[0]:
            sc.wait("tensor", dws, w3_ready)
            w3_waited[0] = True
        j = {8: 0, 16: 1, 0: 2}[m]
        wbuf = wq3_sb[:, j * 1024:(j + 1) * 1024]
        sc.wait("tensor", sv, v_gb_slice[n])
        sc.wait("tensor", ss, last_qkv_evac[0])
        for kc in range(8):
            last = kc == 7
            def mk(o, w, r, s0, s1, last=last):
                def f(e):
                    ins = e.matmul(o, w, r, start=s0, stop=s1)
                    if last:
                        ins.then_inc(st.h, 1)
                    return ins
                return f
            sc.t(mk(qkv_ps[:], wbuf[:, kc * 128:kc * 128 + 128],
                    xhq_sb[:, kc * T_FULL + n * 512:
                           kc * T_FULL + n * 512 + 512],
                    kc == 0, kc == 7))
        pending_evac.append((st.inc(), m, n))
        return True

    def qkv_evac():
        if not pending_evac:
            return False
        t_mm, m, n = pending_evac.pop(0)
        if m == 8:
            dst = kT_sb[:, n * 512:n * 512 + 512]
        elif m == 16:
            dst = L["vstage_sb"][:, n * 512:n * 512 + 512]
        else:
            dst = qT_sb[:, n * 512:n * 512 + 512]
        sc.wait("scalar", st, t_mm)
        s_ev = sc.s(lambda e, o=dst, i_=qkv_ps[:], b=bqkvT_sb[:, m:m + 1]:
                    e.activation(out=o, in_=i_, func=AF.Identity, bias=b))
        last_qkv_evac[0] = s_ev
        if m == 16:
            state["vstage_ev"][n] = s_ev
        return True

    issue_dma(0, 0)
    issue_dma(1, 0)
    for j in range(33):
        if j < 32:
            b = j % NB
            xt = xtile_sb[:, b * C:(b + 1) * C]
            sc.wait("vector", dxp[b], dma_cnt[b])
            sc.v(lambda e, o=stats_sb[:, 0:6], i_=xt[:, 0:512]:
                 e.bn_stats(o, i_))
            sc.v(lambda e, o=stats_sb[:, 6:12], i_=xt[:, 512:1024]:
                 e.bn_stats(o, i_))
            mvb = mv_sb[:, 2 * (j % 2):2 * (j % 2) + 2]
            v_stats = sc.v(lambda e, o=mvb, i_=stats_sb[:]: e.bn_aggr(o, i_))
            rst = rstd_sb[:, j % 2:j % 2 + 1]
            sc.wait("scalar", sv, v_stats)
            sc.s(lambda e, o=rst, i_=mvb[:, 1:2], b=eps_ap:
                 e.activation(out=o, in_=i_, func=AF.Ln, bias=b))
            s_rstd_t[j] = sc.s(lambda e, o=rst, i_=rst:
                               e.activation(out=o, in_=i_, func=AF.Exp,
                                            scale=-0.5))
        if j >= 1:
            i = j - 1
            qkv_evac()
            par = i % 2
            bi = i % NB
            xt_i = xtile_sb[:, bi * C:(bi + 1) * C]
            xht = xh_tile[:, par * C:(par + 1) * C]
            mvb = mv_sb[:, 2 * par:2 * par + 2]
            rst = rstd_sb[:, par:par + 1]
            sc.wait("vector", ss, s_rstd_t[i])
            sc.wait("vector", st, last_xh_read[par])
            v_xh = sc.v(lambda e, o=xht, i_=xt_i, m=mvb[:, 0:1], r=rst:
                        e.tensor_scalar(o, i_, m, r, op0=AluOpType.subtract,
                                        op1=AluOpType.mult))
            v_xh_t[i] = v_xh
            issue_dma(i + 2, v_xh_t[i - 1] if i >= 1 else 0)
            sps = score_ps[:, par * 1024:par * 1024 + 1024]
            sc.wait("tensor", sv, v_xh)
            sc.wait("tensor", ss, last_score_evac[par])
            for kc in range(8):
                last = kc == 7
                def mk(o, w, r, last=last):
                    def f(e):
                        ins = e.matmul(o, w, r, start=True, stop=True,
                                       skip_group_check=True)
                        if last:
                            ins.then_inc(st.h, 1)
                        return ins
                    return f
                sc.t(mk(sps[:, kc * 128:kc * 128 + 128],
                        xht[:, kc * 128:kc * 128 + 128], ident_sb[:]))
            t_tp = st.inc()
            last_xh_read[par] = t_tp
            sc.wait("scalar", st, t_tp)
            dst = xhq_sb[:].rearrange("p (kc t) -> p kc t", kc=8)[
                :, :, i * 128:i * 128 + 128]
            srcv = sps.rearrange("p (kc t) -> p kc t", kc=8)
            s_e = sc.s(lambda e, o=dst, i_=srcv: e.copy(out=o, in_=i_))
            s_evac_t[i] = s_e
            last_score_evac[par] = s_e
            if i % 4 == 3:
                n = i // 4
                sc.wait("vector", ss, s_e)
                v_gb = 0
                for kc in range(8):
                    slc = xhq_sb[:, kc * T_FULL + n * 512:
                                 kc * T_FULL + n * 512 + 512]
                    v_gb = sc.v(lambda e, o=slc, i_=slc,
                                ga=gamT_sb[:, kc:kc + 1],
                                be=betT_sb[:, kc:kc + 1]:
                                e.tensor_scalar(o, i_, ga, be,
                                                op0=AluOpType.mult,
                                                op1=AluOpType.add))
                v_gb_slice[n] = v_gb
            qkv_burst()
    while pi[0] < len(pend) or pending_evac:
        made = qkv_evac()
        made = qkv_burst() or made
        if not made:
            break
    L["v_gb"] = v_gb_slice[7]
    state["lqe_s"] = last_qkv_evac[0]
    state["last_qkv_evac"] = 0
    state["last_tp_evac"] = 0
    L["ln_tp_evac"] = 0


def pipeline(sc, L):
    st, sv, ss = L["st"], L["sv"], L["ss"]
    state = L["plstate"]
    sc.wait("vector", ss, state["lqe_s"])
    sc.wait("tensor", ss, state["lqe_s"])

    def full_gen(sg):
        yield from qkv_steps_gen(sc, L, sg, state)
        yield from bound_gen(sc, L, sg, state)

    ident_sb, tp_ps, v_sb = L["ident_sb"], L["tp_ps"], L["v_sb"]
    vstage = L["vstage_sb"]

    def vstep_gen():
        for n in range(8):
            sc.wait("tensor", ss, state["vstage_ev"][n])
            sc.wait("tensor", ss, state["last_tp_evac"])
            sc.wait("tensor", sv, state["last_tp_evac_v"])
            for blk in range(4):
                last = blk == 3
                def mk(o, i_, idn, last=last):
                    def f(e):
                        ins = e.transpose(o, i_, idn)
                        if last:
                            ins.then_inc(st.h, 1)
                        return ins
                    return f
                sc.t(mk(tp_ps[:, blk * 128:blk * 128 + 128],
                        vstage[:, n * 512 + blk * 128:
                               n * 512 + blk * 128 + 128], ident_sb[:]))
            t_vt = st.inc()
            sc.wait("vector", st, t_vt)
            state["last_tp_evac_v"] = sc.v(
                lambda e, o=v_sb[:, (n * 4) * 128:(n * 4 + 4) * 128],
                i_=tp_ps[:, 0:512]: e.tensor_copy(o, i_))
            yield n

    vsteps = vstep_gen()
    for _ in bound_gen(sc, L, 0, state):
        next(vsteps, None)
    for _ in vsteps:
        pass
    state["pvz_free"] = state["last_tp_evac_v"]
    for sg in range(8):
        gen = full_gen(sg + 1) if sg + 1 < 8 else None
        p2_subgroup(sc, L, sg, state, gen)
        if gen is not None:
            for _ in gen:
                pass
        state["kv_parity_free"][sg % 2] = st.n


def qkv_steps_gen(sc, L, sg, state):
    dwp = L["dwp"]
    st, sv, ss = L["st"], L["sv"], L["ss"]
    xhq_sb, wq_sb, vtmp_sb = L["xhq_sb"], L["wq_sb"], L["vtmp_sb"]
    kT_sb, v_sb, qT_sb = L["kT_sb"], L["v_sb"], L["qT_sb"]
    qkv_ps, tp_ps = L["qkv_ps"], L["tp_ps"]
    wqkv = L["wqkv"]
    ident_sb, bqkvT_sb = L["ident_sb"], L["bqkvT_sb"]
    par = sg % 2
    kbase = par * T_FULL
    qbase = par * T_OWN

    k_tiles = [8 + sg]
    v_tiles = [16 + sg]
    q_tiles = [sg]
    mlist = k_tiles + v_tiles + q_tiles
    wq_ready = {}

    def issue_wload(j):
        m = mlist[j]
        wbuf = wq_sb[:, (j % 2) * 1024:(j % 2) * 1024 + 1024]
        sc.wait("gpsimd", st, state["last_wq_consumed"][j % 2])
        srcw = wqkv[:, 128 * m:128 * m + 128].rearrange(
            "(kc p) c -> p kc c", p=128)
        dws = dwp[j % 2]
        sc.g(lambda e, d=wbuf.rearrange("p (kc c) -> p kc c", kc=8), s=srcw,
             hh=dws.h: e.dma_start(out=d, in_=s).then_inc(hh, 16))
        wq_ready[j] = (dws, dws.inc(16))

    issue_wload(0)
    issue_wload(1)
    first = True
    for j, m in enumerate(mlist):
        wbuf = wq_sb[:, (j % 2) * 1024:(j % 2) * 1024 + 1024]
        nrange = 4 if m in q_tiles else 8
        for n in range(nrange):
            if m in k_tiles:
                dst = kT_sb[:, kbase + n * 512:kbase + n * 512 + 512]
            elif m in v_tiles:
                dst = vtmp_sb[:, (n % 2) * 512:(n % 2) * 512 + 512]
            else:
                dst = qT_sb[:, qbase + n * 512:qbase + n * 512 + 512]
            sc.wait("tensor", wq_ready[j][0], wq_ready[j][1])
            if first:
                sc.wait("tensor", sv, L["v_gb"])
                if sg >= 2:
                    sc.wait("vector", st, state["kv_parity_free"][par])
                first = False
            sc.wait("tensor", sv, state["last_qkv_evac"])
            for kc in range(8):
                if kc == 4:
                    yield (m, n, "mid")
                last = kc == 7
                def mk(o, w, r, s0, s1, last=last):
                    def f(e):
                        ins = e.matmul(o, w, r, start=s0, stop=s1)
                        if last:
                            ins.then_inc(st.h, 1)
                        return ins
                    return f
                sc.t(mk(qkv_ps[:], wbuf[:, kc * 128:kc * 128 + 128],
                        xhq_sb[:, kc * T_FULL + n * 512:
                               kc * T_FULL + n * 512 + 512],
                        kc == 0, kc == 7))
            t_mm = st.inc()
            yield (m, n, 0)
            sc.wait("vector", st, t_mm)
            if m in v_tiles:
                sc.wait("vector", st, state["last_vtmp_read"][n % 2])
            v_ev = sc.v(lambda e, o=dst, i_=qkv_ps[:], b=bqkvT_sb[:, m:m + 1]:
                        e.tensor_scalar(o, i_, b, None, op0=AluOpType.add,
                                        op1=AluOpType.bypass))
            state["last_qkv_evac"] = v_ev
            if n == nrange - 1:
                state["last_wq_consumed"][j % 2] = st.n
                if j + 2 < len(mlist):
                    issue_wload(j + 2)
            if m in v_tiles:
                sc.wait("tensor", sv, v_ev)
                sc.wait("tensor", ss, state["last_tp_evac"])
                sc.wait("tensor", sv, state["last_tp_evac_v"])
                for blk in range(4):
                    last = blk == 3
                    def mk(o, i_, idn, last=last):
                        def f(e):
                            ins = e.transpose(o, i_, idn)
                            if last:
                                ins.then_inc(st.h, 1)
                            return ins
                        return f
                    sc.t(mk(tp_ps[:, blk * 128:blk * 128 + 128],
                            dst[:, blk * 128:blk * 128 + 128],
                            ident_sb[:]))
                t_vt = st.inc()
                state["last_vtmp_read"][n % 2] = t_vt
                vdst = v_sb[:, kbase + (n * 4) * 128:
                            kbase + (n * 4 + 4) * 128]
                sc.wait("vector", st, t_vt)
                state["last_tp_evac_v"] = sc.v(
                    lambda e, o=vdst, i_=tp_ps[:, 0:512]:
                    e.tensor_copy(o, i_))
            yield (m, n, 4)


def bound_gen(sc, L, sg, state):
    st, sv, ss = L["st"], L["sv"], L["ss"]
    kT_sb, qT_sb, sq_sb, nmax_sb, negB_sb = (
        L["kT_sb"], L["qT_sb"], L["sq_sb"], L["nmax_sb"], L["negB_sb"])
    onesb_sb = L["onesb_sb"]
    qkv_ps = L["qkv_ps"]
    mln8_ap = L["consts_sb"][:, 1:2]
    par = sg % 2
    kbase = par * T_FULL
    qbase = par * T_OWN
    last_sq_read = state.setdefault("bound_sq_read", [0])

    v_nb = 0
    for pl in range(1):
        p = sg
        srcs = [(qT_sb[:, qbase:qbase + T_OWN], 0),
                (kT_sb[:, kbase:kbase + 2048], 1),
                (kT_sb[:, kbase + 2048:kbase + 4096], 2)]
        for (srcq, ci) in srcs:
            sc.wait("vector", st, last_sq_read[0])
            v_sq = sc.v(lambda e, o=sq_sb[:], a=srcq:
                        e.tensor_tensor(o, a, a, op=AluOpType.mult))
            sc.wait("tensor", sv, v_sq)
            sc.wait("tensor", sv, state["last_qkv_evac"])
            acc = nmax_sb[:, ci + 4:ci + 5]
            tgt = nmax_sb[:, ci:ci + 1]
            for n4 in range(4):
                def mk(o, w, r):
                    def f(e):
                        ins = e.matmul(o, w, r, start=True, stop=True)
                        ins.then_inc(st.h, 1)
                        return ins
                    return f
                sc.t(mk(qkv_ps[:], onesb_sb[:],
                        sq_sb[:, n4 * 512:n4 * 512 + 512]))
                t_nm = st.inc()
                last_sq_read[0] = t_nm
                sc.wait("vector", st, t_nm)
                if n4 == 0:
                    v_r = sc.v(lambda e, o=tgt, i_=qkv_ps[:]:
                               e.tensor_reduce(o, i_, axis=AX.X,
                                               op=AluOpType.max))
                else:
                    sc.v(lambda e, o=acc, i_=qkv_ps[:]:
                         e.tensor_reduce(o, i_, axis=AX.X, op=AluOpType.max))
                    v_r = sc.v(lambda e, o=tgt, a=tgt, b=acc:
                               e.tensor_tensor(o, a, b, op=AluOpType.max))
                sc.wait("tensor", sv, v_r)
                yield ("bound", p, ci, n4)
        v_k = sc.v(lambda e, o=nmax_sb[:, 3:4], a=nmax_sb[:, 1:2],
                   b=nmax_sb[:, 2:3]:
                   e.tensor_tensor(o, a, b, op=AluOpType.max))
        sc.wait("scalar", sv, v_k)
        sc.s(lambda e, o=nmax_sb[:, 4:5], i_=nmax_sb[:, 0:1]:
             e.activation(out=o, in_=i_, func=AF.Ln, bias=0.0))
        s_ln = sc.s(lambda e, o=nmax_sb[:, 5:6], i_=nmax_sb[:, 3:4]:
                    e.activation(out=o, in_=i_, func=AF.Ln, bias=0.0))
        sc.wait("vector", ss, s_ln)
        v_add = sc.v(lambda e, o=nmax_sb[:, 6:7], a=nmax_sb[:, 4:5],
                     b=nmax_sb[:, 5:6]:
                     e.tensor_tensor(o, a, b, op=AluOpType.add))
        sc.wait("scalar", sv, v_add)
        s_B = sc.s(lambda e, o=nmax_sb[:, 7:8], i_=nmax_sb[:, 6:7], b=mln8_ap:
                   e.activation(out=o, in_=i_, func=AF.Exp, scale=0.5,
                                bias=b))
        sc.wait("vector", ss, s_B)
        v_nb = sc.v(lambda e, o=negB_sb[:, p:p + 1], i_=nmax_sb[:, 7:8]:
                    e.tensor_scalar(o, i_, -1.0, -1.0, op0=AluOpType.mult,
                                    op1=AluOpType.add))
        yield ("negB", p, 0)
    state["v_negB"][sg] = v_nb


def p2_subgroup(sc, L, sg, state, gen):
    st, sv, ss = L["st"], L["sv"], L["ss"]
    kT_sb, v_sb, qT_sb, PT_sb, zb_sb, attnT_sb = (
        L["kT_sb"], L["v_sb"], L["qT_sb"], L["PT_sb"], L["zb_sb"],
        L["attnT_sb"])
    onesb_sb, negB_sb = L["onesb_sb"], L["negB_sb"]
    score_ps, pv_ps, z_ps = L["score_ps"], L["pv_ps"], L["z_ps"]
    par = sg % 2
    kbase = par * T_FULL
    qbase = par * T_OWN
    it_hist = state["it_hist"]
    p = sg

    def emit_scores(qt, kt, guard):
        spar = kt % 2
        sps = score_ps[:, spar * 1024:spar * 1024 + 1024]
        kA = kT_sb[0:64, kbase + kt * 128:kbase + kt * 128 + 128]
        kB = kT_sb[64:128, kbase + kt * 128:kbase + kt * 128 + 128]
        qA = qT_sb[0:64, qbase + qt * 512:qbase + qt * 512 + 512]
        qB = qT_sb[64:128, qbase + qt * 512:qbase + qt * 512 + 512]
        if guard is not None:
            sc.wait("tensor", ss, guard)
        sc.t(lambda e, o=sps[:, 0:512], w=kA, r=qA:
             e.matmul(o, w, r, start=True, stop=True))
        sc.t(lambda e, o=sps[:, 512:1024], w=kB, r=qB:
             e.matmul(o, w, r, start=True, stop=True).then_inc(st.h, 1))
        return st.inc()

    def emit_pvz(qt, kt, it, exp_done_ref=None):
        spar = kt % 4
        pt = PT_sb[:, spar * 1024:spar * 1024 + 1024]
        vch = v_sb[:, kbase + kt * 128:kbase + kt * 128 + 128]
        sc.wait("tensor", ss, exp_done_ref[kt])
        if kt == 0 and it >= 1:
            sc.wait("tensor", sv, it_hist[it - 1]["norm_v"])
        elif kt == 0:
            sc.wait("tensor", sv, state["pvz_free"])
        s0, s1 = (kt == 0), (kt == NKT - 1)
        sc.t(lambda e, o=pv_ps[0:64, :], w=vch[:, 0:64], r=pt[:, 0:512],
             a=s0, b=s1: e.matmul(o, w, r, start=a, stop=b,
                                  skip_group_check=True))
        sc.t(lambda e, o=pv_ps[64:128, :], w=vch[:, 64:128],
             r=pt[:, 512:1024], a=s0, b=s1:
             e.matmul(o, w, r, start=a, stop=b, skip_group_check=True))
        sc.t(lambda e, o=z_ps[0:64, :], w=onesb_sb[:, 0:64],
             r=pt[:, 0:512], a=s0, b=s1:
             e.matmul(o, w, r, start=a, stop=b, skip_group_check=True))
        sc.t(lambda e, o=z_ps[64:128, :], w=onesb_sb[:, 0:64],
             r=pt[:, 512:1024], a=s0, b=s1:
             e.matmul(o, w, r, start=a, stop=b,
                      skip_group_check=True).then_inc(st.h, 1))
        return st.inc()

    def finish_qt(h, pqt, hit):
        sc.wait("tensor", ss, h["exp_done"][NKT - 1])
        for j in (NKT - 2, NKT - 1):
            h["pvz_done"][j] = emit_pvz(pqt, j, hit, h["exp_done"])
        if gen is not None:
            next(gen, None)
            next(gen, None)
        zbh = zb_sb[:]
        sc.wait("vector", st, h["pvz_done"][NKT - 1])
        sc.v(lambda e, o=zbh, i_=z_ps[:]: e.reciprocal(o, i_))
        dst = attnT_sb[:, p * T_OWN + pqt * 512:p * T_OWN + pqt * 512 + 512]
        h["norm_v"] = sc.v(lambda e, o=dst, a=pv_ps[:], b=zbh:
                           e.tensor_tensor(o, a, b, op=AluOpType.mult))

    first_of_sg = True
    for qt in range(NQT):
        it = len(it_hist)
        exp_done = []
        sc_done = []
        pvz_done = {}
        gprev = it_hist[it - 1]["exp_done"][NKT - 2] if it >= 1 else None
        sc_done.append(emit_scores(qt, 0, gprev))
        if first_of_sg:
            sc.wait("scalar", sv, state["v_negB"][sg])
            first_of_sg = False
        if qt >= 1:
            finish_qt(it_hist[it - 1], qt - 1, it - 1)
        it_hist.append(dict(exp_done=exp_done, pvz_done=pvz_done, norm_v=0))
        for kt in range(NKT):
            if kt + 1 < NKT:
                if kt + 1 >= 2:
                    gg = exp_done[kt - 1]
                elif it >= 1:
                    gg = it_hist[it - 1]["exp_done"][NKT - 1]
                else:
                    gg = None
                sc_done.append(emit_scores(qt, kt + 1, gg))
            else:
                sc.wait("tensor", ss, exp_done[kt - 1])
            if kt % 2 == 1 and kt >= 3:
                pvz_done[kt - 3] = emit_pvz(qt, kt - 3, it, exp_done)
                pvz_done[kt - 2] = emit_pvz(qt, kt - 2, it, exp_done)
            spar = kt % 4
            pt = PT_sb[:, spar * 1024:spar * 1024 + 1024]
            g1 = sc_done[kt]
            if kt >= 4:
                g1 = max(g1, pvz_done[kt - 4])
            sc.wait("scalar", st, g1)
            if kt < 4 and it >= 1:
                sc.wait("scalar", st,
                        it_hist[it - 1]["pvz_done"][NKT - 4 + kt])
            s_exp = sc.s(lambda e, o=pt,
                         i_=score_ps[:, (kt % 2) * 1024:(kt % 2) * 1024 + 1024],
                         b=negB_sb[:, p:p + 1]:
                         e.activation(out=o, in_=i_, func=AF.Exp, bias=b,
                                      scale=0.125))
            exp_done.append(s_exp)
            if gen is not None and kt % 2 == 0:
                next(gen, None)
    finish_qt(it_hist[-1], NQT - 1, len(it_hist) - 1)


def p34(sc, L):
    dxp, dwo, dop = L["dxp"], L["dwo"], L["dop"]
    st, sv, ss = L["st"], L["sv"], L["ss"]
    attnT_sb, wout_sb = L["attnT_sb"], L["wout_sb"]
    x2_sb, x2bf_sb, x2T_sb, xres_sb = (L["x2_sb"], L["x2bf_sb"], L["x2T_sb"],
                                       L["xres_sb"])
    boutbc_sb, bffobc_sb = L["boutbc_sb"], L["bffobc_sb"]
    sT_sb, hT_sb, gT_sb, u_sb = L["sT_sb"], L["hT_sb"], L["gT_sb"], L["u_sb"]
    stats2_sb, mv2_sb, rstd2_sb = L["stats2_sb"], L["mv2_sb"], L["rstd2_sb"]
    A_ps, B_ps, C_ps, F_ps = L["A_ps"], L["B_ps"], L["C_ps"], L["F_ps"]
    onesf_sb, boutrow_sb, bfforow_sb = (L["onesf_sb"], L["boutrow_sb"],
                                        L["bfforow_sb"])
    win_sb, rw1_sb, w2b_sb, wfo_sb = (L["win_sb"], L["rw1_sb"], L["w2b_sb"],
                                      L["wfo_sb"])
    bin_sb, b1T_sb, b2_sb = L["bin_sb"], L["b1T_sb"], L["b2_sb"]
    ident_sb = L["ident_sb"]
    x_hbm, out_hbm = L["x_hbm"], L["out_hbm"]
    eps_ap = L["consts_sb"][:, 0:1]

    for dstw, srcw in (
        (wout_sb[:].rearrange("p (kc m) -> p kc m", kc=8),
         L["wout"].rearrange("(kc p) m -> p kc m", p=128)),
        (win_sb[:].rearrange("p (kc m) -> p kc m", kc=8),
         L["win"].rearrange("(kc p) m -> p kc m", p=128)),
        (rw1_sb[:], L["rw1"]),
        (w2b_sb[:].rearrange("p (kc m) -> p kc m", kc=8),
         L["w2b"].rearrange("(kc p) m -> p kc m", p=128)),
        (wfo_sb[:], L["wfo"]),
        (boutrow_sb[:], L["bout"][None, :]),
        (bfforow_sb[:], L["bffo"][None, :]),
    ):
        sc.g(lambda e, d=dstw, s=srcw, hh=dwo.h:
             e.dma_start(out=d, in_=s).then_inc(hh, 16))
        dwo.inc(16)
    w_all = dwo.n
    sc.wait("tensor", dwo, w_all)
    sc.wait("scalar", dwo, w_all)
    sc.wait("vector", dwo, w_all)

    v_bcp = 0
    for (bcdst, brow) in ((boutbc_sb, boutrow_sb), (bffobc_sb, bfforow_sb)):
        sc.wait("tensor", sv, v_bcp)
        for nh in range(2):
            last = nh == 1
            def mk(o, w, r, last=last):
                def f(e):
                    ins = e.matmul(o, w, r, start=True, stop=True)
                    if last:
                        ins.then_inc(st.h, 1)
                    return ins
                return f
            sc.t(mk(A_ps[:, nh * 512:nh * 512 + 512], onesf_sb[:],
                    brow[:, nh * 512:nh * 512 + 512]))
        t_bc = st.inc()
        sc.wait("vector", st, t_bc)
        v_bcp = sc.v(lambda e, o=bcdst[:], i_=A_ps[:, 0:1024]:
                     e.tensor_copy(o, i_))
    sc.wait("tensor", sv, v_bcp)

    def batched_ln(uslot0, outs, outs2, slot):
        stt = stats2_sb[:, slot * 48:slot * 48 + 48]
        mv = mv2_sb[:, slot * 8:slot * 8 + 8]
        rst = rstd2_sb[:, slot * 4:slot * 4 + 4]
        for t in range(4):
            u_ap = u_sb[:, (uslot0 + t) * C:(uslot0 + t + 1) * C]
            sc.v(lambda e, o=stt[:, t * 12:t * 12 + 6], i_=u_ap[:, 0:512]:
                 e.bn_stats(o, i_))
            sc.v(lambda e, o=stt[:, t * 12 + 6:t * 12 + 12],
                 i_=u_ap[:, 512:1024]: e.bn_stats(o, i_))
        v_st = 0
        for t in range(4):
            v_st = sc.v(lambda e, o=mv[:, t * 2:t * 2 + 2],
                        i_=stt[:, t * 12:t * 12 + 12]: e.bn_aggr(o, i_))
        sc.wait("scalar", sv, v_st)
        var_ap = mv[:].rearrange("p (t two) -> p two t", two=2)[:, 1, :]
        sc.s(lambda e, o=rst, i_=var_ap, b=eps_ap:
             e.activation(out=o, in_=i_, func=AF.Ln, bias=b))
        s_r = sc.s(lambda e, o=rst, i_=rst:
                   e.activation(out=o, in_=i_, func=AF.Exp, scale=-0.5))
        sc.wait("vector", ss, s_r)
        rets = []
        for t in range(4):
            u_ap = u_sb[:, (uslot0 + t) * C:(uslot0 + t + 1) * C]
            v_ap = sc.v(lambda e, o=outs[t], i_=u_ap, m=mv[:, t * 2:t * 2 + 1],
                        r=rst[:, t:t + 1]:
                        e.tensor_scalar(o, i_, m, r, op0=AluOpType.subtract,
                                        op1=AluOpType.mult))
            if outs2[t] is not None:
                sc.wait("scalar", sv, v_ap)
                s_c = sc.s(lambda e, o=outs2[t], i_=outs[t]:
                           e.copy(out=o, in_=i_))
                rets.append(("s", s_c))
            else:
                rets.append(("v", v_ap))
        return rets

    a_guard_v = [0, 0]
    b_guard = [0, 0]
    o_stored = [0, 0, 0, 0]
    last_xrb = [0, 0, 0, 0]
    t_sT_read = [0, 0]
    t_h_read = [0, 0]
    t_g_read = [0, 0]
    s_binev = [0, 0]
    s_gev = [0, 0]
    f_guard_v = [0]
    f_guard_s = [0]

    x2T_done = [0, 0]
    x2cast_st = [[0] * 4, [0] * 4]

    def p3a_stage(sl):
            spar = sl % 2
            t_ops = []
            for tt in range(4):
                half = tt % 2
                row0 = sl * 512 + tt * 128
                ap = A_ps[:, half * 1024:half * 1024 + 1024]
                xr = xres_sb[:, tt * C:(tt + 1) * C]
                sc.wait("sync", sv, last_xrb[tt])
                dxs = dxp[tt]
                sc.y(lambda e, d=xr, s=x_hbm[row0:row0 + 128, :], hh=dxs.h:
                     e.dma_start(out=d, in_=s).then_inc(hh, 16))
                v_dx = (dxs, dxs.inc(16))
                xrb = xr
                sc.wait("vector", v_dx[0], v_dx[1])
                v_xrb = sc.v(lambda e, o=xrb, i0=xr, b=boutbc_sb[:]:
                             e.scalar_tensor_tensor(o, i0, 2.0, b,
                                                    op0=AluOpType.mult,
                                                    op1=AluOpType.add))
                last_xrb[tt] = v_xrb
                sc.wait("tensor", sv, a_guard_v[half])
                for nh in range(2):
                    for kc in range(8):
                        last = (nh == 1 and kc == 7)
                        def mk(o, w, r, s0, s1, last=last):
                            def f(e):
                                ins = e.matmul(o, w, r, start=s0, stop=s1)
                                if last:
                                    ins.then_inc(st.h, 1)
                                return ins
                            return f
                        sc.t(mk(ap[:, nh * 512:nh * 512 + 512],
                                attnT_sb[:, kc * T_OWN + row0:
                                         kc * T_OWN + row0 + 128],
                                wout_sb[:, kc * 1024 + nh * 512:
                                        kc * 1024 + nh * 512 + 512],
                                kc == 0, kc == 7))
                t_op = st.inc()
                u = u_sb[:, tt * C:(tt + 1) * C]
                sc.wait("vector", st, t_op)
                a_guard_v[half] = sc.v(lambda e, o=u, a=ap, b=xrb:
                                       e.tensor_tensor(o, a, b,
                                                       op=AluOpType.add))
                last_xrb[tt] = a_guard_v[half]

    def p3bc_stage(sl):
            spar = sl % 2
            outs = [x2_sb[:, spar * 4 * C + t * C:spar * 4 * C + (t + 1) * C]
                    for t in range(4)]
            outs2 = [x2bf_sb[:, t * C:(t + 1) * C] for t in range(4)]
            ln_rets = batched_ln(0, outs, outs2, 0)
            x2cast_st[spar] = [c for (k, c) in ln_rets]
            for tt in range(4):
                tph = B_ps[:]
                kind, cnt = ln_rets[tt]
                sc.wait("tensor", ss if kind == "s" else sv, cnt)
                sc.wait("tensor", ss, b_guard[0])
                for kc in range(8):
                    last = kc == 7
                    def mk(o, i_, idn, last=last):
                        def f(e):
                            ins = e.transpose(o, i_, idn)
                            if last:
                                ins.then_inc(st.h, 1)
                            return ins
                        return f
                    sc.t(mk(tph[:, kc * 128:kc * 128 + 128],
                            x2bf_sb[:, tt * C + kc * 128:tt * C + kc * 128 + 128],
                            ident_sb[:]))
                t_tp = st.inc()
                sc.wait("scalar", st, t_tp)
                sc.wait("scalar", st, t_sT_read[spar])
                dst = x2T_sb[:].rearrange("p (sp kc t) -> p sp kc t",
                                          sp=2, kc=8)[
                    :, spar, :, tt * 128:tt * 128 + 128]
                srcv = tph.rearrange("p (kc t) -> p kc t", kc=8)
                b_guard[0] = sc.s(lambda e, o=dst, i_=srcv:
                                  e.copy(out=o, in_=i_))
            x2T_done[sl % 2] = max(b_guard)


    def p4ab_stage(sl):
            spar = sl % 2
            x2Tsl = x2T_sb[:, spar * 4096:spar * 4096 + 4096]
            sTsl = sT_sb[:, spar * 512:spar * 512 + 512]
            hTsl = hT_sb[:, spar * 4096:spar * 4096 + 4096]
            gTsl = gT_sb[:, spar * 512:spar * 512 + 512]
            sc.wait("tensor", ss, x2T_done[spar])
            sc.wait("tensor", ss, max(s_gev))
            sc.wait("tensor", ss, s_binev[spar])
            for kc in range(8):
                last = kc == 7
                def mk(o, w, r, s0, s1, last=last):
                    def f(e):
                        ins = e.matmul(o, w, r, start=s0, stop=s1)
                        if last:
                            ins.then_inc(st.h, 1)
                        return ins
                    return f
                sc.t(mk(C_ps[0:64, :], win_sb[:, kc * 64:kc * 64 + 64],
                        x2Tsl[:, kc * 512:kc * 512 + 512], kc == 0, kc == 7))
            t_sT = st.inc()
            t_sT_read[spar] = t_sT
            sc.wait("scalar", st, t_sT)
            sc.wait("scalar", st, t_h_read[spar])
            s_sT = sc.s(lambda e, o=sTsl, i_=C_ps[0:64, :], b=bin_sb[:]:
                        e.activation(out=o, in_=i_, func=AF.Identity, bias=b))
            s_binev[spar] = s_sT
            sc.wait("tensor", ss, s_sT)
            h_evac = [0, 0]
            for j in range(8):
                half = j % 2
                hp = F_ps[:, half * 512:half * 512 + 512]
                sc.wait("tensor", ss, h_evac[half])
                if j < 2:
                    sc.wait("tensor", sv, f_guard_v[0])
                    sc.wait("tensor", ss, f_guard_s[0])
                sc.t(lambda e, o=hp, w=rw1_sb[:, j * 128:j * 128 + 128],
                     r=sTsl:
                     e.matmul(o, w, r, start=True, stop=True,
                              skip_group_check=True).then_inc(st.h, 1))
                t_h = st.inc()
                t_h_read[spar] = t_h
                sc.wait("scalar", st, t_h)
                if j < 2:
                    sc.wait("scalar", st, t_g_read[spar])
                hev = sc.s(lambda e, o=hTsl[:, j * 512:j * 512 + 512], i_=hp,
                           b=b1T_sb[:, j:j + 1]:
                           e.activation(out=o, in_=i_, func=AF.Relu, bias=b))
                h_evac[half] = hev
                f_guard_s[0] = hev
            sc.wait("tensor", ss, h_evac[0])
            sc.wait("tensor", ss, h_evac[1])
            for j in range(8):
                last = j == 7
                def mk(o, w, r, s0, s1, last=last):
                    def f(e):
                        ins = e.matmul(o, w, r, start=s0, stop=s1)
                        if last:
                            ins.then_inc(st.h, 1)
                        return ins
                    return f
                sc.t(mk(C_ps[0:64, :], w2b_sb[:, j * 64:j * 64 + 64],
                        hTsl[:, j * 512:j * 512 + 512], j == 0, j == 7))
            t_g = st.inc()
            sc.wait("scalar", st, t_g)
            s_g = sc.s(lambda e, o=gTsl, i_=C_ps[0:64, :], b=b2_sb[:]:
                       e.activation(out=o, in_=i_, func=AF.Identity, bias=b))
            s_gev[spar] = s_g
            sc.wait("tensor", ss, s_g)
            for tt in range(4):
                row0 = sl * 512 + tt * 128
                fpA = F_ps[:]
                x2t = x2_sb[:, spar * 4 * C + tt * C:spar * 4 * C + (tt + 1) * C]
                x2pb = x2t
                sc.wait("vector", ss, x2cast_st[spar][tt])
                sc.v(lambda e, o=x2pb, a=x2t, b=bffobc_sb[:]:
                     e.tensor_tensor(o, a, b, op=AluOpType.add))
                sc.wait("tensor", sv, f_guard_v[0])
                sc.wait("tensor", ss, f_guard_s[0])
                for nh in range(2):
                    last = nh == 1
                    def mk(o, w, r, last=last):
                        def f(e):
                            ins = e.matmul(o, w, r, start=True, stop=True)
                            if last:
                                ins.then_inc(st.h, 1)
                            return ins
                        return f
                    sc.t(mk(fpA[:, nh * 512:nh * 512 + 512],
                            gTsl[:, tt * 128:tt * 128 + 128],
                            wfo_sb[:, nh * 512:nh * 512 + 512]))
                t_f = st.inc()
                t_g_read[spar] = t_f
                u = u_sb[:, (4 + tt) * C:(5 + tt) * C]
                sc.wait("vector", st, t_f)
                if o_stored[tt]:
                    sc.wait("vector", dop[tt % 2], dop[tt % 2].n)
                f_guard_v[0] = sc.v(lambda e, o=u, a=fpA, b=x2pb:
                                    e.tensor_tensor(o, a, b, op=AluOpType.add))

    def p4c_stage(sl):
            spar = sl % 2
            outs = [u_sb[:, (4 + t) * C:(5 + t) * C] for t in range(4)]
            v_out = batched_ln(4, outs, [None] * 4, 1)
            for tt in range(4):
                row0 = sl * 512 + tt * 128
                u = u_sb[:, (4 + tt) * C:(5 + tt) * C]
                sc.wait("sync", sv, v_out[tt][1])
                dos = dop[tt % 2]
                sc.y(lambda e, d=out_hbm[row0:row0 + 128, :], s=u, hh=dos.h:
                     e.dma_start(out=d, in_=s).then_inc(hh, 16))
                o_stored[tt] = dos.inc(16)

    for sl in range(5):
        if sl < 4:
            p3a_stage(sl)
        if sl >= 1:
            p4ab_stage(sl - 1)
        if sl < 4:
            p3bc_stage(sl)
        if sl >= 1:
            p4c_stage(sl - 1)

_NC_CACHE = {}


def kernel(**inputs):
    x = np.ascontiguousarray(np.asarray(inputs["L0_attn_x"], dtype=np.float32))
    gam = np.asarray(inputs["L0_attn_ln_gamma"], dtype=np.float32)
    bet = np.asarray(inputs["L0_attn_ln_beta"], dtype=np.float32)
    wqkv = np.ascontiguousarray(np.asarray(inputs["L0_attn_w_qkv"], dtype=np.float32))
    bqkv = np.asarray(inputs["L0_attn_b_qkv"], dtype=np.float32)
    wout = np.ascontiguousarray(np.asarray(inputs["L0_attn_w_out"], dtype=np.float32))
    bout = np.asarray(inputs["L0_attn_b_out"], dtype=np.float32)
    win = np.ascontiguousarray(np.asarray(inputs["L0_ffn_Win"], dtype=np.float32))
    binf = np.asarray(inputs["L0_ffn_bin_"], dtype=np.float32)
    w1 = np.asarray(inputs["L0_ffn_W1"], dtype=np.float32)
    b1 = np.asarray(inputs["L0_ffn_b1"], dtype=np.float32)
    w2 = np.asarray(inputs["L0_ffn_W2"], dtype=np.float32)
    b2 = np.asarray(inputs["L0_ffn_b2"], dtype=np.float32)
    wfo = np.ascontiguousarray(np.asarray(inputs["L0_ffn_Wout"], dtype=np.float32))
    bffo = np.asarray(inputs["L0_ffn_bout"], dtype=np.float32)

    M, INNER = 64, 16
    rw1 = np.zeros((M, M * INNER), np.float32)
    for m in range(M):
        rw1[m, m * INNER:(m + 1) * INNER] = w1[m]
    w2b = np.zeros((M * INNER, M), np.float32)
    for m in range(M):
        w2b[m * INNER:(m + 1) * INNER, m] = w2[m]
    b1flat = np.ascontiguousarray(b1.reshape(M * INNER))

    ident = np.eye(128, dtype=np.float32).astype(ml_dtypes.bfloat16)
    identf = np.eye(128, dtype=np.float32)
    onesb = np.ones((128, 128), np.float32).astype(ml_dtypes.bfloat16)
    onesf = np.ones((1, 128), np.float32)
    consts = np.zeros((128, 2), np.float32)
    consts[:, 0] = EPS
    consts[:, 1] = MINUS_LN8

    shared = dict(wqkv=wqkv, bqkv=bqkv, gam=gam, bet=bet, wout=wout, bout=bout,
                  win=win, binf=binf, rw1=rw1, b1f=b1flat, w2b=w2b, b2=b2,
                  wfo=wfo, bffo=bffo, ident=ident, identf=identf, onesb=onesb, onesf=onesf,
                  consts=consts)

    in_maps = []
    for c in range(8):
        b, h = c // 2, c % 2
        own = x[b, h * T_OWN:(h + 1) * T_OWN]
        other = x[b, (1 - h) * T_OWN:(2 - h) * T_OWN]
        xp = np.ascontiguousarray(np.concatenate([own, other], axis=0))
        m = dict(shared)
        m["x"] = xp
        in_maps.append(m)

    if "nc" not in _NC_CACHE:
        _NC_CACHE["nc"] = build_nc()
    nc = _NC_CACHE["nc"]
    res = run_bass_kernel_spmd(nc, in_maps, core_ids=list(range(8)))

    out = np.empty((4, T_FULL, C), np.float32)
    for c in range(8):
        b, h = c // 2, c % 2
        out[b, h * T_OWN:(h + 1) * T_OWN] = res.results[c]["out"]
    return out


# version 28
# speedup vs baseline: 1.0552x; 1.0185x over previous
from contextlib import ExitStack

import numpy as np
import ml_dtypes

import concourse.bass as bass
import concourse.mybir as mybir
from concourse.alu_op_type import AluOpType
from concourse.bass_utils import run_bass_kernel_spmd

F32 = mybir.dt.float32
BF16 = mybir.dt.bfloat16
AF = mybir.ActivationFunctionType
AX = mybir.AxisListType

T_FULL = 4096
T_OWN = 2048
C = 1024
NKC = 8
NPAIR = 8
PAIRS_PER_G = 4
NGROUP = 2
NQT = 4
NKT = 32
EPS = 1e-5
MINUS_LN8 = -2.0794415416798357


class Sem:
    def __init__(self, handle):
        self.h = handle
        self.n = 0

    def inc(self, k=1):
        self.n += k
        return self.n


class Script:

    def __init__(self, sv, ss):
        self.items = []
        self._sv = sv
        self._ss = ss

    def t(self, fn):
        self.items.append(("tensor", fn, False))

    def v(self, fn):
        self.items.append(("vector", fn, True))
        return self._sv.inc()

    def s(self, fn):
        self.items.append(("scalar", fn, True))
        return self._ss.inc()

    def y(self, fn):
        self.items.append(("sync", fn, False))

    def g(self, fn):
        self.items.append(("gpsimd", fn, False))

    def wait(self, eng, sem, val):
        if val <= 0:
            return
        self.items.append(
            (eng, lambda e, s=sem.h, v=val: e.wait_ge(s, v), False))

    def run(self, engine_name, eng, chain_sem, chain_base):
        n = chain_base
        for name, fn, chained in self.items:
            if name != engine_name:
                continue
            if chained:
                if n > 0:
                    eng.wait_ge(chain_sem, n)
                ins = fn(eng)
                assert ins is not None, "chained entry must return instruction"
                ins.then_inc(chain_sem, 1)
                n += 1
            else:
                fn(eng)
        return n


def emit_block(nc, script, L):
    sv, ss = L["sv"], L["ss"]
    with nc.Block() as block:
        @block.tensor
        def _(t):
            script.run("tensor", t, None, 0)

        @block.vector
        def _(v):
            base = L["chain_done"]["vector"]
            L["chain_done"]["vector"] = script.run("vector", v, sv.h, base)

        @block.scalar
        def _(s):
            base = L["chain_done"]["scalar"]
            L["chain_done"]["scalar"] = script.run("scalar", s, ss.h, base)

        @block.sync
        def _(y):
            script.run("sync", y, None, 0)

        @block.gpsimd
        def _(g):
            script.run("gpsimd", g, None, 0)


def barrier(sc, L):
    st, sv, ss, sy, sg = L["st"], L["sv"], L["ss"], L["sy"], L["sg"]
    t_t = st.inc()
    sc.t(lambda e, s=st.h: e.drain().then_inc(s, 1))
    t_v = sc.v(lambda e: e.drain())
    t_s = sc.s(lambda e: e.drain())
    t_y = sy.inc()
    sc.y(lambda e, s=sy.h: e.drain().then_inc(s, 1))
    t_g = sg.inc()
    sc.g(lambda e, s=sg.h: e.drain().then_inc(s, 1))
    targets = {"tensor": (st, t_t), "vector": (sv, t_v), "scalar": (ss, t_s),
               "sync": (sy, t_y), "gpsimd": (sg, t_g)}
    for e in targets:
        for e2, (sem, val) in targets.items():
            if e2 != e:
                sc.wait(e, sem, val)


def build_nc():
    nc = bass.Bass("TRN2", target_bir_lowering=False, debug=False)

    L = {}
    L["x_hbm"] = nc.dram_tensor("x", [T_FULL, C], F32, kind="ExternalInput").ap()
    L["wqkv"] = nc.dram_tensor("wqkv", [C, 3 * C], F32, kind="ExternalInput").ap()
    L["bqkv"] = nc.dram_tensor("bqkv", [3 * C], F32, kind="ExternalInput").ap()
    L["gam"] = nc.dram_tensor("gam", [C], F32, kind="ExternalInput").ap()
    L["bet"] = nc.dram_tensor("bet", [C], F32, kind="ExternalInput").ap()
    L["wout"] = nc.dram_tensor("wout", [C, C], F32, kind="ExternalInput").ap()
    L["bout"] = nc.dram_tensor("bout", [C], F32, kind="ExternalInput").ap()
    L["win"] = nc.dram_tensor("win", [C, 64], F32, kind="ExternalInput").ap()
    L["binf"] = nc.dram_tensor("binf", [64], F32, kind="ExternalInput").ap()
    L["rw1"] = nc.dram_tensor("rw1", [64, C], F32, kind="ExternalInput").ap()
    L["b1f"] = nc.dram_tensor("b1f", [C], F32, kind="ExternalInput").ap()
    L["w2b"] = nc.dram_tensor("w2b", [C, 64], F32, kind="ExternalInput").ap()
    L["b2"] = nc.dram_tensor("b2", [64], F32, kind="ExternalInput").ap()
    L["wfo"] = nc.dram_tensor("wfo", [64, C], F32, kind="ExternalInput").ap()
    L["bffo"] = nc.dram_tensor("bffo", [C], F32, kind="ExternalInput").ap()
    L["ident_h"] = nc.dram_tensor("ident", [128, 128], BF16, kind="ExternalInput").ap()
    L["identf_h"] = nc.dram_tensor("identf", [128, 128], F32, kind="ExternalInput").ap()
    L["onesb_h"] = nc.dram_tensor("onesb", [128, 128], BF16, kind="ExternalInput").ap()
    L["onesf_h"] = nc.dram_tensor("onesf", [1, 128], F32, kind="ExternalInput").ap()
    L["consts_h"] = nc.dram_tensor("consts", [128, 2], F32, kind="ExternalInput").ap()
    L["out_hbm"] = nc.dram_tensor("out", [T_OWN, C], F32, kind="ExternalOutput").ap()

    es = ExitStack()
    with es:
        def SB(name, shape, dt):
            return es.enter_context(nc.sbuf_tensor(name, shape, dt))

        def SEM(name):
            return es.enter_context(nc.semaphore(name))

        L.update(
            ident_sb=SB("ident_sb", [128, 128], BF16),
            identf_sb=SB("identf_sb", [128, 128], F32),
            onesb_sb=SB("onesb_sb", [128, 128], BF16),
            onesf_sb=SB("onesf_sb", [1, 128], F32),
            consts_sb=SB("consts_sb", [128, 2], F32),
            bqkvT_sb=SB("bqkvT_sb", [128, 24], F32),
            gamT_sb=SB("gamT_sb", [128, 8], F32),
            betT_sb=SB("betT_sb", [128, 8], F32),
            bin_sb=SB("bin_sb", [64, 1], F32),
            b1T_sb=SB("b1T_sb", [128, 8], F32),
            b2_sb=SB("b2_sb", [64, 1], F32),
            negB_sb=SB("negB_sb", [128, NPAIR], F32),
            attnT_sb=SB("attnT_sb", [128, NPAIR * T_OWN], BF16),
        )
        dc, dg = Sem(SEM("s_dc")), Sem(SEM("s_dg2"))
        dxp = [Sem(SEM("s_dx0")), Sem(SEM("s_dx1")),
               Sem(SEM("s_dx2")), Sem(SEM("s_dx3"))]
        dwp = [Sem(SEM("s_dw0")), Sem(SEM("s_dw1"))]
        dwo = Sem(SEM("s_dwo"))
        dop = [Sem(SEM("s_do0")), Sem(SEM("s_do1"))]
        st, sv, ss = Sem(SEM("s_t")), Sem(SEM("s_v")), Sem(SEM("s_s"))
        sy, sg = Sem(SEM("s_y")), Sem(SEM("s_g"))
        L.update(dc=dc, dg=dg, dxp=dxp, dwp=dwp, dwo=dwo, dop=dop,
                 st=st, sv=sv, ss=ss, sy=sy, sg=sg,
                 chain_done={"vector": 0, "scalar": 0})

        with ExitStack() as es12:
            def SB12(name, shape, dt):
                return es12.enter_context(nc.sbuf_tensor(name, shape, dt))

            L.update(
                kT_sb=SB12("kT_sb", [128, 2 * T_FULL], BF16),
                v_sb=SB12("v_sb", [128, 2 * T_FULL], BF16),
                qT_sb=SB12("qT_sb", [128, 2 * T_OWN], BF16),
                xhq_sb=SB12("xhq_sb", [128, NKC * T_FULL], BF16),
                xtile_sb=SB12("xtile_sb", [128, 3 * C], F32),
                xh_tile=SB12("xh_tile", [128, 2 * C], BF16),
                wq_sb=SB12("wq_sb", [128, 2 * 1024], BF16),
                wq3_sb=SB12("wq3_sb", [128, 3 * 1024], BF16),
                vstage_sb=SB12("vstage_sb", [128, 8 * 512], BF16),
                vtmp_sb=SB12("vtmp_sb", [128, 2 * 512], BF16),
                stats_sb=SB12("stats_sb", [128, 12], F32),
                mv_sb=SB12("mv_sb", [128, 8], F32),
                rstd_sb=SB12("rstd_sb", [128, 4], F32),
                nmax_sb=SB12("nmax_sb", [128, 8], F32),
                PT_sb=SB12("PT_sb", [128, 8 * 1024], BF16),
                zb_sb=SB12("zb_sb", [128, 512], F32),
            )
            L["sq_sb"] = L["xh_tile"]
            with ExitStack() as esA:
                L["score_ps"] = esA.enter_context(
                    nc.psum_tensor("score_ps", [128, 2048], F32))
                L["pv_ps"] = esA.enter_context(
                    nc.psum_tensor("pv_ps", [128, 512], F32))
                L["z_ps"] = esA.enter_context(
                    nc.psum_tensor("z_ps", [128, 512], F32))
                L["qkv_ps"] = esA.enter_context(
                    nc.psum_tensor("qkv_ps", [128, 512], F32))
                L["tp_ps"] = esA.enter_context(
                    nc.psum_tensor("tp_ps", [128, 1024], BF16))
                sc = Script(sv, ss)
                p0_loads(sc, L)
                L["plstate"] = {
                    "lqe": 0,
                    "last_wq_consumed": [0, 0],
                    "last_tp_evac": 0,
                    "last_tp_evac_v": 0,
                    "last_qkv_evac": 0,
                    "last_vtmp_read": [0, 0],
                    "v_negB": [0] * 8,
                    "kv_parity_free": [0, 0],
                    "pvz_free": 0,
                    "vstage_ev": [0] * 8,
                    "it_hist": [],
                }
                phase_ln(sc, L)
                pipeline(sc, L)
                barrier(sc, L)
                emit_block(nc, sc, L)

        with ExitStack() as es34:
            def SB34(name, shape, dt):
                return es34.enter_context(nc.sbuf_tensor(name, shape, dt))

            L.update(
                wout_sb=SB34("wout_sb", [128, 8 * C], BF16),
                boutrow_sb=SB34("boutrow_sb", [1, C], F32),
                bfforow_sb=SB34("bfforow_sb", [1, C], F32),
                win_sb=SB34("win_sb", [128, 8 * 64], BF16),
                rw1_sb=SB34("rw1_sb", [64, C], BF16),
                w2b_sb=SB34("w2b_sb", [128, 8 * 64], BF16),
                wfo_sb=SB34("wfo_sb", [64, C], BF16),
                boutbc_sb=SB34("boutbc_sb", [128, C], F32),
                bffobc_sb=SB34("bffobc_sb", [128, C], F32),
                x2_sb=SB34("x2_sb", [128, 8 * C], F32),
                x2bf_sb=SB34("x2bf_sb", [128, 4 * C], BF16),
                x2T_sb=SB34("x2T_sb", [128, 2 * 4096], BF16),
                xres_sb=SB34("xres_sb", [128, 4 * C], F32),
                sT_sb=SB34("sT_sb", [64, 2 * 512], BF16),
                hT_sb=SB34("hT_sb", [128, 2 * 4096], BF16),
                gT_sb=SB34("gT_sb", [64, 2 * 512], BF16),
                u_sb=SB34("u_sb", [128, 8 * C], F32),
                stats2_sb=SB34("stats2_sb", [128, 96], F32),
                mv2_sb=SB34("mv2_sb", [128, 16], F32),
                rstd2_sb=SB34("rstd2_sb", [128, 8], F32),
            )
            L["A_ps"] = es34.enter_context(
                nc.psum_tensor("A_ps", [128, 2048], F32))
            L["F_ps"] = es34.enter_context(
                nc.psum_tensor("F_ps", [128, 1024], F32))
            L["B_ps"] = es34.enter_context(
                nc.psum_tensor("B_ps", [128, 1024], BF16))
            L["C_ps"] = es34.enter_context(
                nc.psum_tensor("C_ps", [64, 512], F32))
            sc = Script(sv, ss)
            p34(sc, L)
            sc.wait("sync", dop[0], dop[0].n)
            sc.wait("sync", dop[1], dop[1].n)
            emit_block(nc, sc, L)

    return nc



def p0_loads(sc, L):
    dc, dg = L["dc"], L["dg"]

    def ld(dst, src, slow=False):
        sc.y(lambda e, d=dst, s=src, sl=slow: e.dma_start(
            out=d, in_=s, allow_slow_non_contiguous=sl).then_inc(dc.h, 16))
        dc.inc(16)

    def ldg(dst, src):
        sc.g(lambda e, d=dst, s=src: e.dma_start(
            out=d, in_=s).then_inc(dg.h, 16))
        dg.inc(16)

    ld(L["ident_sb"][:], L["ident_h"])
    ld(L["identf_sb"][:], L["identf_h"])
    ld(L["onesb_sb"][:], L["onesb_h"])
    ld(L["onesf_sb"][:], L["onesf_h"])
    ld(L["consts_sb"][:], L["consts_h"])
    ld(L["bqkvT_sb"][:], L["bqkv"].rearrange("(m p) -> p m", p=128), slow=True)
    ld(L["gamT_sb"][:], L["gam"].rearrange("(m p) -> p m", p=128), slow=True)
    ld(L["betT_sb"][:], L["bet"].rearrange("(m p) -> p m", p=128), slow=True)
    ld(L["bin_sb"][:], L["binf"][:, None], slow=True)
    ld(L["b1T_sb"][:], L["b1f"].rearrange("(m p) -> p m", p=128), slow=True)
    ld(L["b2_sb"][:], L["b2"][:, None], slow=True)
    for e in ("tensor", "vector", "scalar"):
        sc.wait(e, dc, dc.n)
        sc.wait(e, dg, dg.n)


def phase_ln(sc, L):
    dxp, dwo = L["dxp"], L["dwo"]
    st, sv, ss = L["st"], L["sv"], L["ss"]
    xtile_sb, xh_tile, stats_sb, mv_sb, rstd_sb = (
        L["xtile_sb"], L["xh_tile"], L["stats_sb"], L["mv_sb"], L["rstd_sb"])
    xhq_sb, wq3_sb = L["xhq_sb"], L["wq3_sb"]
    kT_sb, qT_sb = L["kT_sb"], L["qT_sb"]
    qkv_ps, score_ps = L["qkv_ps"], L["score_ps"]
    x_hbm, wqkv = L["x_hbm"], L["wqkv"]
    ident_sb, gamT_sb, betT_sb, bqkvT_sb = (
        L["ident_sb"], L["gamT_sb"], L["betT_sb"], L["bqkvT_sb"])
    eps_ap = L["consts_sb"][:, 0:1]
    state = L["plstate"]

    dws = dwo
    for j, m in enumerate((8, 16, 0)):
        srcw = wqkv[:, 128 * m:128 * m + 128].rearrange(
            "(kc p) c -> p kc c", p=128)
        sc.g(lambda e, d=wq3_sb[:, j * 1024:(j + 1) * 1024].rearrange(
             "p (kc c) -> p kc c", kc=8), s=srcw, hh=dws.h:
             e.dma_start(out=d, in_=s).then_inc(hh, 16))
        dws.inc(16)
    w3_ready = dws.n

    NB = 3
    dma_cnt = [0] * NB
    v_xh_t = [0] * 32
    s_rstd_t = [0] * 32
    s_evac_t = [None] * 32
    last_xh_read = [0, 0]
    last_score_evac = [0, 0]
    v_gb_slice = [0] * 8
    last_qkv_evac = [0]
    w3_waited = [False]

    def issue_dma(j, guard):
        if j > 31:
            return
        b = j % NB
        xt = xtile_sb[:, b * C:(b + 1) * C]
        if guard:
            sc.wait("sync", sv, guard)
        dxs = dxp[b]
        sc.y(lambda e, d=xt, s=x_hbm[j * 128:j * 128 + 128, :], hh=dxs.h:
             e.dma_start(out=d, in_=s).then_inc(hh, 16))
        dma_cnt[b] = dxs.inc(16)

    pend = []
    for n in range(8):
        pend.append((8, n))
        pend.append((16, n))
        if n < 4:
            pend.append((0, n))
    pi = [0]
    pending_evac = []

    def qkv_burst():
        if pi[0] >= len(pend):
            return False
        m, n = pend[pi[0]]
        if not v_gb_slice[n]:
            return False
        pi[0] += 1
        if not w3_waited[0]:
            sc.wait("tensor", dws, w3_ready)
            w3_waited[0] = True
        j = {8: 0, 16: 1, 0: 2}[m]
        wbuf = wq3_sb[:, j * 1024:(j + 1) * 1024]
        sc.wait("tensor", sv, v_gb_slice[n])
        sc.wait("tensor", ss, last_qkv_evac[0])
        for kc in range(8):
            last = kc == 7
            def mk(o, w, r, s0, s1, last=last):
                def f(e):
                    ins = e.matmul(o, w, r, start=s0, stop=s1)
                    if last:
                        ins.then_inc(st.h, 1)
                    return ins
                return f
            sc.t(mk(qkv_ps[:], wbuf[:, kc * 128:kc * 128 + 128],
                    xhq_sb[:, kc * T_FULL + n * 512:
                           kc * T_FULL + n * 512 + 512],
                    kc == 0, kc == 7))
        pending_evac.append((st.inc(), m, n))
        return True

    def qkv_evac():
        if not pending_evac:
            return False
        t_mm, m, n = pending_evac.pop(0)
        if m == 8:
            dst = kT_sb[:, n * 512:n * 512 + 512]
        elif m == 16:
            dst = L["vstage_sb"][:, n * 512:n * 512 + 512]
        else:
            dst = qT_sb[:, n * 512:n * 512 + 512]
        sc.wait("scalar", st, t_mm)
        s_ev = sc.s(lambda e, o=dst, i_=qkv_ps[:], b=bqkvT_sb[:, m:m + 1]:
                    e.activation(out=o, in_=i_, func=AF.Identity, bias=b))
        last_qkv_evac[0] = s_ev
        if m == 16:
            state["vstage_ev"][n] = s_ev
        return True

    issue_dma(0, 0)
    issue_dma(1, 0)
    for j in range(33):
        if j < 32:
            b = j % NB
            xt = xtile_sb[:, b * C:(b + 1) * C]
            sc.wait("vector", dxp[b], dma_cnt[b])
            sc.v(lambda e, o=stats_sb[:, 0:6], i_=xt[:, 0:512]:
                 e.bn_stats(o, i_))
            sc.v(lambda e, o=stats_sb[:, 6:12], i_=xt[:, 512:1024]:
                 e.bn_stats(o, i_))
            mvb = mv_sb[:, 2 * (j % 2):2 * (j % 2) + 2]
            v_stats = sc.v(lambda e, o=mvb, i_=stats_sb[:]: e.bn_aggr(o, i_))
            rst = rstd_sb[:, j % 2:j % 2 + 1]
            sc.wait("scalar", sv, v_stats)
            sc.s(lambda e, o=rst, i_=mvb[:, 1:2], b=eps_ap:
                 e.activation(out=o, in_=i_, func=AF.Ln, bias=b))
            s_rstd_t[j] = sc.s(lambda e, o=rst, i_=rst:
                               e.activation(out=o, in_=i_, func=AF.Exp,
                                            scale=-0.5))
        if j >= 1:
            i = j - 1
            qkv_evac()
            par = i % 2
            bi = i % NB
            xt_i = xtile_sb[:, bi * C:(bi + 1) * C]
            xht = xh_tile[:, par * C:(par + 1) * C]
            mvb = mv_sb[:, 2 * par:2 * par + 2]
            rst = rstd_sb[:, par:par + 1]
            sc.wait("vector", ss, s_rstd_t[i])
            sc.wait("vector", st, last_xh_read[par])
            v_xh = sc.v(lambda e, o=xht, i_=xt_i, m=mvb[:, 0:1], r=rst:
                        e.tensor_scalar(o, i_, m, r, op0=AluOpType.subtract,
                                        op1=AluOpType.mult))
            v_xh_t[i] = v_xh
            issue_dma(i + 2, v_xh_t[i - 1] if i >= 1 else 0)
            sps = score_ps[:, par * 1024:par * 1024 + 1024]
            sc.wait("tensor", sv, v_xh)
            sc.wait("tensor", ss, last_score_evac[par])
            for kc in range(8):
                last = kc == 7
                def mk(o, w, r, last=last):
                    def f(e):
                        ins = e.matmul(o, w, r, start=True, stop=True,
                                       skip_group_check=True)
                        if last:
                            ins.then_inc(st.h, 1)
                        return ins
                    return f
                sc.t(mk(sps[:, kc * 128:kc * 128 + 128],
                        xht[:, kc * 128:kc * 128 + 128], ident_sb[:]))
            t_tp = st.inc()
            last_xh_read[par] = t_tp
            sc.wait("scalar", st, t_tp)
            dst = xhq_sb[:].rearrange("p (kc t) -> p kc t", kc=8)[
                :, :, i * 128:i * 128 + 128]
            srcv = sps.rearrange("p (kc t) -> p kc t", kc=8)
            s_e = sc.s(lambda e, o=dst, i_=srcv: e.copy(out=o, in_=i_))
            s_evac_t[i] = s_e
            last_score_evac[par] = s_e
            if i % 4 == 3:
                n = i // 4
                sc.wait("vector", ss, s_e)
                v_gb = 0
                for kc in range(8):
                    slc = xhq_sb[:, kc * T_FULL + n * 512:
                                 kc * T_FULL + n * 512 + 512]
                    v_gb = sc.v(lambda e, o=slc, i_=slc,
                                ga=gamT_sb[:, kc:kc + 1],
                                be=betT_sb[:, kc:kc + 1]:
                                e.tensor_scalar(o, i_, ga, be,
                                                op0=AluOpType.mult,
                                                op1=AluOpType.add))
                v_gb_slice[n] = v_gb
            qkv_burst()
    while pi[0] < len(pend) or pending_evac:
        made = qkv_evac()
        made = qkv_burst() or made
        if not made:
            break
    L["v_gb"] = v_gb_slice[7]
    state["lqe_s"] = last_qkv_evac[0]
    state["last_qkv_evac"] = 0
    state["last_tp_evac"] = 0
    L["ln_tp_evac"] = 0


def pipeline(sc, L):
    st, sv, ss = L["st"], L["sv"], L["ss"]
    state = L["plstate"]
    sc.wait("vector", ss, state["lqe_s"])
    sc.wait("tensor", ss, state["lqe_s"])

    def full_gen(sg):
        yield from qkv_steps_gen(sc, L, sg, state)
        yield from bound_gen(sc, L, sg, state)

    ident_sb, tp_ps, v_sb = L["ident_sb"], L["tp_ps"], L["v_sb"]
    vstage = L["vstage_sb"]

    def vstep_gen():
        for n in range(8):
            sc.wait("tensor", ss, state["vstage_ev"][n])
            sc.wait("tensor", ss, state["last_tp_evac"])
            sc.wait("tensor", sv, state["last_tp_evac_v"])
            for blk in range(4):
                last = blk == 3
                def mk(o, i_, idn, last=last):
                    def f(e):
                        ins = e.transpose(o, i_, idn)
                        if last:
                            ins.then_inc(st.h, 1)
                        return ins
                    return f
                sc.t(mk(tp_ps[:, blk * 128:blk * 128 + 128],
                        vstage[:, n * 512 + blk * 128:
                               n * 512 + blk * 128 + 128], ident_sb[:]))
            t_vt = st.inc()
            sc.wait("vector", st, t_vt)
            state["last_tp_evac_v"] = sc.v(
                lambda e, o=v_sb[:, (n * 4) * 128:(n * 4 + 4) * 128],
                i_=tp_ps[:, 0:512]: e.tensor_copy(o, i_))
            yield n

    vsteps = vstep_gen()
    for _ in bound_gen(sc, L, 0, state):
        next(vsteps, None)
    for _ in vsteps:
        pass
    state["pvz_free"] = state["last_tp_evac_v"]
    for sg in range(8):
        gen = full_gen(sg + 1) if sg + 1 < 8 else None
        p2_subgroup(sc, L, sg, state, gen)
        if gen is not None:
            for _ in gen:
                pass
        state["kv_parity_free"][sg % 2] = st.n


def qkv_steps_gen(sc, L, sg, state):
    dwp = L["dwp"]
    st, sv, ss = L["st"], L["sv"], L["ss"]
    xhq_sb, wq_sb, vtmp_sb = L["xhq_sb"], L["wq_sb"], L["vtmp_sb"]
    kT_sb, v_sb, qT_sb = L["kT_sb"], L["v_sb"], L["qT_sb"]
    qkv_ps, tp_ps = L["qkv_ps"], L["tp_ps"]
    wqkv = L["wqkv"]
    ident_sb, bqkvT_sb = L["ident_sb"], L["bqkvT_sb"]
    par = sg % 2
    kbase = par * T_FULL
    qbase = par * T_OWN

    k_tiles = [8 + sg]
    v_tiles = [16 + sg]
    q_tiles = [sg]
    mlist = k_tiles + v_tiles + q_tiles
    wq_ready = {}

    def issue_wload(j):
        m = mlist[j]
        wbuf = wq_sb[:, (j % 2) * 1024:(j % 2) * 1024 + 1024]
        sc.wait("gpsimd", st, state["last_wq_consumed"][j % 2])
        srcw = wqkv[:, 128 * m:128 * m + 128].rearrange(
            "(kc p) c -> p kc c", p=128)
        dws = dwp[j % 2]
        sc.g(lambda e, d=wbuf.rearrange("p (kc c) -> p kc c", kc=8), s=srcw,
             hh=dws.h: e.dma_start(out=d, in_=s).then_inc(hh, 16))
        wq_ready[j] = (dws, dws.inc(16))

    issue_wload(0)
    issue_wload(1)
    first = True
    for j, m in enumerate(mlist):
        wbuf = wq_sb[:, (j % 2) * 1024:(j % 2) * 1024 + 1024]
        nrange = 4 if m in q_tiles else 8
        for n in range(nrange):
            if m in k_tiles:
                dst = kT_sb[:, kbase + n * 512:kbase + n * 512 + 512]
            elif m in v_tiles:
                dst = vtmp_sb[:, (n % 2) * 512:(n % 2) * 512 + 512]
            else:
                dst = qT_sb[:, qbase + n * 512:qbase + n * 512 + 512]
            sc.wait("tensor", wq_ready[j][0], wq_ready[j][1])
            if first:
                sc.wait("tensor", sv, L["v_gb"])
                if sg >= 2:
                    sc.wait("vector", st, state["kv_parity_free"][par])
                first = False
            sc.wait("tensor", sv, state["last_qkv_evac"])
            for kc in range(8):
                if kc == 4:
                    yield (m, n, "mid")
                last = kc == 7
                def mk(o, w, r, s0, s1, last=last):
                    def f(e):
                        ins = e.matmul(o, w, r, start=s0, stop=s1)
                        if last:
                            ins.then_inc(st.h, 1)
                        return ins
                    return f
                sc.t(mk(qkv_ps[:], wbuf[:, kc * 128:kc * 128 + 128],
                        xhq_sb[:, kc * T_FULL + n * 512:
                               kc * T_FULL + n * 512 + 512],
                        kc == 0, kc == 7))
            t_mm = st.inc()
            yield (m, n, 0)
            sc.wait("vector", st, t_mm)
            if m in v_tiles:
                sc.wait("vector", st, state["last_vtmp_read"][n % 2])
            v_ev = sc.v(lambda e, o=dst, i_=qkv_ps[:], b=bqkvT_sb[:, m:m + 1]:
                        e.tensor_scalar(o, i_, b, None, op0=AluOpType.add,
                                        op1=AluOpType.bypass))
            state["last_qkv_evac"] = v_ev
            if n == nrange - 1:
                state["last_wq_consumed"][j % 2] = st.n
                if j + 2 < len(mlist):
                    issue_wload(j + 2)
            if m in v_tiles:
                sc.wait("tensor", sv, v_ev)
                sc.wait("tensor", ss, state["last_tp_evac"])
                sc.wait("tensor", sv, state["last_tp_evac_v"])
                for blk in range(4):
                    last = blk == 3
                    def mk(o, i_, idn, last=last):
                        def f(e):
                            ins = e.transpose(o, i_, idn)
                            if last:
                                ins.then_inc(st.h, 1)
                            return ins
                        return f
                    sc.t(mk(tp_ps[:, blk * 128:blk * 128 + 128],
                            dst[:, blk * 128:blk * 128 + 128],
                            ident_sb[:]))
                t_vt = st.inc()
                state["last_vtmp_read"][n % 2] = t_vt
                vdst = v_sb[:, kbase + (n * 4) * 128:
                            kbase + (n * 4 + 4) * 128]
                sc.wait("vector", st, t_vt)
                state["last_tp_evac_v"] = sc.v(
                    lambda e, o=vdst, i_=tp_ps[:, 0:512]:
                    e.tensor_copy(o, i_))
            yield (m, n, 4)


def bound_gen(sc, L, sg, state):
    st, sv, ss = L["st"], L["sv"], L["ss"]
    kT_sb, qT_sb, sq_sb, nmax_sb, negB_sb = (
        L["kT_sb"], L["qT_sb"], L["sq_sb"], L["nmax_sb"], L["negB_sb"])
    onesb_sb = L["onesb_sb"]
    qkv_ps = L["qkv_ps"]
    mln8_ap = L["consts_sb"][:, 1:2]
    par = sg % 2
    kbase = par * T_FULL
    qbase = par * T_OWN
    last_sq_read = state.setdefault("bound_sq_read", [0])

    v_nb = 0
    for pl in range(1):
        p = sg
        srcs = [(qT_sb[:, qbase:qbase + T_OWN], 0),
                (kT_sb[:, kbase:kbase + 2048], 1),
                (kT_sb[:, kbase + 2048:kbase + 4096], 2)]
        for (srcq, ci) in srcs:
            sc.wait("vector", st, last_sq_read[0])
            v_sq = sc.v(lambda e, o=sq_sb[:], a=srcq:
                        e.tensor_tensor(o, a, a, op=AluOpType.mult))
            sc.wait("tensor", sv, v_sq)
            sc.wait("tensor", sv, state["last_qkv_evac"])
            acc = nmax_sb[:, ci + 4:ci + 5]
            tgt = nmax_sb[:, ci:ci + 1]
            for n4 in range(4):
                def mk(o, w, r):
                    def f(e):
                        ins = e.matmul(o, w, r, start=True, stop=True)
                        ins.then_inc(st.h, 1)
                        return ins
                    return f
                sc.t(mk(qkv_ps[:], onesb_sb[:],
                        sq_sb[:, n4 * 512:n4 * 512 + 512]))
                t_nm = st.inc()
                last_sq_read[0] = t_nm
                sc.wait("vector", st, t_nm)
                if n4 == 0:
                    v_r = sc.v(lambda e, o=tgt, i_=qkv_ps[:]:
                               e.tensor_reduce(o, i_, axis=AX.X,
                                               op=AluOpType.max))
                else:
                    sc.v(lambda e, o=acc, i_=qkv_ps[:]:
                         e.tensor_reduce(o, i_, axis=AX.X, op=AluOpType.max))
                    v_r = sc.v(lambda e, o=tgt, a=tgt, b=acc:
                               e.tensor_tensor(o, a, b, op=AluOpType.max))
                sc.wait("tensor", sv, v_r)
                yield ("bound", p, ci, n4)
        v_k = sc.v(lambda e, o=nmax_sb[:, 3:4], a=nmax_sb[:, 1:2],
                   b=nmax_sb[:, 2:3]:
                   e.tensor_tensor(o, a, b, op=AluOpType.max))
        sc.wait("scalar", sv, v_k)
        sc.s(lambda e, o=nmax_sb[:, 4:5], i_=nmax_sb[:, 0:1]:
             e.activation(out=o, in_=i_, func=AF.Ln, bias=0.0))
        s_ln = sc.s(lambda e, o=nmax_sb[:, 5:6], i_=nmax_sb[:, 3:4]:
                    e.activation(out=o, in_=i_, func=AF.Ln, bias=0.0))
        sc.wait("vector", ss, s_ln)
        v_add = sc.v(lambda e, o=nmax_sb[:, 6:7], a=nmax_sb[:, 4:5],
                     b=nmax_sb[:, 5:6]:
                     e.tensor_tensor(o, a, b, op=AluOpType.add))
        sc.wait("scalar", sv, v_add)
        s_B = sc.s(lambda e, o=nmax_sb[:, 7:8], i_=nmax_sb[:, 6:7], b=mln8_ap:
                   e.activation(out=o, in_=i_, func=AF.Exp, scale=0.5,
                                bias=b))
        sc.wait("vector", ss, s_B)
        v_nb = sc.v(lambda e, o=negB_sb[:, p:p + 1], i_=nmax_sb[:, 7:8]:
                    e.tensor_scalar(o, i_, -1.0, -1.0, op0=AluOpType.mult,
                                    op1=AluOpType.add))
        yield ("negB", p, 0)
    state["v_negB"][sg] = v_nb


def p2_subgroup(sc, L, sg, state, gen):
    st, sv, ss = L["st"], L["sv"], L["ss"]
    kT_sb, v_sb, qT_sb, PT_sb, zb_sb, attnT_sb = (
        L["kT_sb"], L["v_sb"], L["qT_sb"], L["PT_sb"], L["zb_sb"],
        L["attnT_sb"])
    onesb_sb, negB_sb = L["onesb_sb"], L["negB_sb"]
    score_ps, pv_ps, z_ps = L["score_ps"], L["pv_ps"], L["z_ps"]
    par = sg % 2
    kbase = par * T_FULL
    qbase = par * T_OWN
    it_hist = state["it_hist"]
    p = sg

    def emit_scores(qt, kt, guard):
        spar = kt % 2
        sps = score_ps[:, spar * 1024:spar * 1024 + 1024]
        kA = kT_sb[0:64, kbase + kt * 128:kbase + kt * 128 + 128]
        kB = kT_sb[64:128, kbase + kt * 128:kbase + kt * 128 + 128]
        qA = qT_sb[0:64, qbase + qt * 512:qbase + qt * 512 + 512]
        qB = qT_sb[64:128, qbase + qt * 512:qbase + qt * 512 + 512]
        if guard is not None:
            sc.wait("tensor", ss, guard)
        sc.t(lambda e, o=sps[:, 0:512], w=kA, r=qA:
             e.matmul(o, w, r, start=True, stop=True))
        sc.t(lambda e, o=sps[:, 512:1024], w=kB, r=qB:
             e.matmul(o, w, r, start=True, stop=True).then_inc(st.h, 1))
        return st.inc()

    def emit_pvz(qt, kt, it, exp_done_ref=None):
        spar = kt % 8
        pt = PT_sb[:, spar * 1024:spar * 1024 + 1024]
        vch = v_sb[:, kbase + kt * 128:kbase + kt * 128 + 128]
        sc.wait("tensor", ss, exp_done_ref[kt])
        if kt == 0 and it >= 1:
            sc.wait("tensor", sv, it_hist[it - 1]["norm_v"])
        elif kt == 0:
            sc.wait("tensor", sv, state["pvz_free"])
        s0, s1 = (kt == 0), (kt == NKT - 1)
        sc.t(lambda e, o=pv_ps[0:64, :], w=vch[:, 0:64], r=pt[:, 0:512],
             a=s0, b=s1: e.matmul(o, w, r, start=a, stop=b,
                                  skip_group_check=True))
        sc.t(lambda e, o=pv_ps[64:128, :], w=vch[:, 64:128],
             r=pt[:, 512:1024], a=s0, b=s1:
             e.matmul(o, w, r, start=a, stop=b, skip_group_check=True))
        sc.t(lambda e, o=z_ps[0:64, :], w=onesb_sb[:, 0:64],
             r=pt[:, 0:512], a=s0, b=s1:
             e.matmul(o, w, r, start=a, stop=b, skip_group_check=True))
        sc.t(lambda e, o=z_ps[64:128, :], w=onesb_sb[:, 0:64],
             r=pt[:, 512:1024], a=s0, b=s1:
             e.matmul(o, w, r, start=a, stop=b,
                      skip_group_check=True).then_inc(st.h, 1))
        return st.inc()

    def finish_qt(h, pqt, hit):
        sc.wait("tensor", ss, h["exp_done"][NKT - 1])
        for j in (NKT - 2, NKT - 1):
            h["pvz_done"][j] = emit_pvz(pqt, j, hit, h["exp_done"])
        if gen is not None:
            next(gen, None)
            next(gen, None)
        zbh = zb_sb[:]
        sc.wait("vector", st, h["pvz_done"][NKT - 1])
        sc.v(lambda e, o=zbh, i_=z_ps[:]: e.reciprocal(o, i_))
        dst = attnT_sb[:, p * T_OWN + pqt * 512:p * T_OWN + pqt * 512 + 512]
        h["norm_v"] = sc.v(lambda e, o=dst, a=pv_ps[:], b=zbh:
                           e.tensor_tensor(o, a, b, op=AluOpType.mult))

    first_of_sg = True
    for qt in range(NQT):
        it = len(it_hist)
        exp_done = []
        sc_done = []
        pvz_done = {}
        gprev = it_hist[it - 1]["exp_done"][NKT - 2] if it >= 1 else None
        sc_done.append(emit_scores(qt, 0, gprev))
        if first_of_sg:
            sc.wait("scalar", sv, state["v_negB"][sg])
            first_of_sg = False
        if qt >= 1:
            finish_qt(it_hist[it - 1], qt - 1, it - 1)
        it_hist.append(dict(exp_done=exp_done, pvz_done=pvz_done, norm_v=0))
        for kt in range(NKT):
            if kt + 1 < NKT:
                if kt + 1 >= 2:
                    gg = exp_done[kt - 1]
                elif it >= 1:
                    gg = it_hist[it - 1]["exp_done"][NKT - 1]
                else:
                    gg = None
                sc_done.append(emit_scores(qt, kt + 1, gg))
            else:
                sc.wait("tensor", ss, exp_done[kt - 1])
            if kt % 2 == 1 and kt >= 3:
                pvz_done[kt - 3] = emit_pvz(qt, kt - 3, it, exp_done)
                pvz_done[kt - 2] = emit_pvz(qt, kt - 2, it, exp_done)
            spar = kt % 8
            pt = PT_sb[:, spar * 1024:spar * 1024 + 1024]
            g1 = sc_done[kt]
            if kt >= 8:
                g1 = max(g1, pvz_done[kt - 8])
            sc.wait("scalar", st, g1)
            if kt < 8 and it >= 1:
                sc.wait("scalar", st,
                        it_hist[it - 1]["pvz_done"][NKT - 8 + kt])
            s_exp = sc.s(lambda e, o=pt,
                         i_=score_ps[:, (kt % 2) * 1024:(kt % 2) * 1024 + 1024],
                         b=negB_sb[:, p:p + 1]:
                         e.activation(out=o, in_=i_, func=AF.Exp, bias=b,
                                      scale=0.125))
            exp_done.append(s_exp)
            if gen is not None and kt % 2 == 0:
                next(gen, None)
    finish_qt(it_hist[-1], NQT - 1, len(it_hist) - 1)


def p34(sc, L):
    dxp, dwo, dop = L["dxp"], L["dwo"], L["dop"]
    st, sv, ss = L["st"], L["sv"], L["ss"]
    attnT_sb, wout_sb = L["attnT_sb"], L["wout_sb"]
    x2_sb, x2bf_sb, x2T_sb, xres_sb = (L["x2_sb"], L["x2bf_sb"], L["x2T_sb"],
                                       L["xres_sb"])
    boutbc_sb, bffobc_sb = L["boutbc_sb"], L["bffobc_sb"]
    sT_sb, hT_sb, gT_sb, u_sb = L["sT_sb"], L["hT_sb"], L["gT_sb"], L["u_sb"]
    stats2_sb, mv2_sb, rstd2_sb = L["stats2_sb"], L["mv2_sb"], L["rstd2_sb"]
    A_ps, B_ps, C_ps, F_ps = L["A_ps"], L["B_ps"], L["C_ps"], L["F_ps"]
    onesf_sb, boutrow_sb, bfforow_sb = (L["onesf_sb"], L["boutrow_sb"],
                                        L["bfforow_sb"])
    win_sb, rw1_sb, w2b_sb, wfo_sb = (L["win_sb"], L["rw1_sb"], L["w2b_sb"],
                                      L["wfo_sb"])
    bin_sb, b1T_sb, b2_sb = L["bin_sb"], L["b1T_sb"], L["b2_sb"]
    ident_sb = L["ident_sb"]
    x_hbm, out_hbm = L["x_hbm"], L["out_hbm"]
    eps_ap = L["consts_sb"][:, 0:1]

    for dstw, srcw in (
        (wout_sb[:].rearrange("p (kc m) -> p kc m", kc=8),
         L["wout"].rearrange("(kc p) m -> p kc m", p=128)),
        (win_sb[:].rearrange("p (kc m) -> p kc m", kc=8),
         L["win"].rearrange("(kc p) m -> p kc m", p=128)),
        (rw1_sb[:], L["rw1"]),
        (w2b_sb[:].rearrange("p (kc m) -> p kc m", kc=8),
         L["w2b"].rearrange("(kc p) m -> p kc m", p=128)),
        (wfo_sb[:], L["wfo"]),
        (boutrow_sb[:], L["bout"][None, :]),
        (bfforow_sb[:], L["bffo"][None, :]),
    ):
        sc.g(lambda e, d=dstw, s=srcw, hh=dwo.h:
             e.dma_start(out=d, in_=s).then_inc(hh, 16))
        dwo.inc(16)
    w_all = dwo.n
    sc.wait("tensor", dwo, w_all)
    sc.wait("scalar", dwo, w_all)
    sc.wait("vector", dwo, w_all)

    v_bcp = 0
    for (bcdst, brow) in ((boutbc_sb, boutrow_sb), (bffobc_sb, bfforow_sb)):
        sc.wait("tensor", sv, v_bcp)
        for nh in range(2):
            last = nh == 1
            def mk(o, w, r, last=last):
                def f(e):
                    ins = e.matmul(o, w, r, start=True, stop=True)
                    if last:
                        ins.then_inc(st.h, 1)
                    return ins
                return f
            sc.t(mk(A_ps[:, nh * 512:nh * 512 + 512], onesf_sb[:],
                    brow[:, nh * 512:nh * 512 + 512]))
        t_bc = st.inc()
        sc.wait("vector", st, t_bc)
        v_bcp = sc.v(lambda e, o=bcdst[:], i_=A_ps[:, 0:1024]:
                     e.tensor_copy(o, i_))
    sc.wait("tensor", sv, v_bcp)

    def batched_ln(uslot0, outs, outs2, slot):
        stt = stats2_sb[:, slot * 48:slot * 48 + 48]
        mv = mv2_sb[:, slot * 8:slot * 8 + 8]
        rst = rstd2_sb[:, slot * 4:slot * 4 + 4]
        for t in range(4):
            u_ap = u_sb[:, (uslot0 + t) * C:(uslot0 + t + 1) * C]
            sc.v(lambda e, o=stt[:, t * 12:t * 12 + 6], i_=u_ap[:, 0:512]:
                 e.bn_stats(o, i_))
            sc.v(lambda e, o=stt[:, t * 12 + 6:t * 12 + 12],
                 i_=u_ap[:, 512:1024]: e.bn_stats(o, i_))
        v_st = 0
        for t in range(4):
            v_st = sc.v(lambda e, o=mv[:, t * 2:t * 2 + 2],
                        i_=stt[:, t * 12:t * 12 + 12]: e.bn_aggr(o, i_))
        sc.wait("scalar", sv, v_st)
        var_ap = mv[:].rearrange("p (t two) -> p two t", two=2)[:, 1, :]
        sc.s(lambda e, o=rst, i_=var_ap, b=eps_ap:
             e.activation(out=o, in_=i_, func=AF.Ln, bias=b))
        s_r = sc.s(lambda e, o=rst, i_=rst:
                   e.activation(out=o, in_=i_, func=AF.Exp, scale=-0.5))
        sc.wait("vector", ss, s_r)
        rets = []
        for t in range(4):
            u_ap = u_sb[:, (uslot0 + t) * C:(uslot0 + t + 1) * C]
            v_ap = sc.v(lambda e, o=outs[t], i_=u_ap, m=mv[:, t * 2:t * 2 + 1],
                        r=rst[:, t:t + 1]:
                        e.tensor_scalar(o, i_, m, r, op0=AluOpType.subtract,
                                        op1=AluOpType.mult))
            if outs2[t] is not None:
                sc.wait("scalar", sv, v_ap)
                s_c = sc.s(lambda e, o=outs2[t], i_=outs[t]:
                           e.copy(out=o, in_=i_))
                rets.append(("s", s_c))
            else:
                rets.append(("v", v_ap))
        return rets

    a_guard_v = [0, 0]
    b_guard = [0, 0]
    o_stored = [0, 0, 0, 0]
    last_xrb = [0, 0, 0, 0]
    t_sT_read = [0, 0]
    t_h_read = [0, 0]
    t_g_read = [0, 0]
    s_binev = [0, 0]
    s_gev = [0, 0]
    f_guard_v = [0]
    f_guard_s = [0]

    x2T_done = [0, 0]
    x2cast_st = [[0] * 4, [0] * 4]

    def p3a_stage(sl):
            spar = sl % 2
            t_ops = []
            for tt in range(4):
                half = tt % 2
                row0 = sl * 512 + tt * 128
                ap = A_ps[:, half * 1024:half * 1024 + 1024]
                xr = xres_sb[:, tt * C:(tt + 1) * C]
                sc.wait("sync", sv, last_xrb[tt])
                dxs = dxp[tt]
                sc.y(lambda e, d=xr, s=x_hbm[row0:row0 + 128, :], hh=dxs.h:
                     e.dma_start(out=d, in_=s).then_inc(hh, 16))
                v_dx = (dxs, dxs.inc(16))
                xrb = xr
                sc.wait("vector", v_dx[0], v_dx[1])
                v_xrb = sc.v(lambda e, o=xrb, i0=xr, b=boutbc_sb[:]:
                             e.scalar_tensor_tensor(o, i0, 2.0, b,
                                                    op0=AluOpType.mult,
                                                    op1=AluOpType.add))
                last_xrb[tt] = v_xrb
                sc.wait("tensor", sv, a_guard_v[half])
                for nh in range(2):
                    for kc in range(8):
                        last = (nh == 1 and kc == 7)
                        def mk(o, w, r, s0, s1, last=last):
                            def f(e):
                                ins = e.matmul(o, w, r, start=s0, stop=s1)
                                if last:
                                    ins.then_inc(st.h, 1)
                                return ins
                            return f
                        sc.t(mk(ap[:, nh * 512:nh * 512 + 512],
                                attnT_sb[:, kc * T_OWN + row0:
                                         kc * T_OWN + row0 + 128],
                                wout_sb[:, kc * 1024 + nh * 512:
                                        kc * 1024 + nh * 512 + 512],
                                kc == 0, kc == 7))
                t_op = st.inc()
                u = u_sb[:, tt * C:(tt + 1) * C]
                sc.wait("vector", st, t_op)
                a_guard_v[half] = sc.v(lambda e, o=u, a=ap, b=xrb:
                                       e.tensor_tensor(o, a, b,
                                                       op=AluOpType.add))
                last_xrb[tt] = a_guard_v[half]

    def p3bc_stage(sl):
            spar = sl % 2
            outs = [x2_sb[:, spar * 4 * C + t * C:spar * 4 * C + (t + 1) * C]
                    for t in range(4)]
            outs2 = [x2bf_sb[:, t * C:(t + 1) * C] for t in range(4)]
            ln_rets = batched_ln(0, outs, outs2, 0)
            x2cast_st[spar] = [c for (k, c) in ln_rets]
            for tt in range(4):
                tph = B_ps[:]
                kind, cnt = ln_rets[tt]
                sc.wait("tensor", ss if kind == "s" else sv, cnt)
                sc.wait("tensor", ss, b_guard[0])
                for kc in range(8):
                    last = kc == 7
                    def mk(o, i_, idn, last=last):
                        def f(e):
                            ins = e.transpose(o, i_, idn)
                            if last:
                                ins.then_inc(st.h, 1)
                            return ins
                        return f
                    sc.t(mk(tph[:, kc * 128:kc * 128 + 128],
                            x2bf_sb[:, tt * C + kc * 128:tt * C + kc * 128 + 128],
                            ident_sb[:]))
                t_tp = st.inc()
                sc.wait("scalar", st, t_tp)
                sc.wait("scalar", st, t_sT_read[spar])
                dst = x2T_sb[:].rearrange("p (sp kc t) -> p sp kc t",
                                          sp=2, kc=8)[
                    :, spar, :, tt * 128:tt * 128 + 128]
                srcv = tph.rearrange("p (kc t) -> p kc t", kc=8)
                b_guard[0] = sc.s(lambda e, o=dst, i_=srcv:
                                  e.copy(out=o, in_=i_))
            x2T_done[sl % 2] = max(b_guard)


    def p4ab_stage(sl):
            spar = sl % 2
            x2Tsl = x2T_sb[:, spar * 4096:spar * 4096 + 4096]
            sTsl = sT_sb[:, spar * 512:spar * 512 + 512]
            hTsl = hT_sb[:, spar * 4096:spar * 4096 + 4096]
            gTsl = gT_sb[:, spar * 512:spar * 512 + 512]
            sc.wait("tensor", ss, x2T_done[spar])
            sc.wait("tensor", ss, max(s_gev))
            sc.wait("tensor", ss, s_binev[spar])
            for kc in range(8):
                last = kc == 7
                def mk(o, w, r, s0, s1, last=last):
                    def f(e):
                        ins = e.matmul(o, w, r, start=s0, stop=s1)
                        if last:
                            ins.then_inc(st.h, 1)
                        return ins
                    return f
                sc.t(mk(C_ps[0:64, :], win_sb[:, kc * 64:kc * 64 + 64],
                        x2Tsl[:, kc * 512:kc * 512 + 512], kc == 0, kc == 7))
            t_sT = st.inc()
            t_sT_read[spar] = t_sT
            sc.wait("scalar", st, t_sT)
            sc.wait("scalar", st, t_h_read[spar])
            s_sT = sc.s(lambda e, o=sTsl, i_=C_ps[0:64, :], b=bin_sb[:]:
                        e.activation(out=o, in_=i_, func=AF.Identity, bias=b))
            s_binev[spar] = s_sT
            sc.wait("tensor", ss, s_sT)
            h_evac = [0, 0]
            for j in range(8):
                half = j % 2
                hp = F_ps[:, half * 512:half * 512 + 512]
                sc.wait("tensor", ss, h_evac[half])
                if j < 2:
                    sc.wait("tensor", sv, f_guard_v[0])
                    sc.wait("tensor", ss, f_guard_s[0])
                sc.t(lambda e, o=hp, w=rw1_sb[:, j * 128:j * 128 + 128],
                     r=sTsl:
                     e.matmul(o, w, r, start=True, stop=True,
                              skip_group_check=True).then_inc(st.h, 1))
                t_h = st.inc()
                t_h_read[spar] = t_h
                sc.wait("scalar", st, t_h)
                if j < 2:
                    sc.wait("scalar", st, t_g_read[spar])
                hev = sc.s(lambda e, o=hTsl[:, j * 512:j * 512 + 512], i_=hp,
                           b=b1T_sb[:, j:j + 1]:
                           e.activation(out=o, in_=i_, func=AF.Relu, bias=b))
                h_evac[half] = hev
                f_guard_s[0] = hev
            sc.wait("tensor", ss, h_evac[0])
            sc.wait("tensor", ss, h_evac[1])
            for j in range(8):
                last = j == 7
                def mk(o, w, r, s0, s1, last=last):
                    def f(e):
                        ins = e.matmul(o, w, r, start=s0, stop=s1)
                        if last:
                            ins.then_inc(st.h, 1)
                        return ins
                    return f
                sc.t(mk(C_ps[0:64, :], w2b_sb[:, j * 64:j * 64 + 64],
                        hTsl[:, j * 512:j * 512 + 512], j == 0, j == 7))
            t_g = st.inc()
            sc.wait("scalar", st, t_g)
            s_g = sc.s(lambda e, o=gTsl, i_=C_ps[0:64, :], b=b2_sb[:]:
                       e.activation(out=o, in_=i_, func=AF.Identity, bias=b))
            s_gev[spar] = s_g
            sc.wait("tensor", ss, s_g)
            for tt in range(4):
                row0 = sl * 512 + tt * 128
                fpA = F_ps[:]
                x2t = x2_sb[:, spar * 4 * C + tt * C:spar * 4 * C + (tt + 1) * C]
                x2pb = x2t
                sc.wait("vector", ss, x2cast_st[spar][tt])
                sc.v(lambda e, o=x2pb, a=x2t, b=bffobc_sb[:]:
                     e.tensor_tensor(o, a, b, op=AluOpType.add))
                sc.wait("tensor", sv, f_guard_v[0])
                sc.wait("tensor", ss, f_guard_s[0])
                for nh in range(2):
                    last = nh == 1
                    def mk(o, w, r, last=last):
                        def f(e):
                            ins = e.matmul(o, w, r, start=True, stop=True)
                            if last:
                                ins.then_inc(st.h, 1)
                            return ins
                        return f
                    sc.t(mk(fpA[:, nh * 512:nh * 512 + 512],
                            gTsl[:, tt * 128:tt * 128 + 128],
                            wfo_sb[:, nh * 512:nh * 512 + 512]))
                t_f = st.inc()
                t_g_read[spar] = t_f
                u = u_sb[:, (4 + tt) * C:(5 + tt) * C]
                sc.wait("vector", st, t_f)
                if o_stored[tt]:
                    sc.wait("vector", dop[tt % 2], dop[tt % 2].n)
                f_guard_v[0] = sc.v(lambda e, o=u, a=fpA, b=x2pb:
                                    e.tensor_tensor(o, a, b, op=AluOpType.add))

    def p4c_stage(sl):
            spar = sl % 2
            outs = [u_sb[:, (4 + t) * C:(5 + t) * C] for t in range(4)]
            v_out = batched_ln(4, outs, [None] * 4, 1)
            for tt in range(4):
                row0 = sl * 512 + tt * 128
                u = u_sb[:, (4 + tt) * C:(5 + tt) * C]
                sc.wait("sync", sv, v_out[tt][1])
                dos = dop[tt % 2]
                sc.y(lambda e, d=out_hbm[row0:row0 + 128, :], s=u, hh=dos.h:
                     e.dma_start(out=d, in_=s).then_inc(hh, 16))
                o_stored[tt] = dos.inc(16)

    for sl in range(5):
        if sl < 4:
            p3a_stage(sl)
        if sl >= 1:
            p4ab_stage(sl - 1)
        if sl < 4:
            p3bc_stage(sl)
        if sl >= 1:
            p4c_stage(sl - 1)

_NC_CACHE = {}


def kernel(**inputs):
    x = np.ascontiguousarray(np.asarray(inputs["L0_attn_x"], dtype=np.float32))
    gam = np.asarray(inputs["L0_attn_ln_gamma"], dtype=np.float32)
    bet = np.asarray(inputs["L0_attn_ln_beta"], dtype=np.float32)
    wqkv = np.ascontiguousarray(np.asarray(inputs["L0_attn_w_qkv"], dtype=np.float32))
    bqkv = np.asarray(inputs["L0_attn_b_qkv"], dtype=np.float32)
    wout = np.ascontiguousarray(np.asarray(inputs["L0_attn_w_out"], dtype=np.float32))
    bout = np.asarray(inputs["L0_attn_b_out"], dtype=np.float32)
    win = np.ascontiguousarray(np.asarray(inputs["L0_ffn_Win"], dtype=np.float32))
    binf = np.asarray(inputs["L0_ffn_bin_"], dtype=np.float32)
    w1 = np.asarray(inputs["L0_ffn_W1"], dtype=np.float32)
    b1 = np.asarray(inputs["L0_ffn_b1"], dtype=np.float32)
    w2 = np.asarray(inputs["L0_ffn_W2"], dtype=np.float32)
    b2 = np.asarray(inputs["L0_ffn_b2"], dtype=np.float32)
    wfo = np.ascontiguousarray(np.asarray(inputs["L0_ffn_Wout"], dtype=np.float32))
    bffo = np.asarray(inputs["L0_ffn_bout"], dtype=np.float32)

    M, INNER = 64, 16
    rw1 = np.zeros((M, M * INNER), np.float32)
    for m in range(M):
        rw1[m, m * INNER:(m + 1) * INNER] = w1[m]
    w2b = np.zeros((M * INNER, M), np.float32)
    for m in range(M):
        w2b[m * INNER:(m + 1) * INNER, m] = w2[m]
    b1flat = np.ascontiguousarray(b1.reshape(M * INNER))

    ident = np.eye(128, dtype=np.float32).astype(ml_dtypes.bfloat16)
    identf = np.eye(128, dtype=np.float32)
    onesb = np.ones((128, 128), np.float32).astype(ml_dtypes.bfloat16)
    onesf = np.ones((1, 128), np.float32)
    consts = np.zeros((128, 2), np.float32)
    consts[:, 0] = EPS
    consts[:, 1] = MINUS_LN8

    shared = dict(wqkv=wqkv, bqkv=bqkv, gam=gam, bet=bet, wout=wout, bout=bout,
                  win=win, binf=binf, rw1=rw1, b1f=b1flat, w2b=w2b, b2=b2,
                  wfo=wfo, bffo=bffo, ident=ident, identf=identf, onesb=onesb, onesf=onesf,
                  consts=consts)

    in_maps = []
    for c in range(8):
        b, h = c // 2, c % 2
        own = x[b, h * T_OWN:(h + 1) * T_OWN]
        other = x[b, (1 - h) * T_OWN:(2 - h) * T_OWN]
        xp = np.ascontiguousarray(np.concatenate([own, other], axis=0))
        m = dict(shared)
        m["x"] = xp
        in_maps.append(m)

    if "nc" not in _NC_CACHE:
        _NC_CACHE["nc"] = build_nc()
    nc = _NC_CACHE["nc"]
    res = run_bass_kernel_spmd(nc, in_maps, core_ids=list(range(8)))

    out = np.empty((4, T_FULL, C), np.float32)
    for c in range(8):
        b, h = c // 2, c % 2
        out[b, h * T_OWN:(h + 1) * T_OWN] = res.results[c]["out"]
    return out


# version 29
# speedup vs baseline: 1.0637x; 1.0081x over previous
from contextlib import ExitStack

import numpy as np
import ml_dtypes

import concourse.bass as bass
import concourse.mybir as mybir
from concourse.alu_op_type import AluOpType
from concourse.bass_utils import run_bass_kernel_spmd

F32 = mybir.dt.float32
BF16 = mybir.dt.bfloat16
AF = mybir.ActivationFunctionType
AX = mybir.AxisListType

T_FULL = 4096
T_OWN = 2048
C = 1024
NKC = 8
NPAIR = 8
PAIRS_PER_G = 4
NGROUP = 2
NQT = 4
NKT = 32
EPS = 1e-5
MINUS_LN8 = -2.0794415416798357


class Sem:
    def __init__(self, handle):
        self.h = handle
        self.n = 0

    def inc(self, k=1):
        self.n += k
        return self.n


class Script:

    def __init__(self, sv, ss):
        self.items = []
        self._sv = sv
        self._ss = ss

    def t(self, fn):
        self.items.append(("tensor", fn, False))

    def v(self, fn):
        self.items.append(("vector", fn, True))
        return self._sv.inc()

    def s(self, fn):
        self.items.append(("scalar", fn, True))
        return self._ss.inc()

    def y(self, fn):
        self.items.append(("sync", fn, False))

    def g(self, fn):
        self.items.append(("gpsimd", fn, False))

    def wait(self, eng, sem, val):
        if val <= 0:
            return
        self.items.append(
            (eng, lambda e, s=sem.h, v=val: e.wait_ge(s, v), False))

    def run(self, engine_name, eng, chain_sem, chain_base):
        n = chain_base
        for name, fn, chained in self.items:
            if name != engine_name:
                continue
            if chained:
                if n > 0:
                    eng.wait_ge(chain_sem, n)
                ins = fn(eng)
                assert ins is not None, "chained entry must return instruction"
                ins.then_inc(chain_sem, 1)
                n += 1
            else:
                fn(eng)
        return n


def emit_block(nc, script, L):
    sv, ss = L["sv"], L["ss"]
    with nc.Block() as block:
        @block.tensor
        def _(t):
            script.run("tensor", t, None, 0)

        @block.vector
        def _(v):
            base = L["chain_done"]["vector"]
            L["chain_done"]["vector"] = script.run("vector", v, sv.h, base)

        @block.scalar
        def _(s):
            base = L["chain_done"]["scalar"]
            L["chain_done"]["scalar"] = script.run("scalar", s, ss.h, base)

        @block.sync
        def _(y):
            script.run("sync", y, None, 0)

        @block.gpsimd
        def _(g):
            script.run("gpsimd", g, None, 0)


def barrier(sc, L):
    st, sv, ss, sy, sg = L["st"], L["sv"], L["ss"], L["sy"], L["sg"]
    t_t = st.inc()
    sc.t(lambda e, s=st.h: e.drain().then_inc(s, 1))
    t_v = sc.v(lambda e: e.drain())
    t_s = sc.s(lambda e: e.drain())
    t_y = sy.inc()
    sc.y(lambda e, s=sy.h: e.drain().then_inc(s, 1))
    t_g = sg.inc()
    sc.g(lambda e, s=sg.h: e.drain().then_inc(s, 1))
    targets = {"tensor": (st, t_t), "vector": (sv, t_v), "scalar": (ss, t_s),
               "sync": (sy, t_y), "gpsimd": (sg, t_g)}
    for e in targets:
        for e2, (sem, val) in targets.items():
            if e2 != e:
                sc.wait(e, sem, val)


def build_nc():
    nc = bass.Bass("TRN2", target_bir_lowering=False, debug=False)

    L = {}
    L["x_hbm"] = nc.dram_tensor("x", [T_FULL, C], F32, kind="ExternalInput").ap()
    L["wqkv"] = nc.dram_tensor("wqkv", [C, 3 * C], F32, kind="ExternalInput").ap()
    L["bqkv"] = nc.dram_tensor("bqkv", [3 * C], F32, kind="ExternalInput").ap()
    L["gam"] = nc.dram_tensor("gam", [C], F32, kind="ExternalInput").ap()
    L["bet"] = nc.dram_tensor("bet", [C], F32, kind="ExternalInput").ap()
    L["wout"] = nc.dram_tensor("wout", [C, C], F32, kind="ExternalInput").ap()
    L["bout"] = nc.dram_tensor("bout", [C], F32, kind="ExternalInput").ap()
    L["win"] = nc.dram_tensor("win", [C, 64], F32, kind="ExternalInput").ap()
    L["binf"] = nc.dram_tensor("binf", [64], F32, kind="ExternalInput").ap()
    L["rw1"] = nc.dram_tensor("rw1", [64, C], F32, kind="ExternalInput").ap()
    L["b1f"] = nc.dram_tensor("b1f", [C], F32, kind="ExternalInput").ap()
    L["w2b"] = nc.dram_tensor("w2b", [C, 64], F32, kind="ExternalInput").ap()
    L["b2"] = nc.dram_tensor("b2", [64], F32, kind="ExternalInput").ap()
    L["wfo"] = nc.dram_tensor("wfo", [64, C], F32, kind="ExternalInput").ap()
    L["bffo"] = nc.dram_tensor("bffo", [C], F32, kind="ExternalInput").ap()
    L["ident_h"] = nc.dram_tensor("ident", [128, 128], BF16, kind="ExternalInput").ap()
    L["identf_h"] = nc.dram_tensor("identf", [128, 128], F32, kind="ExternalInput").ap()
    L["onesb_h"] = nc.dram_tensor("onesb", [128, 128], BF16, kind="ExternalInput").ap()
    L["onesf_h"] = nc.dram_tensor("onesf", [1, 128], F32, kind="ExternalInput").ap()
    L["consts_h"] = nc.dram_tensor("consts", [128, 2], F32, kind="ExternalInput").ap()
    L["out_hbm"] = nc.dram_tensor("out", [T_OWN, C], F32, kind="ExternalOutput").ap()

    es = ExitStack()
    with es:
        def SB(name, shape, dt):
            return es.enter_context(nc.sbuf_tensor(name, shape, dt))

        def SEM(name):
            return es.enter_context(nc.semaphore(name))

        L.update(
            ident_sb=SB("ident_sb", [128, 128], BF16),
            identf_sb=SB("identf_sb", [128, 128], F32),
            onesb_sb=SB("onesb_sb", [128, 128], BF16),
            onesf_sb=SB("onesf_sb", [1, 128], F32),
            consts_sb=SB("consts_sb", [128, 2], F32),
            bqkvT_sb=SB("bqkvT_sb", [128, 24], F32),
            gamT_sb=SB("gamT_sb", [128, 8], F32),
            betT_sb=SB("betT_sb", [128, 8], F32),
            bin_sb=SB("bin_sb", [64, 1], F32),
            b1T_sb=SB("b1T_sb", [128, 8], F32),
            b2_sb=SB("b2_sb", [64, 1], F32),
            negB_sb=SB("negB_sb", [128, NPAIR], F32),
            attnT_sb=SB("attnT_sb", [128, NPAIR * T_OWN], BF16),
        )
        dc, dg = Sem(SEM("s_dc")), Sem(SEM("s_dg2"))
        dxp = [Sem(SEM("s_dx0")), Sem(SEM("s_dx1")),
               Sem(SEM("s_dx2")), Sem(SEM("s_dx3"))]
        dwp = [Sem(SEM("s_dw0")), Sem(SEM("s_dw1"))]
        dwo = Sem(SEM("s_dwo"))
        dop = [Sem(SEM("s_do0")), Sem(SEM("s_do1"))]
        st, sv, ss = Sem(SEM("s_t")), Sem(SEM("s_v")), Sem(SEM("s_s"))
        sy, sg = Sem(SEM("s_y")), Sem(SEM("s_g"))
        L.update(dc=dc, dg=dg, dxp=dxp, dwp=dwp, dwo=dwo, dop=dop,
                 st=st, sv=sv, ss=ss, sy=sy, sg=sg,
                 chain_done={"vector": 0, "scalar": 0})

        with ExitStack() as es12:
            def SB12(name, shape, dt):
                return es12.enter_context(nc.sbuf_tensor(name, shape, dt))

            L.update(
                kT_sb=SB12("kT_sb", [128, 2 * T_FULL], BF16),
                v_sb=SB12("v_sb", [128, 2 * T_FULL], BF16),
                qT_sb=SB12("qT_sb", [128, 2 * T_OWN], BF16),
                xhq_sb=SB12("xhq_sb", [128, NKC * T_FULL], BF16),
                xtile_sb=SB12("xtile_sb", [128, 3 * C], F32),
                xh_tile=SB12("xh_tile", [128, 2 * C], BF16),
                wq_sb=SB12("wq_sb", [128, 2 * 1024], BF16),
                wq3_sb=SB12("wq3_sb", [128, 3 * 1024], BF16),
                vstage_sb=SB12("vstage_sb", [128, 8 * 512], BF16),
                vtmp_sb=SB12("vtmp_sb", [128, 2 * 512], BF16),
                stats_sb=SB12("stats_sb", [128, 12], F32),
                mv_sb=SB12("mv_sb", [128, 8], F32),
                rstd_sb=SB12("rstd_sb", [128, 4], F32),
                nmax_sb=SB12("nmax_sb", [128, 8], F32),
                PT_sb=SB12("PT_sb", [128, 8 * 1024], BF16),
                zb_sb=SB12("zb_sb", [128, 512], F32),
            )
            L["sq_sb"] = L["xh_tile"]
            with ExitStack() as esA:
                L["score_ps"] = esA.enter_context(
                    nc.psum_tensor("score_ps", [128, 2048], F32))
                L["pv_ps"] = esA.enter_context(
                    nc.psum_tensor("pv_ps", [128, 512], F32))
                L["z_ps"] = esA.enter_context(
                    nc.psum_tensor("z_ps", [128, 512], F32))
                L["qkv_ps"] = esA.enter_context(
                    nc.psum_tensor("qkv_ps", [128, 512], F32))
                L["tp_ps"] = esA.enter_context(
                    nc.psum_tensor("tp_ps", [128, 1024], BF16))
                sc = Script(sv, ss)
                p0_loads(sc, L)
                L["plstate"] = {
                    "lqe": 0,
                    "last_wq_consumed": [0, 0],
                    "last_tp_evac": 0,
                    "last_tp_evac_v": 0,
                    "last_qkv_evac": 0,
                    "last_vtmp_read": [0, 0],
                    "v_negB": [0] * 8,
                    "kv_parity_free": [0, 0],
                    "pvz_free": 0,
                    "vstage_ev": [0] * 8,
                    "it_hist": [],
                }
                phase_ln(sc, L)
                pipeline(sc, L)
                barrier(sc, L)
                emit_block(nc, sc, L)

        with ExitStack() as es34:
            def SB34(name, shape, dt):
                return es34.enter_context(nc.sbuf_tensor(name, shape, dt))

            L.update(
                wout_sb=SB34("wout_sb", [128, 8 * C], BF16),
                boutrow_sb=SB34("boutrow_sb", [1, C], F32),
                bfforow_sb=SB34("bfforow_sb", [1, C], F32),
                win_sb=SB34("win_sb", [128, 8 * 64], BF16),
                rw1_sb=SB34("rw1_sb", [64, C], BF16),
                w2b_sb=SB34("w2b_sb", [128, 8 * 64], BF16),
                wfo_sb=SB34("wfo_sb", [64, C], BF16),
                boutbc_sb=SB34("boutbc_sb", [128, C], F32),
                bffobc_sb=SB34("bffobc_sb", [128, C], F32),
                x2_sb=SB34("x2_sb", [128, 8 * C], F32),
                x2bf_sb=SB34("x2bf_sb", [128, 4 * C], BF16),
                x2T_sb=SB34("x2T_sb", [128, 2 * 4096], BF16),
                xres_sb=SB34("xres_sb", [128, 4 * C], F32),
                sT_sb=SB34("sT_sb", [64, 2 * 512], BF16),
                hT_sb=SB34("hT_sb", [128, 2 * 4096], BF16),
                gT_sb=SB34("gT_sb", [64, 2 * 512], BF16),
                u_sb=SB34("u_sb", [128, 8 * C], F32),
                stats2_sb=SB34("stats2_sb", [128, 96], F32),
                mv2_sb=SB34("mv2_sb", [128, 16], F32),
                rstd2_sb=SB34("rstd2_sb", [128, 8], F32),
            )
            L["A_ps"] = es34.enter_context(
                nc.psum_tensor("A_ps", [128, 2048], F32))
            L["F_ps"] = es34.enter_context(
                nc.psum_tensor("F_ps", [128, 1024], F32))
            L["B_ps"] = es34.enter_context(
                nc.psum_tensor("B_ps", [128, 1024], BF16))
            L["C_ps"] = es34.enter_context(
                nc.psum_tensor("C_ps", [64, 512], F32))
            sc = Script(sv, ss)
            p34(sc, L)
            sc.wait("sync", dop[0], dop[0].n)
            sc.wait("sync", dop[1], dop[1].n)
            emit_block(nc, sc, L)

    return nc



def p0_loads(sc, L):
    dc, dg = L["dc"], L["dg"]

    def ld(dst, src, slow=False):
        sc.y(lambda e, d=dst, s=src, sl=slow: e.dma_start(
            out=d, in_=s, allow_slow_non_contiguous=sl).then_inc(dc.h, 16))
        dc.inc(16)

    def ldg(dst, src):
        sc.g(lambda e, d=dst, s=src: e.dma_start(
            out=d, in_=s).then_inc(dg.h, 16))
        dg.inc(16)

    ld(L["ident_sb"][:], L["ident_h"])
    ld(L["identf_sb"][:], L["identf_h"])
    ld(L["onesb_sb"][:], L["onesb_h"])
    ld(L["onesf_sb"][:], L["onesf_h"])
    ld(L["consts_sb"][:], L["consts_h"])
    ld(L["bqkvT_sb"][:], L["bqkv"].rearrange("(m p) -> p m", p=128), slow=True)
    ld(L["gamT_sb"][:], L["gam"].rearrange("(m p) -> p m", p=128), slow=True)
    ld(L["betT_sb"][:], L["bet"].rearrange("(m p) -> p m", p=128), slow=True)
    ld(L["bin_sb"][:], L["binf"][:, None], slow=True)
    ld(L["b1T_sb"][:], L["b1f"].rearrange("(m p) -> p m", p=128), slow=True)
    ld(L["b2_sb"][:], L["b2"][:, None], slow=True)
    for e in ("tensor", "vector", "scalar"):
        sc.wait(e, dc, dc.n)
        sc.wait(e, dg, dg.n)


def phase_ln(sc, L):
    dxp, dwo = L["dxp"], L["dwo"]
    st, sv, ss = L["st"], L["sv"], L["ss"]
    xtile_sb, xh_tile, stats_sb, mv_sb, rstd_sb = (
        L["xtile_sb"], L["xh_tile"], L["stats_sb"], L["mv_sb"], L["rstd_sb"])
    xhq_sb, wq3_sb = L["xhq_sb"], L["wq3_sb"]
    kT_sb, qT_sb = L["kT_sb"], L["qT_sb"]
    qkv_ps, score_ps = L["qkv_ps"], L["score_ps"]
    x_hbm, wqkv = L["x_hbm"], L["wqkv"]
    ident_sb, gamT_sb, betT_sb, bqkvT_sb = (
        L["ident_sb"], L["gamT_sb"], L["betT_sb"], L["bqkvT_sb"])
    eps_ap = L["consts_sb"][:, 0:1]
    state = L["plstate"]

    dws = dwo
    for j, m in enumerate((8, 16, 0)):
        srcw = wqkv[:, 128 * m:128 * m + 128].rearrange(
            "(kc p) c -> p kc c", p=128)
        sc.g(lambda e, d=wq3_sb[:, j * 1024:(j + 1) * 1024].rearrange(
             "p (kc c) -> p kc c", kc=8), s=srcw, hh=dws.h:
             e.dma_start(out=d, in_=s).then_inc(hh, 16))
        dws.inc(16)
    w3_ready = dws.n

    NB = 3
    dma_cnt = [0] * NB
    v_xh_t = [0] * 32
    s_rstd_t = [0] * 32
    s_evac_t = [None] * 32
    last_xh_read = [0, 0]
    last_score_evac = [0, 0]
    v_gb_slice = [0] * 8
    last_qkv_evac = [0]
    w3_waited = [False]

    def issue_dma(j, guard):
        if j > 31:
            return
        b = j % NB
        xt = xtile_sb[:, b * C:(b + 1) * C]
        if guard:
            sc.wait("sync", sv, guard)
        dxs = dxp[b]
        sc.y(lambda e, d=xt, s=x_hbm[j * 128:j * 128 + 128, :], hh=dxs.h:
             e.dma_start(out=d, in_=s).then_inc(hh, 16))
        dma_cnt[b] = dxs.inc(16)

    pend = []
    for n in range(8):
        pend.append((8, n))
        pend.append((16, n))
        if n < 4:
            pend.append((0, n))
    pi = [0]
    pending_evac = []

    def qkv_burst():
        if pi[0] >= len(pend):
            return False
        m, n = pend[pi[0]]
        if not v_gb_slice[n]:
            return False
        pi[0] += 1
        if not w3_waited[0]:
            sc.wait("tensor", dws, w3_ready)
            w3_waited[0] = True
        j = {8: 0, 16: 1, 0: 2}[m]
        wbuf = wq3_sb[:, j * 1024:(j + 1) * 1024]
        sc.wait("tensor", sv, v_gb_slice[n])
        sc.wait("tensor", ss, last_qkv_evac[0])
        for kc in range(8):
            last = kc == 7
            def mk(o, w, r, s0, s1, last=last):
                def f(e):
                    ins = e.matmul(o, w, r, start=s0, stop=s1)
                    if last:
                        ins.then_inc(st.h, 1)
                    return ins
                return f
            sc.t(mk(qkv_ps[:], wbuf[:, kc * 128:kc * 128 + 128],
                    xhq_sb[:, kc * T_FULL + n * 512:
                           kc * T_FULL + n * 512 + 512],
                    kc == 0, kc == 7))
        pending_evac.append((st.inc(), m, n))
        return True

    def qkv_evac():
        if not pending_evac:
            return False
        t_mm, m, n = pending_evac.pop(0)
        if m == 8:
            dst = kT_sb[:, n * 512:n * 512 + 512]
        elif m == 16:
            dst = L["vstage_sb"][:, n * 512:n * 512 + 512]
        else:
            dst = qT_sb[:, n * 512:n * 512 + 512]
        sc.wait("scalar", st, t_mm)
        s_ev = sc.s(lambda e, o=dst, i_=qkv_ps[:], b=bqkvT_sb[:, m:m + 1]:
                    e.activation(out=o, in_=i_, func=AF.Identity, bias=b))
        last_qkv_evac[0] = s_ev
        if m == 16:
            state["vstage_ev"][n] = s_ev
        return True

    issue_dma(0, 0)
    issue_dma(1, 0)
    for j in range(33):
        if j < 32:
            b = j % NB
            xt = xtile_sb[:, b * C:(b + 1) * C]
            sc.wait("vector", dxp[b], dma_cnt[b])
            sc.v(lambda e, o=stats_sb[:, 0:6], i_=xt[:, 0:512]:
                 e.bn_stats(o, i_))
            sc.v(lambda e, o=stats_sb[:, 6:12], i_=xt[:, 512:1024]:
                 e.bn_stats(o, i_))
            mvb = mv_sb[:, 2 * (j % 2):2 * (j % 2) + 2]
            v_stats = sc.v(lambda e, o=mvb, i_=stats_sb[:]: e.bn_aggr(o, i_))
            rst = rstd_sb[:, j % 2:j % 2 + 1]
            sc.wait("scalar", sv, v_stats)
            sc.s(lambda e, o=rst, i_=mvb[:, 1:2], b=eps_ap:
                 e.activation(out=o, in_=i_, func=AF.Ln, bias=b))
            s_rstd_t[j] = sc.s(lambda e, o=rst, i_=rst:
                               e.activation(out=o, in_=i_, func=AF.Exp,
                                            scale=-0.5))
        if j >= 1:
            i = j - 1
            qkv_evac()
            par = i % 2
            bi = i % NB
            xt_i = xtile_sb[:, bi * C:(bi + 1) * C]
            xht = xh_tile[:, par * C:(par + 1) * C]
            mvb = mv_sb[:, 2 * par:2 * par + 2]
            rst = rstd_sb[:, par:par + 1]
            sc.wait("vector", ss, s_rstd_t[i])
            sc.wait("vector", st, last_xh_read[par])
            v_xh = sc.v(lambda e, o=xht, i_=xt_i, m=mvb[:, 0:1], r=rst:
                        e.tensor_scalar(o, i_, m, r, op0=AluOpType.subtract,
                                        op1=AluOpType.mult))
            v_xh_t[i] = v_xh
            issue_dma(i + 2, v_xh_t[i - 1] if i >= 1 else 0)
            sps = score_ps[:, par * 1024:par * 1024 + 1024]
            sc.wait("tensor", sv, v_xh)
            sc.wait("tensor", ss, last_score_evac[par])
            for kc in range(8):
                last = kc == 7
                def mk(o, w, r, last=last):
                    def f(e):
                        ins = e.matmul(o, w, r, start=True, stop=True,
                                       skip_group_check=True)
                        if last:
                            ins.then_inc(st.h, 1)
                        return ins
                    return f
                sc.t(mk(sps[:, kc * 128:kc * 128 + 128],
                        xht[:, kc * 128:kc * 128 + 128], ident_sb[:]))
            t_tp = st.inc()
            last_xh_read[par] = t_tp
            sc.wait("scalar", st, t_tp)
            dst = xhq_sb[:].rearrange("p (kc t) -> p kc t", kc=8)[
                :, :, i * 128:i * 128 + 128]
            srcv = sps.rearrange("p (kc t) -> p kc t", kc=8)
            s_e = sc.s(lambda e, o=dst, i_=srcv: e.copy(out=o, in_=i_))
            s_evac_t[i] = s_e
            last_score_evac[par] = s_e
            if i % 4 == 3:
                n = i // 4
                sc.wait("vector", ss, s_e)
                v_gb = 0
                for kc in range(8):
                    slc = xhq_sb[:, kc * T_FULL + n * 512:
                                 kc * T_FULL + n * 512 + 512]
                    v_gb = sc.v(lambda e, o=slc, i_=slc,
                                ga=gamT_sb[:, kc:kc + 1],
                                be=betT_sb[:, kc:kc + 1]:
                                e.tensor_scalar(o, i_, ga, be,
                                                op0=AluOpType.mult,
                                                op1=AluOpType.add))
                v_gb_slice[n] = v_gb
            qkv_burst()
    while pi[0] < len(pend) or pending_evac:
        made = qkv_evac()
        made = qkv_burst() or made
        if not made:
            break
    L["v_gb"] = v_gb_slice[7]
    state["lqe_s"] = last_qkv_evac[0]
    state["last_qkv_evac"] = 0
    state["last_tp_evac"] = 0
    L["ln_tp_evac"] = 0


def pipeline(sc, L):
    st, sv, ss = L["st"], L["sv"], L["ss"]
    state = L["plstate"]
    sc.wait("vector", ss, state["lqe_s"])
    sc.wait("tensor", ss, state["lqe_s"])

    def full_gen(sg):
        yield from qkv_steps_gen(sc, L, sg, state)
        yield from bound_gen(sc, L, sg, state)

    ident_sb, tp_ps, v_sb = L["ident_sb"], L["tp_ps"], L["v_sb"]
    vstage = L["vstage_sb"]

    def vstep_gen():
        for n in range(8):
            sc.wait("tensor", ss, state["vstage_ev"][n])
            sc.wait("tensor", ss, state["last_tp_evac"])
            sc.wait("tensor", sv, state["last_tp_evac_v"])
            for blk in range(4):
                last = blk == 3
                def mk(o, i_, idn, last=last):
                    def f(e):
                        ins = e.transpose(o, i_, idn)
                        if last:
                            ins.then_inc(st.h, 1)
                        return ins
                    return f
                sc.t(mk(tp_ps[:, blk * 128:blk * 128 + 128],
                        vstage[:, n * 512 + blk * 128:
                               n * 512 + blk * 128 + 128], ident_sb[:]))
            t_vt = st.inc()
            sc.wait("vector", st, t_vt)
            state["last_tp_evac_v"] = sc.v(
                lambda e, o=v_sb[:, (n * 4) * 128:(n * 4 + 4) * 128],
                i_=tp_ps[:, 0:512]: e.tensor_copy(o, i_))
            yield n

    vsteps = vstep_gen()
    for _ in bound_gen(sc, L, 0, state):
        next(vsteps, None)
    for _ in vsteps:
        pass
    state["pvz_free"] = state["last_tp_evac_v"]
    for sg in range(8):
        gen = full_gen(sg + 1) if sg + 1 < 8 else None
        p2_subgroup(sc, L, sg, state, gen)
        if gen is not None:
            for _ in gen:
                pass
        state["kv_parity_free"][sg % 2] = st.n


def qkv_steps_gen(sc, L, sg, state):
    dwp = L["dwp"]
    st, sv, ss = L["st"], L["sv"], L["ss"]
    xhq_sb, wq_sb, vtmp_sb = L["xhq_sb"], L["wq_sb"], L["vtmp_sb"]
    kT_sb, v_sb, qT_sb = L["kT_sb"], L["v_sb"], L["qT_sb"]
    qkv_ps, tp_ps = L["qkv_ps"], L["tp_ps"]
    wqkv = L["wqkv"]
    ident_sb, bqkvT_sb = L["ident_sb"], L["bqkvT_sb"]
    par = sg % 2
    kbase = par * T_FULL
    qbase = par * T_OWN

    k_tiles = [8 + sg]
    v_tiles = [16 + sg]
    q_tiles = [sg]
    mlist = k_tiles + v_tiles + q_tiles
    wq_ready = {}

    def issue_wload(j):
        m = mlist[j]
        wbuf = wq_sb[:, (j % 2) * 1024:(j % 2) * 1024 + 1024]
        sc.wait("gpsimd", st, state["last_wq_consumed"][j % 2])
        srcw = wqkv[:, 128 * m:128 * m + 128].rearrange(
            "(kc p) c -> p kc c", p=128)
        dws = dwp[j % 2]
        sc.g(lambda e, d=wbuf.rearrange("p (kc c) -> p kc c", kc=8), s=srcw,
             hh=dws.h: e.dma_start(out=d, in_=s).then_inc(hh, 16))
        wq_ready[j] = (dws, dws.inc(16))

    issue_wload(0)
    issue_wload(1)
    first = True
    for j, m in enumerate(mlist):
        wbuf = wq_sb[:, (j % 2) * 1024:(j % 2) * 1024 + 1024]
        nrange = 4 if m in q_tiles else 8
        for n in range(nrange):
            if m in k_tiles:
                dst = kT_sb[:, kbase + n * 512:kbase + n * 512 + 512]
            elif m in v_tiles:
                dst = vtmp_sb[:, (n % 2) * 512:(n % 2) * 512 + 512]
            else:
                dst = qT_sb[:, qbase + n * 512:qbase + n * 512 + 512]
            sc.wait("tensor", wq_ready[j][0], wq_ready[j][1])
            if first:
                sc.wait("tensor", sv, L["v_gb"])
                if sg >= 2:
                    sc.wait("vector", st, state["kv_parity_free"][par])
                first = False
            sc.wait("tensor", sv, state["last_qkv_evac"])
            for kc in range(8):
                if kc == 4:
                    yield (m, n, "mid")
                last = kc == 7
                def mk(o, w, r, s0, s1, last=last):
                    def f(e):
                        ins = e.matmul(o, w, r, start=s0, stop=s1)
                        if last:
                            ins.then_inc(st.h, 1)
                        return ins
                    return f
                sc.t(mk(qkv_ps[:], wbuf[:, kc * 128:kc * 128 + 128],
                        xhq_sb[:, kc * T_FULL + n * 512:
                               kc * T_FULL + n * 512 + 512],
                        kc == 0, kc == 7))
            t_mm = st.inc()
            yield (m, n, 0)
            sc.wait("vector", st, t_mm)
            if m in v_tiles:
                sc.wait("vector", st, state["last_vtmp_read"][n % 2])
            v_ev = sc.v(lambda e, o=dst, i_=qkv_ps[:], b=bqkvT_sb[:, m:m + 1]:
                        e.tensor_scalar(o, i_, b, None, op0=AluOpType.add,
                                        op1=AluOpType.bypass))
            state["last_qkv_evac"] = v_ev
            if n == nrange - 1:
                state["last_wq_consumed"][j % 2] = st.n
                if j + 2 < len(mlist):
                    issue_wload(j + 2)
            if m in v_tiles:
                sc.wait("tensor", sv, v_ev)
                sc.wait("tensor", ss, state["last_tp_evac"])
                sc.wait("tensor", sv, state["last_tp_evac_v"])
                for blk in range(4):
                    last = blk == 3
                    def mk(o, i_, idn, last=last):
                        def f(e):
                            ins = e.transpose(o, i_, idn)
                            if last:
                                ins.then_inc(st.h, 1)
                            return ins
                        return f
                    sc.t(mk(tp_ps[:, blk * 128:blk * 128 + 128],
                            dst[:, blk * 128:blk * 128 + 128],
                            ident_sb[:]))
                t_vt = st.inc()
                state["last_vtmp_read"][n % 2] = t_vt
                vdst = v_sb[:, kbase + (n * 4) * 128:
                            kbase + (n * 4 + 4) * 128]
                sc.wait("vector", st, t_vt)
                state["last_tp_evac_v"] = sc.v(
                    lambda e, o=vdst, i_=tp_ps[:, 0:512]:
                    e.tensor_copy(o, i_))
            yield (m, n, 4)


def bound_gen(sc, L, sg, state):
    st, sv, ss = L["st"], L["sv"], L["ss"]
    kT_sb, qT_sb, sq_sb, nmax_sb, negB_sb = (
        L["kT_sb"], L["qT_sb"], L["sq_sb"], L["nmax_sb"], L["negB_sb"])
    onesb_sb = L["onesb_sb"]
    qkv_ps = L["qkv_ps"]
    mln8_ap = L["consts_sb"][:, 1:2]
    par = sg % 2
    kbase = par * T_FULL
    qbase = par * T_OWN
    last_sq_read = state.setdefault("bound_sq_read", [0])

    v_nb = 0
    for pl in range(1):
        p = sg
        srcs = [(qT_sb[:, qbase:qbase + T_OWN], 0),
                (kT_sb[:, kbase:kbase + 2048], 1),
                (kT_sb[:, kbase + 2048:kbase + 4096], 2)]
        for (srcq, ci) in srcs:
            sc.wait("vector", st, last_sq_read[0])
            v_sq = sc.v(lambda e, o=sq_sb[:], a=srcq:
                        e.tensor_tensor(o, a, a, op=AluOpType.mult))
            sc.wait("tensor", sv, v_sq)
            sc.wait("tensor", sv, state["last_qkv_evac"])
            acc = nmax_sb[:, ci + 4:ci + 5]
            tgt = nmax_sb[:, ci:ci + 1]
            for n4 in range(4):
                def mk(o, w, r):
                    def f(e):
                        ins = e.matmul(o, w, r, start=True, stop=True)
                        ins.then_inc(st.h, 1)
                        return ins
                    return f
                sc.t(mk(qkv_ps[:], onesb_sb[:],
                        sq_sb[:, n4 * 512:n4 * 512 + 512]))
                t_nm = st.inc()
                last_sq_read[0] = t_nm
                sc.wait("vector", st, t_nm)
                if n4 == 0:
                    v_r = sc.v(lambda e, o=tgt, i_=qkv_ps[:]:
                               e.tensor_reduce(o, i_, axis=AX.X,
                                               op=AluOpType.max))
                else:
                    sc.v(lambda e, o=acc, i_=qkv_ps[:]:
                         e.tensor_reduce(o, i_, axis=AX.X, op=AluOpType.max))
                    v_r = sc.v(lambda e, o=tgt, a=tgt, b=acc:
                               e.tensor_tensor(o, a, b, op=AluOpType.max))
                sc.wait("tensor", sv, v_r)
                yield ("bound", p, ci, n4)
        v_k = sc.v(lambda e, o=nmax_sb[:, 3:4], a=nmax_sb[:, 1:2],
                   b=nmax_sb[:, 2:3]:
                   e.tensor_tensor(o, a, b, op=AluOpType.max))
        sc.wait("scalar", sv, v_k)
        sc.s(lambda e, o=nmax_sb[:, 4:5], i_=nmax_sb[:, 0:1]:
             e.activation(out=o, in_=i_, func=AF.Ln, bias=0.0))
        s_ln = sc.s(lambda e, o=nmax_sb[:, 5:6], i_=nmax_sb[:, 3:4]:
                    e.activation(out=o, in_=i_, func=AF.Ln, bias=0.0))
        sc.wait("vector", ss, s_ln)
        v_add = sc.v(lambda e, o=nmax_sb[:, 6:7], a=nmax_sb[:, 4:5],
                     b=nmax_sb[:, 5:6]:
                     e.tensor_tensor(o, a, b, op=AluOpType.add))
        sc.wait("scalar", sv, v_add)
        s_B = sc.s(lambda e, o=nmax_sb[:, 7:8], i_=nmax_sb[:, 6:7], b=mln8_ap:
                   e.activation(out=o, in_=i_, func=AF.Exp, scale=0.5,
                                bias=b))
        sc.wait("vector", ss, s_B)
        v_nb = sc.v(lambda e, o=negB_sb[:, p:p + 1], i_=nmax_sb[:, 7:8]:
                    e.tensor_scalar(o, i_, -1.0, -1.0, op0=AluOpType.mult,
                                    op1=AluOpType.add))
        yield ("negB", p, 0)
    state["v_negB"][sg] = v_nb


def p2_subgroup(sc, L, sg, state, gen):
    st, sv, ss = L["st"], L["sv"], L["ss"]
    kT_sb, v_sb, qT_sb, PT_sb, zb_sb, attnT_sb = (
        L["kT_sb"], L["v_sb"], L["qT_sb"], L["PT_sb"], L["zb_sb"],
        L["attnT_sb"])
    onesb_sb, negB_sb = L["onesb_sb"], L["negB_sb"]
    score_ps, pv_ps, z_ps = L["score_ps"], L["pv_ps"], L["z_ps"]
    par = sg % 2
    kbase = par * T_FULL
    qbase = par * T_OWN
    it_hist = state["it_hist"]
    p = sg

    def emit_scores(qt, kt, guard):
        spar = kt % 2
        sps = score_ps[:, spar * 1024:spar * 1024 + 1024]
        kA = kT_sb[0:64, kbase + kt * 128:kbase + kt * 128 + 128]
        kB = kT_sb[64:128, kbase + kt * 128:kbase + kt * 128 + 128]
        qA = qT_sb[0:64, qbase + qt * 512:qbase + qt * 512 + 512]
        qB = qT_sb[64:128, qbase + qt * 512:qbase + qt * 512 + 512]
        if guard is not None:
            sc.wait("tensor", ss, guard)
        sc.t(lambda e, o=sps[:, 0:512], w=kA, r=qA:
             e.matmul(o, w, r, start=True, stop=True))
        sc.t(lambda e, o=sps[:, 512:1024], w=kB, r=qB:
             e.matmul(o, w, r, start=True, stop=True).then_inc(st.h, 1))
        return st.inc()

    def emit_pvz(qt, kt, it, exp_done_ref=None):
        spar = kt % 8
        pt = PT_sb[:, spar * 1024:spar * 1024 + 1024]
        vch = v_sb[:, kbase + kt * 128:kbase + kt * 128 + 128]
        sc.wait("tensor", ss, exp_done_ref[kt])
        if kt == 0 and it >= 1:
            sc.wait("tensor", sv, it_hist[it - 1]["norm_v"])
        elif kt == 0:
            sc.wait("tensor", sv, state["pvz_free"])
        s0, s1 = (kt == 0), (kt == NKT - 1)
        sc.t(lambda e, o=pv_ps[0:64, :], w=vch[:, 0:64], r=pt[:, 0:512],
             a=s0, b=s1: e.matmul(o, w, r, start=a, stop=b,
                                  skip_group_check=True))
        sc.t(lambda e, o=pv_ps[64:128, :], w=vch[:, 64:128],
             r=pt[:, 512:1024], a=s0, b=s1:
             e.matmul(o, w, r, start=a, stop=b, skip_group_check=True))
        sc.t(lambda e, o=z_ps[0:64, :], w=onesb_sb[:, 0:64],
             r=pt[:, 0:512], a=s0, b=s1:
             e.matmul(o, w, r, start=a, stop=b, skip_group_check=True))
        sc.t(lambda e, o=z_ps[64:128, :], w=onesb_sb[:, 0:64],
             r=pt[:, 512:1024], a=s0, b=s1:
             e.matmul(o, w, r, start=a, stop=b,
                      skip_group_check=True).then_inc(st.h, 1))
        return st.inc()

    def finish_qt(h, pqt, hit):
        sc.wait("tensor", ss, h["exp_done"][NKT - 1])
        for j in (NKT - 2, NKT - 1):
            h["pvz_done"][j] = emit_pvz(pqt, j, hit, h["exp_done"])
        if gen is not None:
            next(gen, None)
            next(gen, None)
        zbh = zb_sb[:]
        sc.wait("vector", st, h["pvz_done"][NKT - 1])
        sc.v(lambda e, o=zbh, i_=z_ps[:]: e.reciprocal(o, i_))
        dst = attnT_sb[:, p * T_OWN + pqt * 512:p * T_OWN + pqt * 512 + 512]
        h["norm_v"] = sc.v(lambda e, o=dst, a=pv_ps[:], b=zbh:
                           e.tensor_tensor(o, a, b, op=AluOpType.mult))

    first_of_sg = True
    for qt in range(NQT):
        it = len(it_hist)
        exp_done = []
        sc_done = []
        pvz_done = {}
        gprev = it_hist[it - 1]["exp_done"][NKT - 2] if it >= 1 else None
        sc_done.append(emit_scores(qt, 0, gprev))
        if first_of_sg:
            sc.wait("scalar", sv, state["v_negB"][sg])
            first_of_sg = False
        if qt >= 1:
            finish_qt(it_hist[it - 1], qt - 1, it - 1)
        it_hist.append(dict(exp_done=exp_done, pvz_done=pvz_done, norm_v=0))
        for kt in range(NKT):
            if kt + 1 < NKT:
                if kt + 1 >= 2:
                    gg = exp_done[kt - 1]
                elif it >= 1:
                    gg = it_hist[it - 1]["exp_done"][NKT - 1]
                else:
                    gg = None
                sc_done.append(emit_scores(qt, kt + 1, gg))
            else:
                sc.wait("tensor", ss, exp_done[kt - 1])
            if kt % 2 == 1 and kt >= 3:
                pvz_done[kt - 3] = emit_pvz(qt, kt - 3, it, exp_done)
                pvz_done[kt - 2] = emit_pvz(qt, kt - 2, it, exp_done)
            spar = kt % 8
            pt = PT_sb[:, spar * 1024:spar * 1024 + 1024]
            g1 = sc_done[kt]
            if kt >= 8:
                g1 = max(g1, pvz_done[kt - 8])
            sc.wait("scalar", st, g1)
            if kt < 8 and it >= 1:
                sc.wait("scalar", st,
                        it_hist[it - 1]["pvz_done"][NKT - 8 + kt])
            s_exp = sc.s(lambda e, o=pt,
                         i_=score_ps[:, (kt % 2) * 1024:(kt % 2) * 1024 + 1024],
                         b=negB_sb[:, p:p + 1]:
                         e.activation(out=o, in_=i_, func=AF.Exp, bias=b,
                                      scale=0.125))
            exp_done.append(s_exp)
            if gen is not None and kt % 2 == 0:
                next(gen, None)
    finish_qt(it_hist[-1], NQT - 1, len(it_hist) - 1)


def p34(sc, L):
    dxp, dwo, dop = L["dxp"], L["dwo"], L["dop"]
    st, sv, ss = L["st"], L["sv"], L["ss"]
    attnT_sb, wout_sb = L["attnT_sb"], L["wout_sb"]
    x2_sb, x2bf_sb, x2T_sb, xres_sb = (L["x2_sb"], L["x2bf_sb"], L["x2T_sb"],
                                       L["xres_sb"])
    boutbc_sb, bffobc_sb = L["boutbc_sb"], L["bffobc_sb"]
    sT_sb, hT_sb, gT_sb, u_sb = L["sT_sb"], L["hT_sb"], L["gT_sb"], L["u_sb"]
    stats2_sb, mv2_sb, rstd2_sb = L["stats2_sb"], L["mv2_sb"], L["rstd2_sb"]
    A_ps, B_ps, C_ps, F_ps = L["A_ps"], L["B_ps"], L["C_ps"], L["F_ps"]
    onesf_sb, boutrow_sb, bfforow_sb = (L["onesf_sb"], L["boutrow_sb"],
                                        L["bfforow_sb"])
    win_sb, rw1_sb, w2b_sb, wfo_sb = (L["win_sb"], L["rw1_sb"], L["w2b_sb"],
                                      L["wfo_sb"])
    bin_sb, b1T_sb, b2_sb = L["bin_sb"], L["b1T_sb"], L["b2_sb"]
    ident_sb = L["ident_sb"]
    x_hbm, out_hbm = L["x_hbm"], L["out_hbm"]
    eps_ap = L["consts_sb"][:, 0:1]

    totals = []
    for dstw, srcw in (
        (boutrow_sb[:], L["bout"][None, :]),
        (bfforow_sb[:], L["bffo"][None, :]),
        (wout_sb[:].rearrange("p (kc m) -> p kc m", kc=8),
         L["wout"].rearrange("(kc p) m -> p kc m", p=128)),
        (win_sb[:].rearrange("p (kc m) -> p kc m", kc=8),
         L["win"].rearrange("(kc p) m -> p kc m", p=128)),
        (rw1_sb[:], L["rw1"]),
        (w2b_sb[:].rearrange("p (kc m) -> p kc m", kc=8),
         L["w2b"].rearrange("(kc p) m -> p kc m", p=128)),
        (wfo_sb[:], L["wfo"]),
    ):
        sc.wait("gpsimd", dwo, dwo.n)
        sc.g(lambda e, d=dstw, s=srcw, hh=dwo.h:
             e.dma_start(out=d, in_=s).then_inc(hh, 16))
        dwo.inc(16)
        totals.append(dwo.n)
    rows_t, wout_t, w_all = totals[1], totals[2], totals[-1]
    L["p4_w_all"] = [w_all, False]
    sc.wait("tensor", dwo, rows_t)

    v_bcp = 0
    for (bcdst, brow) in ((boutbc_sb, boutrow_sb), (bffobc_sb, bfforow_sb)):
        sc.wait("tensor", sv, v_bcp)
        for nh in range(2):
            last = nh == 1
            def mk(o, w, r, last=last):
                def f(e):
                    ins = e.matmul(o, w, r, start=True, stop=True)
                    if last:
                        ins.then_inc(st.h, 1)
                    return ins
                return f
            sc.t(mk(A_ps[:, nh * 512:nh * 512 + 512], onesf_sb[:],
                    brow[:, nh * 512:nh * 512 + 512]))
        t_bc = st.inc()
        sc.wait("vector", st, t_bc)
        v_bcp = sc.v(lambda e, o=bcdst[:], i_=A_ps[:, 0:1024]:
                     e.tensor_copy(o, i_))
    sc.wait("tensor", sv, v_bcp)
    sc.wait("tensor", dwo, wout_t)

    def batched_ln(uslot0, outs, outs2, slot):
        stt = stats2_sb[:, slot * 48:slot * 48 + 48]
        mv = mv2_sb[:, slot * 8:slot * 8 + 8]
        rst = rstd2_sb[:, slot * 4:slot * 4 + 4]
        for t in range(4):
            u_ap = u_sb[:, (uslot0 + t) * C:(uslot0 + t + 1) * C]
            sc.v(lambda e, o=stt[:, t * 12:t * 12 + 6], i_=u_ap[:, 0:512]:
                 e.bn_stats(o, i_))
            sc.v(lambda e, o=stt[:, t * 12 + 6:t * 12 + 12],
                 i_=u_ap[:, 512:1024]: e.bn_stats(o, i_))
        v_st = 0
        for t in range(4):
            v_st = sc.v(lambda e, o=mv[:, t * 2:t * 2 + 2],
                        i_=stt[:, t * 12:t * 12 + 12]: e.bn_aggr(o, i_))
        sc.wait("scalar", sv, v_st)
        var_ap = mv[:].rearrange("p (t two) -> p two t", two=2)[:, 1, :]
        sc.s(lambda e, o=rst, i_=var_ap, b=eps_ap:
             e.activation(out=o, in_=i_, func=AF.Ln, bias=b))
        s_r = sc.s(lambda e, o=rst, i_=rst:
                   e.activation(out=o, in_=i_, func=AF.Exp, scale=-0.5))
        sc.wait("vector", ss, s_r)
        rets = []
        for t in range(4):
            u_ap = u_sb[:, (uslot0 + t) * C:(uslot0 + t + 1) * C]
            v_ap = sc.v(lambda e, o=outs[t], i_=u_ap, m=mv[:, t * 2:t * 2 + 1],
                        r=rst[:, t:t + 1]:
                        e.tensor_scalar(o, i_, m, r, op0=AluOpType.subtract,
                                        op1=AluOpType.mult))
            if outs2[t] is not None:
                sc.wait("scalar", sv, v_ap)
                s_c = sc.s(lambda e, o=outs2[t], i_=outs[t]:
                           e.copy(out=o, in_=i_))
                rets.append(("s", s_c))
            else:
                rets.append(("v", v_ap))
        return rets

    a_guard_v = [0, 0]
    b_guard = [0, 0]
    o_stored = [0, 0, 0, 0]
    last_xrb = [0, 0, 0, 0]
    t_sT_read = [0, 0]
    t_h_read = [0, 0]
    t_g_read = [0, 0]
    s_binev = [0, 0]
    s_gev = [0, 0]
    f_guard_v = [0]
    f_guard_s = [0]

    x2T_done = [0, 0]
    x2cast_st = [[0] * 4, [0] * 4]

    def p3a_stage(sl):
            spar = sl % 2
            t_ops = []
            for tt in range(4):
                half = tt % 2
                row0 = sl * 512 + tt * 128
                ap = A_ps[:, half * 1024:half * 1024 + 1024]
                xr = xres_sb[:, tt * C:(tt + 1) * C]
                sc.wait("sync", sv, last_xrb[tt])
                dxs = dxp[tt]
                sc.y(lambda e, d=xr, s=x_hbm[row0:row0 + 128, :], hh=dxs.h:
                     e.dma_start(out=d, in_=s).then_inc(hh, 16))
                v_dx = (dxs, dxs.inc(16))
                xrb = xr
                sc.wait("vector", v_dx[0], v_dx[1])
                v_xrb = sc.v(lambda e, o=xrb, i0=xr, b=boutbc_sb[:]:
                             e.scalar_tensor_tensor(o, i0, 2.0, b,
                                                    op0=AluOpType.mult,
                                                    op1=AluOpType.add))
                last_xrb[tt] = v_xrb
                sc.wait("tensor", sv, a_guard_v[half])
                for nh in range(2):
                    for kc in range(8):
                        last = (nh == 1 and kc == 7)
                        def mk(o, w, r, s0, s1, last=last):
                            def f(e):
                                ins = e.matmul(o, w, r, start=s0, stop=s1)
                                if last:
                                    ins.then_inc(st.h, 1)
                                return ins
                            return f
                        sc.t(mk(ap[:, nh * 512:nh * 512 + 512],
                                attnT_sb[:, kc * T_OWN + row0:
                                         kc * T_OWN + row0 + 128],
                                wout_sb[:, kc * 1024 + nh * 512:
                                        kc * 1024 + nh * 512 + 512],
                                kc == 0, kc == 7))
                t_op = st.inc()
                u = u_sb[:, tt * C:(tt + 1) * C]
                sc.wait("vector", st, t_op)
                a_guard_v[half] = sc.v(lambda e, o=u, a=ap, b=xrb:
                                       e.tensor_tensor(o, a, b,
                                                       op=AluOpType.add))
                last_xrb[tt] = a_guard_v[half]

    def p3bc_stage(sl):
            spar = sl % 2
            outs = [x2_sb[:, spar * 4 * C + t * C:spar * 4 * C + (t + 1) * C]
                    for t in range(4)]
            outs2 = [x2bf_sb[:, t * C:(t + 1) * C] for t in range(4)]
            ln_rets = batched_ln(0, outs, outs2, 0)
            x2cast_st[spar] = [c for (k, c) in ln_rets]
            for tt in range(4):
                tph = B_ps[:]
                kind, cnt = ln_rets[tt]
                sc.wait("tensor", ss if kind == "s" else sv, cnt)
                sc.wait("tensor", ss, b_guard[0])
                for kc in range(8):
                    last = kc == 7
                    def mk(o, i_, idn, last=last):
                        def f(e):
                            ins = e.transpose(o, i_, idn)
                            if last:
                                ins.then_inc(st.h, 1)
                            return ins
                        return f
                    sc.t(mk(tph[:, kc * 128:kc * 128 + 128],
                            x2bf_sb[:, tt * C + kc * 128:tt * C + kc * 128 + 128],
                            ident_sb[:]))
                t_tp = st.inc()
                sc.wait("scalar", st, t_tp)
                sc.wait("scalar", st, t_sT_read[spar])
                dst = x2T_sb[:].rearrange("p (sp kc t) -> p sp kc t",
                                          sp=2, kc=8)[
                    :, spar, :, tt * 128:tt * 128 + 128]
                srcv = tph.rearrange("p (kc t) -> p kc t", kc=8)
                b_guard[0] = sc.s(lambda e, o=dst, i_=srcv:
                                  e.copy(out=o, in_=i_))
            x2T_done[sl % 2] = max(b_guard)


    def p4ab_stage(sl):
            if not L["p4_w_all"][1]:
                L["p4_w_all"][1] = True
                sc.wait("tensor", dwo, L["p4_w_all"][0])
            spar = sl % 2
            x2Tsl = x2T_sb[:, spar * 4096:spar * 4096 + 4096]
            sTsl = sT_sb[:, spar * 512:spar * 512 + 512]
            hTsl = hT_sb[:, spar * 4096:spar * 4096 + 4096]
            gTsl = gT_sb[:, spar * 512:spar * 512 + 512]
            sc.wait("tensor", ss, x2T_done[spar])
            sc.wait("tensor", ss, max(s_gev))
            sc.wait("tensor", ss, s_binev[spar])
            for kc in range(8):
                last = kc == 7
                def mk(o, w, r, s0, s1, last=last):
                    def f(e):
                        ins = e.matmul(o, w, r, start=s0, stop=s1)
                        if last:
                            ins.then_inc(st.h, 1)
                        return ins
                    return f
                sc.t(mk(C_ps[0:64, :], win_sb[:, kc * 64:kc * 64 + 64],
                        x2Tsl[:, kc * 512:kc * 512 + 512], kc == 0, kc == 7))
            t_sT = st.inc()
            t_sT_read[spar] = t_sT
            sc.wait("scalar", st, t_sT)
            sc.wait("scalar", st, t_h_read[spar])
            s_sT = sc.s(lambda e, o=sTsl, i_=C_ps[0:64, :], b=bin_sb[:]:
                        e.activation(out=o, in_=i_, func=AF.Identity, bias=b))
            s_binev[spar] = s_sT
            sc.wait("tensor", ss, s_sT)
            h_evac = [0, 0]
            for j in range(8):
                half = j % 2
                hp = F_ps[:, half * 512:half * 512 + 512]
                sc.wait("tensor", ss, h_evac[half])
                if j < 2:
                    sc.wait("tensor", sv, f_guard_v[0])
                    sc.wait("tensor", ss, f_guard_s[0])
                sc.t(lambda e, o=hp, w=rw1_sb[:, j * 128:j * 128 + 128],
                     r=sTsl:
                     e.matmul(o, w, r, start=True, stop=True,
                              skip_group_check=True).then_inc(st.h, 1))
                t_h = st.inc()
                t_h_read[spar] = t_h
                sc.wait("scalar", st, t_h)
                if j < 2:
                    sc.wait("scalar", st, t_g_read[spar])
                hev = sc.s(lambda e, o=hTsl[:, j * 512:j * 512 + 512], i_=hp,
                           b=b1T_sb[:, j:j + 1]:
                           e.activation(out=o, in_=i_, func=AF.Relu, bias=b))
                h_evac[half] = hev
                f_guard_s[0] = hev
            sc.wait("tensor", ss, h_evac[0])
            sc.wait("tensor", ss, h_evac[1])
            for j in range(8):
                last = j == 7
                def mk(o, w, r, s0, s1, last=last):
                    def f(e):
                        ins = e.matmul(o, w, r, start=s0, stop=s1)
                        if last:
                            ins.then_inc(st.h, 1)
                        return ins
                    return f
                sc.t(mk(C_ps[0:64, :], w2b_sb[:, j * 64:j * 64 + 64],
                        hTsl[:, j * 512:j * 512 + 512], j == 0, j == 7))
            t_g = st.inc()
            sc.wait("scalar", st, t_g)
            s_g = sc.s(lambda e, o=gTsl, i_=C_ps[0:64, :], b=b2_sb[:]:
                       e.activation(out=o, in_=i_, func=AF.Identity, bias=b))
            s_gev[spar] = s_g
            sc.wait("tensor", ss, s_g)
            for tt in range(4):
                row0 = sl * 512 + tt * 128
                fpA = F_ps[:]
                x2t = x2_sb[:, spar * 4 * C + tt * C:spar * 4 * C + (tt + 1) * C]
                x2pb = x2t
                sc.wait("vector", ss, x2cast_st[spar][tt])
                sc.v(lambda e, o=x2pb, a=x2t, b=bffobc_sb[:]:
                     e.tensor_tensor(o, a, b, op=AluOpType.add))
                sc.wait("tensor", sv, f_guard_v[0])
                sc.wait("tensor", ss, f_guard_s[0])
                for nh in range(2):
                    last = nh == 1
                    def mk(o, w, r, last=last):
                        def f(e):
                            ins = e.matmul(o, w, r, start=True, stop=True)
                            if last:
                                ins.then_inc(st.h, 1)
                            return ins
                        return f
                    sc.t(mk(fpA[:, nh * 512:nh * 512 + 512],
                            gTsl[:, tt * 128:tt * 128 + 128],
                            wfo_sb[:, nh * 512:nh * 512 + 512]))
                t_f = st.inc()
                t_g_read[spar] = t_f
                u = u_sb[:, (4 + tt) * C:(5 + tt) * C]
                sc.wait("vector", st, t_f)
                if o_stored[tt]:
                    sc.wait("vector", dop[tt % 2], dop[tt % 2].n)
                f_guard_v[0] = sc.v(lambda e, o=u, a=fpA, b=x2pb:
                                    e.tensor_tensor(o, a, b, op=AluOpType.add))

    def p4c_stage(sl):
            spar = sl % 2
            outs = [u_sb[:, (4 + t) * C:(5 + t) * C] for t in range(4)]
            v_out = batched_ln(4, outs, [None] * 4, 1)
            for tt in range(4):
                row0 = sl * 512 + tt * 128
                u = u_sb[:, (4 + tt) * C:(5 + tt) * C]
                sc.wait("sync", sv, v_out[tt][1])
                dos = dop[tt % 2]
                sc.y(lambda e, d=out_hbm[row0:row0 + 128, :], s=u, hh=dos.h:
                     e.dma_start(out=d, in_=s).then_inc(hh, 16))
                o_stored[tt] = dos.inc(16)

    for sl in range(5):
        if sl < 4:
            p3a_stage(sl)
        if sl >= 1:
            p4ab_stage(sl - 1)
        if sl < 4:
            p3bc_stage(sl)
        if sl >= 1:
            p4c_stage(sl - 1)

_NC_CACHE = {}


def kernel(**inputs):
    x = np.ascontiguousarray(np.asarray(inputs["L0_attn_x"], dtype=np.float32))
    gam = np.asarray(inputs["L0_attn_ln_gamma"], dtype=np.float32)
    bet = np.asarray(inputs["L0_attn_ln_beta"], dtype=np.float32)
    wqkv = np.ascontiguousarray(np.asarray(inputs["L0_attn_w_qkv"], dtype=np.float32))
    bqkv = np.asarray(inputs["L0_attn_b_qkv"], dtype=np.float32)
    wout = np.ascontiguousarray(np.asarray(inputs["L0_attn_w_out"], dtype=np.float32))
    bout = np.asarray(inputs["L0_attn_b_out"], dtype=np.float32)
    win = np.ascontiguousarray(np.asarray(inputs["L0_ffn_Win"], dtype=np.float32))
    binf = np.asarray(inputs["L0_ffn_bin_"], dtype=np.float32)
    w1 = np.asarray(inputs["L0_ffn_W1"], dtype=np.float32)
    b1 = np.asarray(inputs["L0_ffn_b1"], dtype=np.float32)
    w2 = np.asarray(inputs["L0_ffn_W2"], dtype=np.float32)
    b2 = np.asarray(inputs["L0_ffn_b2"], dtype=np.float32)
    wfo = np.ascontiguousarray(np.asarray(inputs["L0_ffn_Wout"], dtype=np.float32))
    bffo = np.asarray(inputs["L0_ffn_bout"], dtype=np.float32)

    M, INNER = 64, 16
    rw1 = np.zeros((M, M * INNER), np.float32)
    for m in range(M):
        rw1[m, m * INNER:(m + 1) * INNER] = w1[m]
    w2b = np.zeros((M * INNER, M), np.float32)
    for m in range(M):
        w2b[m * INNER:(m + 1) * INNER, m] = w2[m]
    b1flat = np.ascontiguousarray(b1.reshape(M * INNER))

    ident = np.eye(128, dtype=np.float32).astype(ml_dtypes.bfloat16)
    identf = np.eye(128, dtype=np.float32)
    onesb = np.ones((128, 128), np.float32).astype(ml_dtypes.bfloat16)
    onesf = np.ones((1, 128), np.float32)
    consts = np.zeros((128, 2), np.float32)
    consts[:, 0] = EPS
    consts[:, 1] = MINUS_LN8

    shared = dict(wqkv=wqkv, bqkv=bqkv, gam=gam, bet=bet, wout=wout, bout=bout,
                  win=win, binf=binf, rw1=rw1, b1f=b1flat, w2b=w2b, b2=b2,
                  wfo=wfo, bffo=bffo, ident=ident, identf=identf, onesb=onesb, onesf=onesf,
                  consts=consts)

    in_maps = []
    for c in range(8):
        b, h = c // 2, c % 2
        own = x[b, h * T_OWN:(h + 1) * T_OWN]
        other = x[b, (1 - h) * T_OWN:(2 - h) * T_OWN]
        xp = np.ascontiguousarray(np.concatenate([own, other], axis=0))
        m = dict(shared)
        m["x"] = xp
        in_maps.append(m)

    if "nc" not in _NC_CACHE:
        _NC_CACHE["nc"] = build_nc()
    nc = _NC_CACHE["nc"]
    res = run_bass_kernel_spmd(nc, in_maps, core_ids=list(range(8)))

    out = np.empty((4, T_FULL, C), np.float32)
    for c in range(8):
        b, h = c // 2, c % 2
        out[b, h * T_OWN:(h + 1) * T_OWN] = res.results[c]["out"]
    return out


# version 32 (final)
# speedup vs baseline: 1.0724x; 1.0081x over previous
from contextlib import ExitStack

import numpy as np
import ml_dtypes

import concourse.bass as bass
import concourse.mybir as mybir
from concourse.alu_op_type import AluOpType
from concourse.bass_utils import run_bass_kernel_spmd

F32 = mybir.dt.float32
BF16 = mybir.dt.bfloat16
AF = mybir.ActivationFunctionType
AX = mybir.AxisListType

T_FULL = 4096
T_OWN = 2048
C = 1024
NKC = 8
NPAIR = 8
PAIRS_PER_G = 4
NGROUP = 2
NQT = 4
NKT = 32
EPS = 1e-5
MINUS_LN8 = -2.0794415416798357


class Sem:
    def __init__(self, handle):
        self.h = handle
        self.n = 0

    def inc(self, k=1):
        self.n += k
        return self.n


class Script:

    def __init__(self, sv, ss):
        self.items = []
        self._sv = sv
        self._ss = ss

    def t(self, fn):
        self.items.append(("tensor", fn, False))

    def v(self, fn):
        self.items.append(("vector", fn, True))
        return self._sv.inc()

    def s(self, fn):
        self.items.append(("scalar", fn, True))
        return self._ss.inc()

    def y(self, fn):
        self.items.append(("sync", fn, False))

    def g(self, fn):
        self.items.append(("gpsimd", fn, False))

    def wait(self, eng, sem, val):
        if val <= 0:
            return
        self.items.append(
            (eng, lambda e, s=sem.h, v=val: e.wait_ge(s, v), False))

    def run(self, engine_name, eng, chain_sem, chain_base):
        n = chain_base
        for name, fn, chained in self.items:
            if name != engine_name:
                continue
            if chained:
                if n > 0:
                    eng.wait_ge(chain_sem, n)
                ins = fn(eng)
                assert ins is not None, "chained entry must return instruction"
                ins.then_inc(chain_sem, 1)
                n += 1
            else:
                fn(eng)
        return n


def emit_block(nc, script, L):
    sv, ss = L["sv"], L["ss"]
    with nc.Block() as block:
        @block.tensor
        def _(t):
            script.run("tensor", t, None, 0)

        @block.vector
        def _(v):
            base = L["chain_done"]["vector"]
            L["chain_done"]["vector"] = script.run("vector", v, sv.h, base)

        @block.scalar
        def _(s):
            base = L["chain_done"]["scalar"]
            L["chain_done"]["scalar"] = script.run("scalar", s, ss.h, base)

        @block.sync
        def _(y):
            script.run("sync", y, None, 0)

        @block.gpsimd
        def _(g):
            script.run("gpsimd", g, None, 0)


def barrier(sc, L):
    st, sv, ss, sy, sg = L["st"], L["sv"], L["ss"], L["sy"], L["sg"]
    t_t = st.inc()
    sc.t(lambda e, s=st.h: e.drain().then_inc(s, 1))
    t_v = sc.v(lambda e: e.drain())
    t_s = sc.s(lambda e: e.drain())
    t_y = sy.inc()
    sc.y(lambda e, s=sy.h: e.drain().then_inc(s, 1))
    t_g = sg.inc()
    sc.g(lambda e, s=sg.h: e.drain().then_inc(s, 1))
    targets = {"tensor": (st, t_t), "vector": (sv, t_v), "scalar": (ss, t_s),
               "sync": (sy, t_y), "gpsimd": (sg, t_g)}
    for e in targets:
        for e2, (sem, val) in targets.items():
            if e2 != e:
                sc.wait(e, sem, val)


def build_nc():
    nc = bass.Bass("TRN2", target_bir_lowering=False, debug=False)

    L = {}
    L["x_hbm"] = nc.dram_tensor("x", [T_FULL, C], F32, kind="ExternalInput").ap()
    L["wqkv"] = nc.dram_tensor("wqkv", [C, 3 * C], F32, kind="ExternalInput").ap()
    L["bqkv"] = nc.dram_tensor("bqkv", [3 * C], F32, kind="ExternalInput").ap()
    L["gam"] = nc.dram_tensor("gam", [C], F32, kind="ExternalInput").ap()
    L["bet"] = nc.dram_tensor("bet", [C], F32, kind="ExternalInput").ap()
    L["wout"] = nc.dram_tensor("wout", [C, C], F32, kind="ExternalInput").ap()
    L["bout"] = nc.dram_tensor("bout", [C], F32, kind="ExternalInput").ap()
    L["win"] = nc.dram_tensor("win", [C, 64], F32, kind="ExternalInput").ap()
    L["binf"] = nc.dram_tensor("binf", [64], F32, kind="ExternalInput").ap()
    L["rw1"] = nc.dram_tensor("rw1", [64, C], F32, kind="ExternalInput").ap()
    L["b1f"] = nc.dram_tensor("b1f", [C], F32, kind="ExternalInput").ap()
    L["w2b"] = nc.dram_tensor("w2b", [C, 64], F32, kind="ExternalInput").ap()
    L["b2"] = nc.dram_tensor("b2", [64], F32, kind="ExternalInput").ap()
    L["wfo"] = nc.dram_tensor("wfo", [64, C], F32, kind="ExternalInput").ap()
    L["bffo"] = nc.dram_tensor("bffo", [C], F32, kind="ExternalInput").ap()
    L["ident_h"] = nc.dram_tensor("ident", [128, 128], BF16, kind="ExternalInput").ap()
    L["identf_h"] = nc.dram_tensor("identf", [128, 128], F32, kind="ExternalInput").ap()
    L["onesb_h"] = nc.dram_tensor("onesb", [128, 128], BF16, kind="ExternalInput").ap()
    L["onesf_h"] = nc.dram_tensor("onesf", [1, 128], F32, kind="ExternalInput").ap()
    L["consts_h"] = nc.dram_tensor("consts", [128, 2], F32, kind="ExternalInput").ap()
    L["out_hbm"] = nc.dram_tensor("out", [T_OWN, C], F32, kind="ExternalOutput").ap()

    es = ExitStack()
    with es:
        def SB(name, shape, dt):
            return es.enter_context(nc.sbuf_tensor(name, shape, dt))

        def SEM(name):
            return es.enter_context(nc.semaphore(name))

        L.update(
            ident_sb=SB("ident_sb", [128, 128], BF16),
            identf_sb=SB("identf_sb", [128, 128], F32),
            onesb_sb=SB("onesb_sb", [128, 128], BF16),
            onesf_sb=SB("onesf_sb", [1, 128], F32),
            consts_sb=SB("consts_sb", [128, 2], F32),
            bqkvT_sb=SB("bqkvT_sb", [128, 24], F32),
            gamT_sb=SB("gamT_sb", [128, 8], F32),
            betT_sb=SB("betT_sb", [128, 8], F32),
            bin_sb=SB("bin_sb", [64, 1], F32),
            b1T_sb=SB("b1T_sb", [128, 8], F32),
            b2_sb=SB("b2_sb", [64, 1], F32),
            negB_sb=SB("negB_sb", [128, NPAIR], F32),
            attnT_sb=SB("attnT_sb", [128, NPAIR * T_OWN], BF16),
        )
        dc, dg = Sem(SEM("s_dc")), Sem(SEM("s_dg2"))
        dxp = [Sem(SEM("s_dx0")), Sem(SEM("s_dx1")),
               Sem(SEM("s_dx2")), Sem(SEM("s_dx3"))]
        dwp = [Sem(SEM("s_dw0")), Sem(SEM("s_dw1"))]
        dwo = Sem(SEM("s_dwo"))
        dop = [Sem(SEM("s_do0")), Sem(SEM("s_do1"))]
        st, sv, ss = Sem(SEM("s_t")), Sem(SEM("s_v")), Sem(SEM("s_s"))
        sy, sg = Sem(SEM("s_y")), Sem(SEM("s_g"))
        L.update(dc=dc, dg=dg, dxp=dxp, dwp=dwp, dwo=dwo, dop=dop,
                 st=st, sv=sv, ss=ss, sy=sy, sg=sg,
                 chain_done={"vector": 0, "scalar": 0})

        with ExitStack() as es12:
            def SB12(name, shape, dt):
                return es12.enter_context(nc.sbuf_tensor(name, shape, dt))

            L.update(
                kT_sb=SB12("kT_sb", [128, 2 * T_FULL], BF16),
                v_sb=SB12("v_sb", [128, 2 * T_FULL], BF16),
                qT_sb=SB12("qT_sb", [128, 2 * T_OWN], BF16),
                xhq_sb=SB12("xhq_sb", [128, NKC * T_FULL], BF16),
                xtile_sb=SB12("xtile_sb", [128, 3 * C], F32),
                xh_tile=SB12("xh_tile", [128, 2 * C], BF16),
                wq_sb=SB12("wq_sb", [128, 2 * 1024], BF16),
                wq3_sb=SB12("wq3_sb", [128, 3 * 1024], BF16),
                vstage_sb=SB12("vstage_sb", [128, 8 * 512], BF16),
                vtmp_sb=SB12("vtmp_sb", [128, 2 * 512], BF16),
                stats_sb=SB12("stats_sb", [128, 12], F32),
                mv_sb=SB12("mv_sb", [128, 8], F32),
                rstd_sb=SB12("rstd_sb", [128, 4], F32),
                nmax_sb=SB12("nmax_sb", [128, 8], F32),
                PT_sb=SB12("PT_sb", [128, 8 * 1024], BF16),
                zb_sb=SB12("zb_sb", [128, 512], F32),
            )
            L["sq_sb"] = L["xh_tile"]
            with ExitStack() as esA:
                L["score_ps"] = esA.enter_context(
                    nc.psum_tensor("score_ps", [128, 2048], F32))
                L["pv_ps"] = esA.enter_context(
                    nc.psum_tensor("pv_ps", [128, 512], F32))
                L["z_ps"] = esA.enter_context(
                    nc.psum_tensor("z_ps", [128, 512], F32))
                L["qkv_ps"] = esA.enter_context(
                    nc.psum_tensor("qkv_ps", [128, 512], F32))
                L["tp_ps"] = esA.enter_context(
                    nc.psum_tensor("tp_ps", [128, 1024], BF16))
                sc = Script(sv, ss)
                p0_loads(sc, L)
                L["plstate"] = {
                    "lqe": 0,
                    "last_wq_consumed": [0, 0],
                    "last_tp_evac": 0,
                    "last_tp_evac_v": 0,
                    "last_qkv_evac": 0,
                    "last_vtmp_read": [0, 0],
                    "v_negB": [0] * 8,
                    "kv_parity_free": [0, 0],
                    "pvz_free": 0,
                    "vstage_ev": [0] * 8,
                    "it_hist": [],
                }
                phase_ln(sc, L)
                pipeline(sc, L)
                barrier(sc, L)
                emit_block(nc, sc, L)

        with ExitStack() as es34:
            def SB34(name, shape, dt):
                return es34.enter_context(nc.sbuf_tensor(name, shape, dt))

            L.update(
                wout_sb=SB34("wout_sb", [128, 8 * C], BF16),
                boutrow_sb=SB34("boutrow_sb", [1, C], F32),
                bfforow_sb=SB34("bfforow_sb", [1, C], F32),
                win_sb=SB34("win_sb", [128, 8 * 64], BF16),
                rw1_sb=SB34("rw1_sb", [64, C], BF16),
                w2b_sb=SB34("w2b_sb", [128, 8 * 64], BF16),
                wfo_sb=SB34("wfo_sb", [64, C], BF16),
                boutbc_sb=SB34("boutbc_sb", [128, C], F32),
                bffobc_sb=SB34("bffobc_sb", [128, C], F32),
                x2_sb=SB34("x2_sb", [128, 8 * C], F32),
                x2bf_sb=SB34("x2bf_sb", [128, 4 * C], BF16),
                x2T_sb=SB34("x2T_sb", [128, 2 * 4096], BF16),
                xres_sb=SB34("xres_sb", [128, 4 * C], F32),
                sT_sb=SB34("sT_sb", [64, 2 * 512], BF16),
                hT_sb=SB34("hT_sb", [128, 2 * 4096], BF16),
                gT_sb=SB34("gT_sb", [64, 2 * 512], BF16),
                u_sb=SB34("u_sb", [128, 8 * C], F32),
                stats2_sb=SB34("stats2_sb", [128, 96], F32),
                mv2_sb=SB34("mv2_sb", [128, 16], F32),
                rstd2_sb=SB34("rstd2_sb", [128, 8], F32),
            )
            L["A_ps"] = es34.enter_context(
                nc.psum_tensor("A_ps", [128, 2048], F32))
            L["F_ps"] = es34.enter_context(
                nc.psum_tensor("F_ps", [128, 1024], F32))
            L["B_ps"] = es34.enter_context(
                nc.psum_tensor("B_ps", [128, 1024], BF16))
            L["C_ps"] = es34.enter_context(
                nc.psum_tensor("C_ps", [64, 512], F32))
            sc = Script(sv, ss)
            p34(sc, L)
            sc.wait("sync", dop[0], dop[0].n)
            sc.wait("sync", dop[1], dop[1].n)
            emit_block(nc, sc, L)

    return nc



def p0_loads(sc, L):
    dc, dg = L["dc"], L["dg"]

    def ld(dst, src, slow=False):
        sc.y(lambda e, d=dst, s=src, sl=slow: e.dma_start(
            out=d, in_=s, allow_slow_non_contiguous=sl).then_inc(dc.h, 16))
        dc.inc(16)

    def ldg(dst, src):
        sc.g(lambda e, d=dst, s=src: e.dma_start(
            out=d, in_=s).then_inc(dg.h, 16))
        dg.inc(16)

    ld(L["ident_sb"][:], L["ident_h"])
    ld(L["identf_sb"][:], L["identf_h"])
    ld(L["onesb_sb"][:], L["onesb_h"])
    ld(L["onesf_sb"][:], L["onesf_h"])
    ld(L["consts_sb"][:], L["consts_h"])
    ld(L["bqkvT_sb"][:], L["bqkv"].rearrange("(m p) -> p m", p=128), slow=True)
    ld(L["gamT_sb"][:], L["gam"].rearrange("(m p) -> p m", p=128), slow=True)
    ld(L["betT_sb"][:], L["bet"].rearrange("(m p) -> p m", p=128), slow=True)
    ld(L["bin_sb"][:], L["binf"][:, None], slow=True)
    ld(L["b1T_sb"][:], L["b1f"].rearrange("(m p) -> p m", p=128), slow=True)
    ld(L["b2_sb"][:], L["b2"][:, None], slow=True)
    for e in ("tensor", "vector", "scalar"):
        sc.wait(e, dc, dc.n)
        sc.wait(e, dg, dg.n)


def phase_ln(sc, L):
    dxp, dwo = L["dxp"], L["dwo"]
    st, sv, ss = L["st"], L["sv"], L["ss"]
    xtile_sb, xh_tile, stats_sb, mv_sb, rstd_sb = (
        L["xtile_sb"], L["xh_tile"], L["stats_sb"], L["mv_sb"], L["rstd_sb"])
    xhq_sb, wq3_sb = L["xhq_sb"], L["wq3_sb"]
    kT_sb, qT_sb = L["kT_sb"], L["qT_sb"]
    qkv_ps, score_ps = L["qkv_ps"], L["score_ps"]
    x_hbm, wqkv = L["x_hbm"], L["wqkv"]
    ident_sb, gamT_sb, betT_sb, bqkvT_sb = (
        L["ident_sb"], L["gamT_sb"], L["betT_sb"], L["bqkvT_sb"])
    eps_ap = L["consts_sb"][:, 0:1]
    state = L["plstate"]

    dws = dwo
    for j, m in enumerate((8, 16, 0)):
        srcw = wqkv[:, 128 * m:128 * m + 128].rearrange(
            "(kc p) c -> p kc c", p=128)
        sc.g(lambda e, d=wq3_sb[:, j * 1024:(j + 1) * 1024].rearrange(
             "p (kc c) -> p kc c", kc=8), s=srcw, hh=dws.h:
             e.dma_start(out=d, in_=s).then_inc(hh, 16))
        dws.inc(16)
    w3_ready = dws.n

    NB = 3
    dma_cnt = [0] * NB
    v_xh_t = [0] * 32
    s_rstd_t = [0] * 32
    s_evac_t = [None] * 32
    last_xh_read = [0, 0]
    last_score_evac = [0, 0]
    v_gb_slice = [0] * 8
    last_qkv_evac = [0]
    w3_waited = [False]

    def issue_dma(j, guard):
        if j > 31:
            return
        b = j % NB
        xt = xtile_sb[:, b * C:(b + 1) * C]
        if guard:
            sc.wait("sync", sv, guard)
        dxs = dxp[b]
        sc.y(lambda e, d=xt, s=x_hbm[j * 128:j * 128 + 128, :], hh=dxs.h:
             e.dma_start(out=d, in_=s).then_inc(hh, 16))
        dma_cnt[b] = dxs.inc(16)

    pend = []
    for n in range(8):
        pend.append((8, n))
        pend.append((16, n))
        if n < 4:
            pend.append((0, n))
    pi = [0]
    pending_evac = []

    def qkv_burst():
        if pi[0] >= len(pend):
            return False
        m, n = pend[pi[0]]
        if not v_gb_slice[n]:
            return False
        pi[0] += 1
        if not w3_waited[0]:
            sc.wait("tensor", dws, w3_ready)
            w3_waited[0] = True
        j = {8: 0, 16: 1, 0: 2}[m]
        wbuf = wq3_sb[:, j * 1024:(j + 1) * 1024]
        sc.wait("tensor", sv, v_gb_slice[n])
        sc.wait("tensor", ss, last_qkv_evac[0])
        for kc in range(8):
            last = kc == 7
            def mk(o, w, r, s0, s1, last=last):
                def f(e):
                    ins = e.matmul(o, w, r, start=s0, stop=s1)
                    if last:
                        ins.then_inc(st.h, 1)
                    return ins
                return f
            sc.t(mk(qkv_ps[:], wbuf[:, kc * 128:kc * 128 + 128],
                    xhq_sb[:, kc * T_FULL + n * 512:
                           kc * T_FULL + n * 512 + 512],
                    kc == 0, kc == 7))
        pending_evac.append((st.inc(), m, n))
        return True

    def qkv_evac():
        if not pending_evac:
            return False
        t_mm, m, n = pending_evac.pop(0)
        if m == 8:
            dst = kT_sb[:, n * 512:n * 512 + 512]
        elif m == 16:
            dst = L["vstage_sb"][:, n * 512:n * 512 + 512]
        else:
            dst = qT_sb[:, n * 512:n * 512 + 512]
        sc.wait("scalar", st, t_mm)
        s_ev = sc.s(lambda e, o=dst, i_=qkv_ps[:], b=bqkvT_sb[:, m:m + 1]:
                    e.activation(out=o, in_=i_, func=AF.Identity, bias=b))
        last_qkv_evac[0] = s_ev
        if m == 16:
            state["vstage_ev"][n] = s_ev
        return True

    issue_dma(0, 0)
    issue_dma(1, 0)
    for j in range(33):
        if j < 32:
            b = j % NB
            xt = xtile_sb[:, b * C:(b + 1) * C]
            sc.wait("vector", dxp[b], dma_cnt[b])
            sc.v(lambda e, o=stats_sb[:, 0:6], i_=xt[:, 0:512]:
                 e.bn_stats(o, i_))
            sc.v(lambda e, o=stats_sb[:, 6:12], i_=xt[:, 512:1024]:
                 e.bn_stats(o, i_))
            mvb = mv_sb[:, 2 * (j % 2):2 * (j % 2) + 2]
            v_stats = sc.v(lambda e, o=mvb, i_=stats_sb[:]: e.bn_aggr(o, i_))
            rst = rstd_sb[:, j % 2:j % 2 + 1]
            sc.wait("scalar", sv, v_stats)
            sc.s(lambda e, o=rst, i_=mvb[:, 1:2], b=eps_ap:
                 e.activation(out=o, in_=i_, func=AF.Ln, bias=b))
            s_rstd_t[j] = sc.s(lambda e, o=rst, i_=rst:
                               e.activation(out=o, in_=i_, func=AF.Exp,
                                            scale=-0.5))
        if j >= 1:
            i = j - 1
            qkv_evac()
            par = i % 2
            bi = i % NB
            xt_i = xtile_sb[:, bi * C:(bi + 1) * C]
            xht = xh_tile[:, par * C:(par + 1) * C]
            mvb = mv_sb[:, 2 * par:2 * par + 2]
            rst = rstd_sb[:, par:par + 1]
            sc.wait("vector", ss, s_rstd_t[i])
            sc.wait("vector", st, last_xh_read[par])
            v_xh = sc.v(lambda e, o=xht, i_=xt_i, m=mvb[:, 0:1], r=rst:
                        e.tensor_scalar(o, i_, m, r, op0=AluOpType.subtract,
                                        op1=AluOpType.mult))
            v_xh_t[i] = v_xh
            issue_dma(i + 2, v_xh_t[i - 1] if i >= 1 else 0)
            sps = score_ps[:, par * 1024:par * 1024 + 1024]
            sc.wait("tensor", sv, v_xh)
            sc.wait("tensor", ss, last_score_evac[par])
            for kc in range(8):
                last = kc == 7
                def mk(o, w, r, last=last):
                    def f(e):
                        ins = e.matmul(o, w, r, start=True, stop=True,
                                       skip_group_check=True)
                        if last:
                            ins.then_inc(st.h, 1)
                        return ins
                    return f
                sc.t(mk(sps[:, kc * 128:kc * 128 + 128],
                        xht[:, kc * 128:kc * 128 + 128], ident_sb[:]))
            t_tp = st.inc()
            last_xh_read[par] = t_tp
            sc.wait("scalar", st, t_tp)
            dst = xhq_sb[:].rearrange("p (kc t) -> p kc t", kc=8)[
                :, :, i * 128:i * 128 + 128]
            srcv = sps.rearrange("p (kc t) -> p kc t", kc=8)
            s_e = sc.s(lambda e, o=dst, i_=srcv: e.copy(out=o, in_=i_))
            s_evac_t[i] = s_e
            last_score_evac[par] = s_e
            if i % 4 == 3:
                n = i // 4
                sc.wait("vector", ss, s_e)
                v_gb = 0
                for kc in range(8):
                    slc = xhq_sb[:, kc * T_FULL + n * 512:
                                 kc * T_FULL + n * 512 + 512]
                    v_gb = sc.v(lambda e, o=slc, i_=slc,
                                ga=gamT_sb[:, kc:kc + 1],
                                be=betT_sb[:, kc:kc + 1]:
                                e.tensor_scalar(o, i_, ga, be,
                                                op0=AluOpType.mult,
                                                op1=AluOpType.add))
                v_gb_slice[n] = v_gb
            qkv_burst()
    while pi[0] < len(pend) or pending_evac:
        made = qkv_evac()
        made = qkv_burst() or made
        if not made:
            break
    L["v_gb"] = v_gb_slice[7]
    state["lqe_s"] = last_qkv_evac[0]
    state["last_qkv_evac"] = 0
    state["last_tp_evac"] = 0
    L["ln_tp_evac"] = 0


def pipeline(sc, L):
    st, sv, ss = L["st"], L["sv"], L["ss"]
    state = L["plstate"]
    sc.wait("vector", ss, state["lqe_s"])
    sc.wait("tensor", ss, state["lqe_s"])

    def full_gen(sg):
        yield from qkv_steps_gen(sc, L, sg, state)
        yield from bound_gen(sc, L, sg, state)

    ident_sb, tp_ps, v_sb = L["ident_sb"], L["tp_ps"], L["v_sb"]
    vstage = L["vstage_sb"]

    def vstep_gen():
        for n in range(8):
            sc.wait("tensor", ss, state["vstage_ev"][n])
            sc.wait("tensor", ss, state["last_tp_evac"])
            sc.wait("tensor", sv, state["last_tp_evac_v"])
            for blk in range(4):
                last = blk == 3
                def mk(o, i_, idn, last=last):
                    def f(e):
                        ins = e.transpose(o, i_, idn)
                        if last:
                            ins.then_inc(st.h, 1)
                        return ins
                    return f
                sc.t(mk(tp_ps[:, blk * 128:blk * 128 + 128],
                        vstage[:, n * 512 + blk * 128:
                               n * 512 + blk * 128 + 128], ident_sb[:]))
            t_vt = st.inc()
            sc.wait("vector", st, t_vt)
            state["last_tp_evac_v"] = sc.v(
                lambda e, o=v_sb[:, (n * 4) * 128:(n * 4 + 4) * 128],
                i_=tp_ps[:, 0:512]: e.tensor_copy(o, i_))
            yield n

    vsteps = vstep_gen()
    for _ in bound_gen(sc, L, 0, state):
        next(vsteps, None)
    for _ in vsteps:
        pass
    state["pvz_free"] = state["last_tp_evac_v"]
    for sg in range(8):
        gen = full_gen(sg + 1) if sg + 1 < 8 else None
        p2_subgroup(sc, L, sg, state, gen)
        if gen is not None:
            for _ in gen:
                pass
        state["kv_parity_free"][sg % 2] = st.n


def qkv_steps_gen(sc, L, sg, state):
    dwp = L["dwp"]
    st, sv, ss = L["st"], L["sv"], L["ss"]
    xhq_sb, wq_sb, vtmp_sb = L["xhq_sb"], L["wq_sb"], L["vtmp_sb"]
    kT_sb, v_sb, qT_sb = L["kT_sb"], L["v_sb"], L["qT_sb"]
    qkv_ps, tp_ps = L["qkv_ps"], L["tp_ps"]
    wqkv = L["wqkv"]
    ident_sb, bqkvT_sb = L["ident_sb"], L["bqkvT_sb"]
    par = sg % 2
    kbase = par * T_FULL
    qbase = par * T_OWN

    k_tiles = [8 + sg]
    v_tiles = [16 + sg]
    q_tiles = [sg]
    mlist = k_tiles + v_tiles + q_tiles
    wq_ready = {}

    def issue_wload(j):
        m = mlist[j]
        wbuf = wq_sb[:, (j % 2) * 1024:(j % 2) * 1024 + 1024]
        sc.wait("gpsimd", st, state["last_wq_consumed"][j % 2])
        srcw = wqkv[:, 128 * m:128 * m + 128].rearrange(
            "(kc p) c -> p kc c", p=128)
        dws = dwp[j % 2]
        sc.g(lambda e, d=wbuf.rearrange("p (kc c) -> p kc c", kc=8), s=srcw,
             hh=dws.h: e.dma_start(out=d, in_=s).then_inc(hh, 16))
        wq_ready[j] = (dws, dws.inc(16))

    issue_wload(0)
    issue_wload(1)
    first = True
    for j, m in enumerate(mlist):
        wbuf = wq_sb[:, (j % 2) * 1024:(j % 2) * 1024 + 1024]
        nrange = 4 if m in q_tiles else 8
        for n in range(nrange):
            if m in k_tiles:
                dst = kT_sb[:, kbase + n * 512:kbase + n * 512 + 512]
            elif m in v_tiles:
                dst = vtmp_sb[:, (n % 2) * 512:(n % 2) * 512 + 512]
            else:
                dst = qT_sb[:, qbase + n * 512:qbase + n * 512 + 512]
            sc.wait("tensor", wq_ready[j][0], wq_ready[j][1])
            if first:
                sc.wait("tensor", sv, L["v_gb"])
                if sg >= 2:
                    sc.wait("vector", st, state["kv_parity_free"][par])
                first = False
            sc.wait("tensor", sv, state["last_qkv_evac"])
            for kc in range(8):
                if kc == 4:
                    yield (m, n, "mid")
                last = kc == 7
                def mk(o, w, r, s0, s1, last=last):
                    def f(e):
                        ins = e.matmul(o, w, r, start=s0, stop=s1)
                        if last:
                            ins.then_inc(st.h, 1)
                        return ins
                    return f
                sc.t(mk(qkv_ps[:], wbuf[:, kc * 128:kc * 128 + 128],
                        xhq_sb[:, kc * T_FULL + n * 512:
                               kc * T_FULL + n * 512 + 512],
                        kc == 0, kc == 7))
            t_mm = st.inc()
            yield (m, n, 0)
            sc.wait("vector", st, t_mm)
            if m in v_tiles:
                sc.wait("vector", st, state["last_vtmp_read"][n % 2])
            v_ev = sc.v(lambda e, o=dst, i_=qkv_ps[:], b=bqkvT_sb[:, m:m + 1]:
                        e.tensor_scalar(o, i_, b, None, op0=AluOpType.add,
                                        op1=AluOpType.bypass))
            state["last_qkv_evac"] = v_ev
            if n == nrange - 1:
                state["last_wq_consumed"][j % 2] = st.n
                if j + 2 < len(mlist):
                    issue_wload(j + 2)
            if m in v_tiles:
                sc.wait("tensor", sv, v_ev)
                sc.wait("tensor", ss, state["last_tp_evac"])
                sc.wait("tensor", sv, state["last_tp_evac_v"])
                for blk in range(4):
                    last = blk == 3
                    def mk(o, i_, idn, last=last):
                        def f(e):
                            ins = e.transpose(o, i_, idn)
                            if last:
                                ins.then_inc(st.h, 1)
                            return ins
                        return f
                    sc.t(mk(tp_ps[:, blk * 128:blk * 128 + 128],
                            dst[:, blk * 128:blk * 128 + 128],
                            ident_sb[:]))
                t_vt = st.inc()
                state["last_vtmp_read"][n % 2] = t_vt
                vdst = v_sb[:, kbase + (n * 4) * 128:
                            kbase + (n * 4 + 4) * 128]
                sc.wait("vector", st, t_vt)
                state["last_tp_evac_v"] = sc.v(
                    lambda e, o=vdst, i_=tp_ps[:, 0:512]:
                    e.tensor_copy(o, i_))
            yield (m, n, 4)


def bound_gen(sc, L, sg, state):
    st, sv, ss = L["st"], L["sv"], L["ss"]
    kT_sb, qT_sb, sq_sb, nmax_sb, negB_sb = (
        L["kT_sb"], L["qT_sb"], L["sq_sb"], L["nmax_sb"], L["negB_sb"])
    onesb_sb = L["onesb_sb"]
    qkv_ps = L["qkv_ps"]
    mln8_ap = L["consts_sb"][:, 1:2]
    par = sg % 2
    kbase = par * T_FULL
    qbase = par * T_OWN
    last_sq_read = state.setdefault("bound_sq_read", [0])

    v_nb = 0
    for pl in range(1):
        p = sg
        srcs = [(qT_sb[:, qbase:qbase + T_OWN], 0),
                (kT_sb[:, kbase:kbase + 2048], 1),
                (kT_sb[:, kbase + 2048:kbase + 4096], 2)]
        for (srcq, ci) in srcs:
            sc.wait("vector", st, last_sq_read[0])
            v_sq = sc.v(lambda e, o=sq_sb[:], a=srcq:
                        e.tensor_tensor(o, a, a, op=AluOpType.mult))
            sc.wait("tensor", sv, v_sq)
            sc.wait("tensor", sv, state["last_qkv_evac"])
            acc = nmax_sb[:, ci + 4:ci + 5]
            tgt = nmax_sb[:, ci:ci + 1]
            for n4 in range(4):
                def mk(o, w, r):
                    def f(e):
                        ins = e.matmul(o, w, r, start=True, stop=True)
                        ins.then_inc(st.h, 1)
                        return ins
                    return f
                sc.t(mk(qkv_ps[:], onesb_sb[:],
                        sq_sb[:, n4 * 512:n4 * 512 + 512]))
                t_nm = st.inc()
                last_sq_read[0] = t_nm
                sc.wait("vector", st, t_nm)
                if n4 == 0:
                    v_r = sc.v(lambda e, o=tgt, i_=qkv_ps[:]:
                               e.tensor_reduce(o, i_, axis=AX.X,
                                               op=AluOpType.max))
                else:
                    sc.v(lambda e, o=acc, i_=qkv_ps[:]:
                         e.tensor_reduce(o, i_, axis=AX.X, op=AluOpType.max))
                    v_r = sc.v(lambda e, o=tgt, a=tgt, b=acc:
                               e.tensor_tensor(o, a, b, op=AluOpType.max))
                sc.wait("tensor", sv, v_r)
                yield ("bound", p, ci, n4)
        v_k = sc.v(lambda e, o=nmax_sb[:, 3:4], a=nmax_sb[:, 1:2],
                   b=nmax_sb[:, 2:3]:
                   e.tensor_tensor(o, a, b, op=AluOpType.max))
        sc.wait("scalar", sv, v_k)
        sc.s(lambda e, o=nmax_sb[:, 4:5], i_=nmax_sb[:, 0:1]:
             e.activation(out=o, in_=i_, func=AF.Ln, bias=0.0))
        s_ln = sc.s(lambda e, o=nmax_sb[:, 5:6], i_=nmax_sb[:, 3:4]:
                    e.activation(out=o, in_=i_, func=AF.Ln, bias=0.0))
        sc.wait("vector", ss, s_ln)
        v_add = sc.v(lambda e, o=nmax_sb[:, 6:7], a=nmax_sb[:, 4:5],
                     b=nmax_sb[:, 5:6]:
                     e.tensor_tensor(o, a, b, op=AluOpType.add))
        sc.wait("scalar", sv, v_add)
        s_B = sc.s(lambda e, o=nmax_sb[:, 7:8], i_=nmax_sb[:, 6:7], b=mln8_ap:
                   e.activation(out=o, in_=i_, func=AF.Exp, scale=0.5,
                                bias=b))
        sc.wait("vector", ss, s_B)
        v_nb = sc.v(lambda e, o=negB_sb[:, p:p + 1], i_=nmax_sb[:, 7:8]:
                    e.tensor_scalar(o, i_, -1.0, -1.0, op0=AluOpType.mult,
                                    op1=AluOpType.add))
        yield ("negB", p, 0)
    state["v_negB"][sg] = v_nb


def p2_subgroup(sc, L, sg, state, gen):
    st, sv, ss = L["st"], L["sv"], L["ss"]
    kT_sb, v_sb, qT_sb, PT_sb, zb_sb, attnT_sb = (
        L["kT_sb"], L["v_sb"], L["qT_sb"], L["PT_sb"], L["zb_sb"],
        L["attnT_sb"])
    onesb_sb, negB_sb = L["onesb_sb"], L["negB_sb"]
    score_ps, pv_ps, z_ps = L["score_ps"], L["pv_ps"], L["z_ps"]
    par = sg % 2
    kbase = par * T_FULL
    qbase = par * T_OWN
    it_hist = state["it_hist"]
    p = sg

    def emit_scores(qt, kt, guard):
        spar = kt % 2
        sps = score_ps[:, spar * 1024:spar * 1024 + 1024]
        kA = kT_sb[0:64, kbase + kt * 128:kbase + kt * 128 + 128]
        kB = kT_sb[64:128, kbase + kt * 128:kbase + kt * 128 + 128]
        qA = qT_sb[0:64, qbase + qt * 512:qbase + qt * 512 + 512]
        qB = qT_sb[64:128, qbase + qt * 512:qbase + qt * 512 + 512]
        if guard is not None:
            sc.wait("tensor", ss, guard)
        sc.t(lambda e, o=sps[:, 0:512], w=kA, r=qA:
             e.matmul(o, w, r, start=True, stop=True))
        sc.t(lambda e, o=sps[:, 512:1024], w=kB, r=qB:
             e.matmul(o, w, r, start=True, stop=True).then_inc(st.h, 1))
        return st.inc()

    def emit_pvz(qt, kt, it, exp_done_ref=None):
        spar = kt % 8
        pt = PT_sb[:, spar * 1024:spar * 1024 + 1024]
        vch = v_sb[:, kbase + kt * 128:kbase + kt * 128 + 128]
        sc.wait("tensor", ss, exp_done_ref[kt])
        if kt == 0 and it >= 1:
            sc.wait("tensor", sv, it_hist[it - 1]["norm_v"])
        elif kt == 0:
            sc.wait("tensor", sv, state["pvz_free"])
        s0, s1 = (kt == 0), (kt == NKT - 1)
        sc.t(lambda e, o=pv_ps[0:64, :], w=vch[:, 0:64], r=pt[:, 0:512],
             a=s0, b=s1: e.matmul(o, w, r, start=a, stop=b,
                                  skip_group_check=True))
        sc.t(lambda e, o=pv_ps[64:128, :], w=vch[:, 64:128],
             r=pt[:, 512:1024], a=s0, b=s1:
             e.matmul(o, w, r, start=a, stop=b, skip_group_check=True))
        sc.t(lambda e, o=z_ps[0:64, :], w=onesb_sb[:, 0:64],
             r=pt[:, 0:512], a=s0, b=s1:
             e.matmul(o, w, r, start=a, stop=b, skip_group_check=True))
        sc.t(lambda e, o=z_ps[64:128, :], w=onesb_sb[:, 0:64],
             r=pt[:, 512:1024], a=s0, b=s1:
             e.matmul(o, w, r, start=a, stop=b,
                      skip_group_check=True).then_inc(st.h, 1))
        return st.inc()

    def finish_qt(h, pqt, hit):
        sc.wait("tensor", ss, h["exp_done"][NKT - 1])
        for j in (NKT - 2, NKT - 1):
            h["pvz_done"][j] = emit_pvz(pqt, j, hit, h["exp_done"])
        if gen is not None:
            next(gen, None)
            next(gen, None)
            next(gen, None)
        zbh = zb_sb[:]
        sc.wait("vector", st, h["pvz_done"][NKT - 1])
        sc.v(lambda e, o=zbh, i_=z_ps[:]: e.reciprocal(o, i_))
        dst = attnT_sb[:, p * T_OWN + pqt * 512:p * T_OWN + pqt * 512 + 512]
        h["norm_v"] = sc.v(lambda e, o=dst, a=pv_ps[:], b=zbh:
                           e.tensor_tensor(o, a, b, op=AluOpType.mult))

    first_of_sg = True
    for qt in range(NQT):
        it = len(it_hist)
        exp_done = []
        sc_done = []
        pvz_done = {}
        gprev = it_hist[it - 1]["exp_done"][NKT - 2] if it >= 1 else None
        sc_done.append(emit_scores(qt, 0, gprev))
        if first_of_sg:
            sc.wait("scalar", sv, state["v_negB"][sg])
            first_of_sg = False
        if qt >= 1:
            finish_qt(it_hist[it - 1], qt - 1, it - 1)
        it_hist.append(dict(exp_done=exp_done, pvz_done=pvz_done, norm_v=0))
        for kt in range(NKT):
            if kt + 1 < NKT:
                if kt + 1 >= 2:
                    gg = exp_done[kt - 1]
                elif it >= 1:
                    gg = it_hist[it - 1]["exp_done"][NKT - 1]
                else:
                    gg = None
                sc_done.append(emit_scores(qt, kt + 1, gg))
            else:
                sc.wait("tensor", ss, exp_done[kt - 1])
            if kt % 2 == 1 and kt >= 3:
                pvz_done[kt - 3] = emit_pvz(qt, kt - 3, it, exp_done)
                pvz_done[kt - 2] = emit_pvz(qt, kt - 2, it, exp_done)
            spar = kt % 8
            pt = PT_sb[:, spar * 1024:spar * 1024 + 1024]
            g1 = sc_done[kt]
            if kt >= 8:
                g1 = max(g1, pvz_done[kt - 8])
            sc.wait("scalar", st, g1)
            if kt < 8 and it >= 1:
                sc.wait("scalar", st,
                        it_hist[it - 1]["pvz_done"][NKT - 8 + kt])
            s_exp = sc.s(lambda e, o=pt,
                         i_=score_ps[:, (kt % 2) * 1024:(kt % 2) * 1024 + 1024],
                         b=negB_sb[:, p:p + 1]:
                         e.activation(out=o, in_=i_, func=AF.Exp, bias=b,
                                      scale=0.125))
            exp_done.append(s_exp)
            if gen is not None and kt % 2 == 0:
                next(gen, None)
    finish_qt(it_hist[-1], NQT - 1, len(it_hist) - 1)


def p34(sc, L):
    dxp, dwo, dop = L["dxp"], L["dwo"], L["dop"]
    st, sv, ss = L["st"], L["sv"], L["ss"]
    attnT_sb, wout_sb = L["attnT_sb"], L["wout_sb"]
    x2_sb, x2bf_sb, x2T_sb, xres_sb = (L["x2_sb"], L["x2bf_sb"], L["x2T_sb"],
                                       L["xres_sb"])
    boutbc_sb, bffobc_sb = L["boutbc_sb"], L["bffobc_sb"]
    sT_sb, hT_sb, gT_sb, u_sb = L["sT_sb"], L["hT_sb"], L["gT_sb"], L["u_sb"]
    stats2_sb, mv2_sb, rstd2_sb = L["stats2_sb"], L["mv2_sb"], L["rstd2_sb"]
    A_ps, B_ps, C_ps, F_ps = L["A_ps"], L["B_ps"], L["C_ps"], L["F_ps"]
    onesf_sb, boutrow_sb, bfforow_sb = (L["onesf_sb"], L["boutrow_sb"],
                                        L["bfforow_sb"])
    win_sb, rw1_sb, w2b_sb, wfo_sb = (L["win_sb"], L["rw1_sb"], L["w2b_sb"],
                                      L["wfo_sb"])
    bin_sb, b1T_sb, b2_sb = L["bin_sb"], L["b1T_sb"], L["b2_sb"]
    ident_sb = L["ident_sb"]
    x_hbm, out_hbm = L["x_hbm"], L["out_hbm"]
    eps_ap = L["consts_sb"][:, 0:1]

    for dstw, srcw in (
        (wout_sb[:].rearrange("p (kc m) -> p kc m", kc=8),
         L["wout"].rearrange("(kc p) m -> p kc m", p=128)),
        (win_sb[:].rearrange("p (kc m) -> p kc m", kc=8),
         L["win"].rearrange("(kc p) m -> p kc m", p=128)),
        (rw1_sb[:], L["rw1"]),
        (w2b_sb[:].rearrange("p (kc m) -> p kc m", kc=8),
         L["w2b"].rearrange("(kc p) m -> p kc m", p=128)),
        (wfo_sb[:], L["wfo"]),
        (boutrow_sb[:], L["bout"][None, :]),
        (bfforow_sb[:], L["bffo"][None, :]),
    ):
        sc.g(lambda e, d=dstw, s=srcw, hh=dwo.h:
             e.dma_start(out=d, in_=s).then_inc(hh, 16))
        dwo.inc(16)
    w_all = dwo.n
    sc.wait("tensor", dwo, w_all)
    sc.wait("scalar", dwo, w_all)
    sc.wait("vector", dwo, w_all)

    v_bcp = 0
    for (bcdst, brow) in ((boutbc_sb, boutrow_sb), (bffobc_sb, bfforow_sb)):
        sc.wait("tensor", sv, v_bcp)
        for nh in range(2):
            last = nh == 1
            def mk(o, w, r, last=last):
                def f(e):
                    ins = e.matmul(o, w, r, start=True, stop=True)
                    if last:
                        ins.then_inc(st.h, 1)
                    return ins
                return f
            sc.t(mk(A_ps[:, nh * 512:nh * 512 + 512], onesf_sb[:],
                    brow[:, nh * 512:nh * 512 + 512]))
        t_bc = st.inc()
        sc.wait("vector", st, t_bc)
        v_bcp = sc.v(lambda e, o=bcdst[:], i_=A_ps[:, 0:1024]:
                     e.tensor_copy(o, i_))
    sc.wait("tensor", sv, v_bcp)

    def batched_ln(uslot0, outs, outs2, slot):
        stt = stats2_sb[:, slot * 48:slot * 48 + 48]
        mv = mv2_sb[:, slot * 8:slot * 8 + 8]
        rst = rstd2_sb[:, slot * 4:slot * 4 + 4]
        for t in range(4):
            u_ap = u_sb[:, (uslot0 + t) * C:(uslot0 + t + 1) * C]
            sc.v(lambda e, o=stt[:, t * 12:t * 12 + 6], i_=u_ap[:, 0:512]:
                 e.bn_stats(o, i_))
            sc.v(lambda e, o=stt[:, t * 12 + 6:t * 12 + 12],
                 i_=u_ap[:, 512:1024]: e.bn_stats(o, i_))
        v_st = 0
        for t in range(4):
            v_st = sc.v(lambda e, o=mv[:, t * 2:t * 2 + 2],
                        i_=stt[:, t * 12:t * 12 + 12]: e.bn_aggr(o, i_))
        sc.wait("scalar", sv, v_st)
        var_ap = mv[:].rearrange("p (t two) -> p two t", two=2)[:, 1, :]
        sc.s(lambda e, o=rst, i_=var_ap, b=eps_ap:
             e.activation(out=o, in_=i_, func=AF.Ln, bias=b))
        s_r = sc.s(lambda e, o=rst, i_=rst:
                   e.activation(out=o, in_=i_, func=AF.Exp, scale=-0.5))
        sc.wait("vector", ss, s_r)
        rets = []
        for t in range(4):
            u_ap = u_sb[:, (uslot0 + t) * C:(uslot0 + t + 1) * C]
            v_ap = sc.v(lambda e, o=outs[t], i_=u_ap, m=mv[:, t * 2:t * 2 + 1],
                        r=rst[:, t:t + 1]:
                        e.tensor_scalar(o, i_, m, r, op0=AluOpType.subtract,
                                        op1=AluOpType.mult))
            if outs2[t] is not None:
                sc.wait("scalar", sv, v_ap)
                s_c = sc.s(lambda e, o=outs2[t], i_=outs[t]:
                           e.copy(out=o, in_=i_))
                rets.append(("s", s_c))
            else:
                rets.append(("v", v_ap))
        return rets

    a_guard_v = [0, 0]
    b_guard = [0, 0]
    o_stored = [0, 0, 0, 0]
    last_xrb = [0, 0, 0, 0]
    t_sT_read = [0, 0]
    t_h_read = [0, 0]
    t_g_read = [0, 0]
    s_binev = [0, 0]
    s_gev = [0, 0]
    f_guard_v = [0]
    f_guard_s = [0]

    x2T_done = [0, 0]
    x2cast_st = [[0] * 4, [0] * 4]

    def p3a_stage(sl):
            spar = sl % 2
            t_ops = []
            for tt in range(4):
                half = tt % 2
                row0 = sl * 512 + tt * 128
                ap = A_ps[:, half * 1024:half * 1024 + 1024]
                xr = xres_sb[:, tt * C:(tt + 1) * C]
                sc.wait("sync", sv, last_xrb[tt])
                dxs = dxp[tt]
                sc.y(lambda e, d=xr, s=x_hbm[row0:row0 + 128, :], hh=dxs.h:
                     e.dma_start(out=d, in_=s).then_inc(hh, 16))
                v_dx = (dxs, dxs.inc(16))
                xrb = xr
                sc.wait("vector", v_dx[0], v_dx[1])
                v_xrb = sc.v(lambda e, o=xrb, i0=xr, b=boutbc_sb[:]:
                             e.scalar_tensor_tensor(o, i0, 2.0, b,
                                                    op0=AluOpType.mult,
                                                    op1=AluOpType.add))
                last_xrb[tt] = v_xrb
                sc.wait("tensor", sv, a_guard_v[half])
                for nh in range(2):
                    for kc in range(8):
                        last = (nh == 1 and kc == 7)
                        def mk(o, w, r, s0, s1, last=last):
                            def f(e):
                                ins = e.matmul(o, w, r, start=s0, stop=s1)
                                if last:
                                    ins.then_inc(st.h, 1)
                                return ins
                            return f
                        sc.t(mk(ap[:, nh * 512:nh * 512 + 512],
                                attnT_sb[:, kc * T_OWN + row0:
                                         kc * T_OWN + row0 + 128],
                                wout_sb[:, kc * 1024 + nh * 512:
                                        kc * 1024 + nh * 512 + 512],
                                kc == 0, kc == 7))
                t_op = st.inc()
                u = u_sb[:, tt * C:(tt + 1) * C]
                sc.wait("vector", st, t_op)
                a_guard_v[half] = sc.v(lambda e, o=u, a=ap, b=xrb:
                                       e.tensor_tensor(o, a, b,
                                                       op=AluOpType.add))
                last_xrb[tt] = a_guard_v[half]

    def p3bc_stage(sl):
            spar = sl % 2
            outs = [x2_sb[:, spar * 4 * C + t * C:spar * 4 * C + (t + 1) * C]
                    for t in range(4)]
            outs2 = [x2bf_sb[:, t * C:(t + 1) * C] for t in range(4)]
            ln_rets = batched_ln(0, outs, outs2, 0)
            x2cast_st[spar] = [c for (k, c) in ln_rets]
            for tt in range(4):
                tph = B_ps[:]
                kind, cnt = ln_rets[tt]
                sc.wait("tensor", ss if kind == "s" else sv, cnt)
                sc.wait("tensor", ss, b_guard[0])
                for kc in range(8):
                    last = kc == 7
                    def mk(o, i_, idn, last=last):
                        def f(e):
                            ins = e.transpose(o, i_, idn)
                            if last:
                                ins.then_inc(st.h, 1)
                            return ins
                        return f
                    sc.t(mk(tph[:, kc * 128:kc * 128 + 128],
                            x2bf_sb[:, tt * C + kc * 128:tt * C + kc * 128 + 128],
                            ident_sb[:]))
                t_tp = st.inc()
                sc.wait("scalar", st, t_tp)
                sc.wait("scalar", st, t_sT_read[spar])
                dst = x2T_sb[:].rearrange("p (sp kc t) -> p sp kc t",
                                          sp=2, kc=8)[
                    :, spar, :, tt * 128:tt * 128 + 128]
                srcv = tph.rearrange("p (kc t) -> p kc t", kc=8)
                b_guard[0] = sc.s(lambda e, o=dst, i_=srcv:
                                  e.copy(out=o, in_=i_))
            x2T_done[sl % 2] = max(b_guard)


    def p4ab_stage(sl):
            spar = sl % 2
            x2Tsl = x2T_sb[:, spar * 4096:spar * 4096 + 4096]
            sTsl = sT_sb[:, spar * 512:spar * 512 + 512]
            hTsl = hT_sb[:, spar * 4096:spar * 4096 + 4096]
            gTsl = gT_sb[:, spar * 512:spar * 512 + 512]
            sc.wait("tensor", ss, x2T_done[spar])
            sc.wait("tensor", ss, max(s_gev))
            sc.wait("tensor", ss, s_binev[spar])
            for kc in range(8):
                last = kc == 7
                def mk(o, w, r, s0, s1, last=last):
                    def f(e):
                        ins = e.matmul(o, w, r, start=s0, stop=s1)
                        if last:
                            ins.then_inc(st.h, 1)
                        return ins
                    return f
                sc.t(mk(C_ps[0:64, :], win_sb[:, kc * 64:kc * 64 + 64],
                        x2Tsl[:, kc * 512:kc * 512 + 512], kc == 0, kc == 7))
            t_sT = st.inc()
            t_sT_read[spar] = t_sT
            sc.wait("scalar", st, t_sT)
            sc.wait("scalar", st, t_h_read[spar])
            s_sT = sc.s(lambda e, o=sTsl, i_=C_ps[0:64, :], b=bin_sb[:]:
                        e.activation(out=o, in_=i_, func=AF.Identity, bias=b))
            s_binev[spar] = s_sT
            sc.wait("tensor", ss, s_sT)
            h_evac = [0, 0]
            for j in range(8):
                half = j % 2
                hp = F_ps[:, half * 512:half * 512 + 512]
                sc.wait("tensor", ss, h_evac[half])
                if j < 2:
                    sc.wait("tensor", sv, f_guard_v[0])
                    sc.wait("tensor", ss, f_guard_s[0])
                sc.t(lambda e, o=hp, w=rw1_sb[:, j * 128:j * 128 + 128],
                     r=sTsl:
                     e.matmul(o, w, r, start=True, stop=True,
                              skip_group_check=True).then_inc(st.h, 1))
                t_h = st.inc()
                t_h_read[spar] = t_h
                sc.wait("scalar", st, t_h)
                if j < 2:
                    sc.wait("scalar", st, t_g_read[spar])
                hev = sc.s(lambda e, o=hTsl[:, j * 512:j * 512 + 512], i_=hp,
                           b=b1T_sb[:, j:j + 1]:
                           e.activation(out=o, in_=i_, func=AF.Relu, bias=b))
                h_evac[half] = hev
                f_guard_s[0] = hev
            sc.wait("tensor", ss, h_evac[0])
            sc.wait("tensor", ss, h_evac[1])
            for j in range(8):
                last = j == 7
                def mk(o, w, r, s0, s1, last=last):
                    def f(e):
                        ins = e.matmul(o, w, r, start=s0, stop=s1)
                        if last:
                            ins.then_inc(st.h, 1)
                        return ins
                    return f
                sc.t(mk(C_ps[0:64, :], w2b_sb[:, j * 64:j * 64 + 64],
                        hTsl[:, j * 512:j * 512 + 512], j == 0, j == 7))
            t_g = st.inc()
            sc.wait("scalar", st, t_g)
            s_g = sc.s(lambda e, o=gTsl, i_=C_ps[0:64, :], b=b2_sb[:]:
                       e.activation(out=o, in_=i_, func=AF.Identity, bias=b))
            s_gev[spar] = s_g
            sc.wait("tensor", ss, s_g)
            for tt in range(4):
                row0 = sl * 512 + tt * 128
                fpA = F_ps[:]
                x2t = x2_sb[:, spar * 4 * C + tt * C:spar * 4 * C + (tt + 1) * C]
                x2pb = x2t
                sc.wait("vector", ss, x2cast_st[spar][tt])
                sc.v(lambda e, o=x2pb, a=x2t, b=bffobc_sb[:]:
                     e.tensor_tensor(o, a, b, op=AluOpType.add))
                sc.wait("tensor", sv, f_guard_v[0])
                sc.wait("tensor", ss, f_guard_s[0])
                for nh in range(2):
                    last = nh == 1
                    def mk(o, w, r, last=last):
                        def f(e):
                            ins = e.matmul(o, w, r, start=True, stop=True)
                            if last:
                                ins.then_inc(st.h, 1)
                            return ins
                        return f
                    sc.t(mk(fpA[:, nh * 512:nh * 512 + 512],
                            gTsl[:, tt * 128:tt * 128 + 128],
                            wfo_sb[:, nh * 512:nh * 512 + 512]))
                t_f = st.inc()
                t_g_read[spar] = t_f
                u = u_sb[:, (4 + tt) * C:(5 + tt) * C]
                sc.wait("vector", st, t_f)
                if o_stored[tt]:
                    sc.wait("vector", dop[tt % 2], dop[tt % 2].n)
                f_guard_v[0] = sc.v(lambda e, o=u, a=fpA, b=x2pb:
                                    e.tensor_tensor(o, a, b, op=AluOpType.add))

    def p4c_stage(sl):
            spar = sl % 2
            outs = [u_sb[:, (4 + t) * C:(5 + t) * C] for t in range(4)]
            v_out = batched_ln(4, outs, [None] * 4, 1)
            for tt in range(4):
                row0 = sl * 512 + tt * 128
                u = u_sb[:, (4 + tt) * C:(5 + tt) * C]
                sc.wait("sync", sv, v_out[tt][1])
                dos = dop[tt % 2]
                sc.y(lambda e, d=out_hbm[row0:row0 + 128, :], s=u, hh=dos.h:
                     e.dma_start(out=d, in_=s).then_inc(hh, 16))
                o_stored[tt] = dos.inc(16)

    for sl in range(5):
        if sl < 4:
            p3a_stage(sl)
        if sl >= 1:
            p4ab_stage(sl - 1)
        if sl < 4:
            p3bc_stage(sl)
        if sl >= 1:
            p4c_stage(sl - 1)

_NC_CACHE = {}


def kernel(**inputs):
    x = np.ascontiguousarray(np.asarray(inputs["L0_attn_x"], dtype=np.float32))
    gam = np.asarray(inputs["L0_attn_ln_gamma"], dtype=np.float32)
    bet = np.asarray(inputs["L0_attn_ln_beta"], dtype=np.float32)
    wqkv = np.ascontiguousarray(np.asarray(inputs["L0_attn_w_qkv"], dtype=np.float32))
    bqkv = np.asarray(inputs["L0_attn_b_qkv"], dtype=np.float32)
    wout = np.ascontiguousarray(np.asarray(inputs["L0_attn_w_out"], dtype=np.float32))
    bout = np.asarray(inputs["L0_attn_b_out"], dtype=np.float32)
    win = np.ascontiguousarray(np.asarray(inputs["L0_ffn_Win"], dtype=np.float32))
    binf = np.asarray(inputs["L0_ffn_bin_"], dtype=np.float32)
    w1 = np.asarray(inputs["L0_ffn_W1"], dtype=np.float32)
    b1 = np.asarray(inputs["L0_ffn_b1"], dtype=np.float32)
    w2 = np.asarray(inputs["L0_ffn_W2"], dtype=np.float32)
    b2 = np.asarray(inputs["L0_ffn_b2"], dtype=np.float32)
    wfo = np.ascontiguousarray(np.asarray(inputs["L0_ffn_Wout"], dtype=np.float32))
    bffo = np.asarray(inputs["L0_ffn_bout"], dtype=np.float32)

    M, INNER = 64, 16
    rw1 = np.zeros((M, M * INNER), np.float32)
    for m in range(M):
        rw1[m, m * INNER:(m + 1) * INNER] = w1[m]
    w2b = np.zeros((M * INNER, M), np.float32)
    for m in range(M):
        w2b[m * INNER:(m + 1) * INNER, m] = w2[m]
    b1flat = np.ascontiguousarray(b1.reshape(M * INNER))

    ident = np.eye(128, dtype=np.float32).astype(ml_dtypes.bfloat16)
    identf = np.eye(128, dtype=np.float32)
    onesb = np.ones((128, 128), np.float32).astype(ml_dtypes.bfloat16)
    onesf = np.ones((1, 128), np.float32)
    consts = np.zeros((128, 2), np.float32)
    consts[:, 0] = EPS
    consts[:, 1] = MINUS_LN8

    shared = dict(wqkv=wqkv, bqkv=bqkv, gam=gam, bet=bet, wout=wout, bout=bout,
                  win=win, binf=binf, rw1=rw1, b1f=b1flat, w2b=w2b, b2=b2,
                  wfo=wfo, bffo=bffo, ident=ident, identf=identf, onesb=onesb, onesf=onesf,
                  consts=consts)

    in_maps = []
    for c in range(8):
        b, h = c // 2, c % 2
        own = x[b, h * T_OWN:(h + 1) * T_OWN]
        other = x[b, (1 - h) * T_OWN:(2 - h) * T_OWN]
        xp = np.ascontiguousarray(np.concatenate([own, other], axis=0))
        m = dict(shared)
        m["x"] = xp
        in_maps.append(m)

    if "nc" not in _NC_CACHE:
        _NC_CACHE["nc"] = build_nc()
    nc = _NC_CACHE["nc"]
    res = run_bass_kernel_spmd(nc, in_maps, core_ids=list(range(8)))

    out = np.empty((4, T_FULL, C), np.float32)
    for c in range(8):
        b, h = c // 2, c % 2
        out[b, h * T_OWN:(h + 1) * T_OWN] = res.results[c]["out"]
    return out
